# version 01
# baseline (speedup 1.0000x reference)
import numpy as np
import concourse.bass as bass
import concourse.mybir as mybir
from concourse.bass import IndirectOffsetOnAxis
from concourse.bass_utils import run_bass_kernel_spmd

F32 = mybir.dt.float32
BF16 = mybir.dt.bfloat16
I32 = mybir.dt.int32
U32 = mybir.dt.uint32
AF = mybir.ActivationFunctionType
ALU = mybir.AluOpType
AX = mybir.AxisListType

D = 1024
T = 2048
DI = 2048
NST = 16
NEXP = 16
CAP = 1024
HF = 4096
ND = D // 128
NDI = DI // 128
NTT = T // 128
EPS = 1e-5
SL = slice

DEBUG_TAPS = {}
PHASES = ("a", "f", "b", "f")


def build_program():
    nc = bass.Bass("TRN2", target_bir_lowering=False, debug=False, num_devices=8)

    def din(name, shape, dt=F32):
        return nc.dram_tensor(name, shape, dt, kind="ExternalInput").ap()

    I = {}
    I["xT"] = din("xT", [D, T])
    I["a_in_proj"] = din("a_in_proj", [D, 2 * DI])
    I["a_conv_w"] = din("a_conv_w", [DI, 4])
    I["a_conv_b"] = din("a_conv_b", [DI])
    I["a_x_proj"] = din("a_x_proj", [DI, 96])
    I["a_dt_proj"] = din("a_dt_proj", [64, DI])
    I["a_dt_b"] = din("a_dt_b", [DI])
    I["a_A_log"] = din("a_A_log", [DI, NST])
    I["a_D"] = din("a_D", [DI])
    I["a_out_proj"] = din("a_out_proj", [DI, D])
    I["a_glu_w"] = din("a_glu_w", [D, 2 * D])
    I["a_glu_b"] = din("a_glu_b", [2 * D])
    I["a_ln_g"] = din("a_ln_g", [D])
    I["a_ln_b"] = din("a_ln_b", [D])
    I["b_W_in"] = din("b_W_in", [D, 2 * DI])
    I["b_conv_w"] = din("b_conv_w", [DI, 4])
    I["b_conv_b"] = din("b_conv_b", [DI])
    I["b_W_x"] = din("b_W_x", [DI, 33])
    I["b_W_dt"] = din("b_W_dt", [DI])
    I["b_b_dt"] = din("b_b_dt", [DI])
    I["b_A_log"] = din("b_A_log", [1, NST])
    I["b_D"] = din("b_D", [DI])
    I["b_W_out"] = din("b_W_out", [DI, D])
    I["f_Wg"] = din("f_Wg", [D, NEXP])
    I["f_bg"] = din("f_bg", [1, NEXP])
    I["f_W1c"] = din("f_W1c", [2, D, HF])
    I["f_b1c"] = din("f_b1c", [2, HF])
    I["f_W2c"] = din("f_W2c", [2, HF, D])
    I["f_b2c"] = din("f_b2c", [2, D])
    I["c_eye"] = din("c_eye", [128, 128], BF16)
    I["c_slo"] = din("c_slo", [128, 128], BF16)
    I["c_sel"] = din("c_sel", [NEXP, NEXP * 128], BF16)
    I["c_iota16b"] = din("c_iota16b", [128, NEXP])
    I["c_iota8c"] = din("c_iota8c", [8, 1])
    I["c_iota128c"] = din("c_iota128c", [128, 1])
    I["c_cid"] = din("c_cid", [1, 1])

    out_h = nc.dram_tensor("outT", [D, T], F32, kind="ExternalOutput").ap()

    S = {}
    S["xres"] = nc.dram_tensor("xres", [D, T], F32).ap()
    S["u_sp"] = nc.dram_tensor("u_sp", [DI, T], BF16).ap()
    S["sz_sp"] = nc.dram_tensor("sz_sp", [DI, T], BF16).ap()
    S["dt_sp"] = nc.dram_tensor("dt_sp", [DI, T], F32).ap()
    S["w_sp"] = nc.dram_tensor("w_sp", [DI, T], BF16).ap()
    S["y_sp"] = nc.dram_tensor("y_sp", [DI, T], F32).ap()
    S["cc_x_in"] = nc.dram_tensor("cc_x_in", [T, D], BF16).ap()
    S["X_all"] = nc.dram_tensor("X_all", [8 * T, D], BF16, addr_space="Shared").ap()
    S["cc_m_in"] = nc.dram_tensor("cc_m_in", [T, 2], F32).ap()
    S["M_all"] = nc.dram_tensor("M_all", [8 * T, 2], F32, addr_space="Shared").ap()
    S["cc_c_in"] = nc.dram_tensor("cc_c_in", [1, NEXP], F32).ap()
    S["C_all"] = nc.dram_tensor("C_all", [8, NEXP], F32, addr_space="Shared").ap()
    S["B_own"] = nc.dram_tensor("B_own", [2 * CAP + 128, D], BF16).ap()
    S["cc_y_in"] = nc.dram_tensor("cc_y_in", [2 * CAP, D], BF16).ap()
    S["Y_all"] = nc.dram_tensor("Y_all", [NEXP * CAP, D], BF16, addr_space="Shared").ap()

    taps = {}
    for name, (shape, dt) in DEBUG_TAPS.items():
        taps[name] = nc.dram_tensor(name, shape, dt, kind="ExternalOutput").ap()

    AR = nc.alloc_sbuf_tensor("AR", [128, 80 * 1024], BF16)

    def arb(off_kb, nbf16):
        return AR[:, off_kb * 512: off_kb * 512 + nbf16]

    def arf(off_kb, nf32):
        return AR[:, off_kb * 512: off_kb * 512 + 2 * nf32].bitcast(F32)

    arow = nc.alloc_sbuf_tensor("arow", [1, 16], F32)
    onesall = nc.alloc_sbuf_tensor("onesall", [128, 128], BF16)
    xbf = nc.alloc_sbuf_tensor("xbf", [128, T], BF16)
    sqb = nc.alloc_sbuf_tensor("sqb", [128, T], BF16)
    rbc = None
    mbc = None
    deltf = nc.alloc_sbuf_tensor("deltf", [128, T], F32)
    vsm = nc.alloc_sbuf_tensor("vsm", [128, 48], F32)
    pvec = nc.alloc_sbuf_tensor("pvec", [128, 96], F32)
    Acol = nc.alloc_sbuf_tensor("Acol", [128, 400], F32)
    selb = nc.alloc_sbuf_tensor("selb", [NEXP, NEXP * 128], BF16)
    eyeb = nc.alloc_sbuf_tensor("eyeb", [128, 128], BF16)
    slob = nc.alloc_sbuf_tensor("slob", [128, 128], BF16)
    onesc = nc.alloc_sbuf_tensor("onesc", [128, 1], BF16)
    ones4 = nc.alloc_sbuf_tensor("ones4", [16, 16], F32)
    onesf = nc.alloc_sbuf_tensor("onesf", [1, 128], F32)
    iotab = nc.alloc_sbuf_tensor("iotab", [128, NEXP], F32)
    iota8 = nc.alloc_sbuf_tensor("iota8", [8, 1], F32)
    cidb = nc.alloc_sbuf_tensor("cidb", [128, 4], F32)
    rbc = arb(36, T)
    mbc = arb(40, T)
    xdbf = arf(112, T)[0:96, :]
    xdbb = arb(120, T)[0:64, :]
    dtpw = arb(124, DI)[0:64, :]
    brow = arb(128, T)[0:16, :]
    crow = arb(132, T)[0:16, :]
    wgb = nc.alloc_sbuf_tensor("wgb", [128, ND * NEXP], BF16)
    bgb = nc.alloc_sbuf_tensor("bgb", [128, NEXP], F32)
    sco = arf(56, NTT * NEXP)
    rtw = arf(60, 64)
    oneh = arb(103, NEXP)
    onehf = arf(51, NEXP)
    posg = arf(61, NTT)
    idxg = arf(62, NTT)
    gateb = nc.alloc_sbuf_tensor("gateb", [128, NTT], F32)
    slotc = nc.alloc_sbuf_tensor("slotc", [128, NTT], I32)
    cnts = arf(52, 64)[0:16, :]
    cnt8 = arf(53, NEXP)[0:8, :]
    mask8 = arf(54, 1)[0:8, :]
    toffb = arf(58, NTT * NEXP)
    meta2 = arf(48, 768)
    scat_i = AR[:, 55 * 512: 55 * 512 + 256].bitcast(I32)
    stage = arb(64, 2 * D)
    b1col = nc.alloc_sbuf_tensor("b1col", [128, 64], F32)
    kcol = nc.alloc_sbuf_tensor("kcol", [128, 2], F32)
    b2col = nc.alloc_sbuf_tensor("b2col", [128, 16], F32)

    pss = [nc.alloc_psum_tensor("ps%d" % i, [128, 512], F32) for i in range(8)]

    with (
        nc.allow_non_contiguous_dma(reason="small per-partition vector loads"),
        nc.semaphore("sm_ld") as sm_ld,
        nc.semaphore("sm_gd") as sm_gd,
        nc.semaphore("sm_t") as sm_t,
        nc.semaphore("sm_v") as sm_v,
        nc.semaphore("sm_s") as sm_s,
        nc.semaphore("sm_g") as sm_g,
        nc.semaphore("sm_le") as sm_le,
        nc.semaphore("sm_ge") as sm_ge,
        nc.Block() as blk,
    ):
        sems = dict(ld=sm_ld, gd=sm_gd, t=sm_t, v=sm_v, s=sm_s, g=sm_g, le=sm_le, ge=sm_ge)
        cnt = {k: 0 for k in sems}
        echo = {"ld": [0, 0], "gd": [0, 0]}
        _q_eng = {"ld": "sync", "gd": "gpsimd"}
        _q_echo = {"ld": "le", "gd": "ge"}

        def _emit_echo(qname):
            raw = cnt[qname]
            esem = _q_echo[qname]

            def f(eng):
                eng.wait_ge(sems[qname], raw)
                eng.sem_inc(sems[esem], 1)
            getattr(blk, _q_eng[qname])(f)
            cnt[esem] += 1
            echo[qname][0] = raw
            echo[qname][1] = cnt[esem]
            return cnt[esem]

        def _xlate(waits, engname):
            out = []
            for nm, val in waits:
                if val <= 0:
                    continue
                if nm in ("ld", "gd"):
                    if engname == _q_eng[nm]:
                        out.append((nm, cnt[nm]))
                    else:
                        if echo[nm][0] < val:
                            _emit_echo(nm)
                        out.append((_q_echo[nm], echo[nm][1]))
                else:
                    out.append((nm, val))
            return out

        class _Proxy:

            def __init__(self, eng, semname, self_wait=True):
                self._eng = eng
                self._sn = semname
                self._sw = self_wait

            def __getattr__(self, name):
                real = getattr(self._eng, name)
                if not callable(real) or name.startswith("wait") or name == "dma_start":
                    return real

                def call(*a, **k):
                    if self._sw and cnt[self._sn] > 0:
                        self._eng.wait_ge(sems[self._sn], cnt[self._sn])
                    inst = real(*a, **k)
                    inst.then_inc(sems[self._sn], 1)
                    cnt[self._sn] += 1
                    return inst
                return call

        def _mk(engname, semname):
            self_wait = semname != "t"

            def emit(build, waits=()):
                tw = _xlate(waits, engname)
                def f(eng):
                    for nm, val in tw:
                        if val > 0:
                            eng.wait_ge(sems[nm], val)
                    build(_Proxy(eng, semname, self_wait))
                getattr(blk, engname)(f)
                return cnt[semname]
            return emit

        EV = _mk("vector", "v")
        ES = _mk("scalar", "s")
        ET = _mk("tensor", "t")
        EG = _mk("gpsimd", "g")

        def SYD(out, in_, waits=(), **kw):
            tw = _xlate(waits, "sync")
            def f(sy):
                for nm, val in tw:
                    if val > 0:
                        sy.wait_ge(sems[nm], val)
                sy.dma_start(out=out, in_=in_, **kw).then_inc(sems["ld"], 16)
            blk.sync(f)
            cnt["ld"] += 16
            return cnt["ld"]

        def GPD(out, in_, waits=(), **kw):
            tw = _xlate(waits, "gpsimd")
            def f(gp):
                for nm, val in tw:
                    if val > 0:
                        gp.wait_ge(sems[nm], val)
                gp.dma_start(out=out, in_=in_, **kw).then_inc(sems["gd"], 16)
            blk.gpsimd(f)
            cnt["gd"] += 16
            return cnt["gd"]

        def GPI(out, out_off, in_, in_off, waits=()):
            tw = _xlate(waits, "gpsimd")
            def f(gp):
                for nm, val in tw:
                    if val > 0:
                        gp.wait_ge(sems[nm], val)
                gp.indirect_dma_start(out, out_off, in_, in_off).then_inc(sems["gd"], 16)
            blk.gpsimd(f)
            cnt["gd"] += 16
            return cnt["gd"]

        def tk(nm):
            return cnt[nm]

        def WAIT(engname, waits):
            tw = _xlate(waits, engname)
            def f(eng):
                for nm, val in tw:
                    if val > 0:
                        eng.wait_ge(sems[nm], val)
            getattr(blk, engname)(f)

        def ALLW():
            return [("v", tk("v")), ("s", tk("s")), ("t", tk("t")),
                    ("g", tk("g")), ("gd", tk("gd")), ("ld", tk("ld"))]

        SYD(eyeb[:], I["c_eye"][:])
        SYD(slob[:], I["c_slo"][:])
        SYD(selb[:], I["c_sel"][:])
        SYD(iotab[:], I["c_iota16b"][:])
        SYD(iota8[:], I["c_iota8c"][:])
        SYD(cidb[:, 3:4], I["c_iota128c"][:])
        SYD(cidb[:, 0:1], I["c_cid"][0:1, 0:1].partition_broadcast(128))
        SYD(S["xres"][:, :], I["xT"][:, :])
        SYD(pvec[:, 0:16], I["a_conv_b"].rearrange("(dt p) -> p dt", p=128))
        SYD(pvec[:, 16:32], I["a_dt_b"].rearrange("(dt p) -> p dt", p=128))
        SYD(pvec[:, 32:48], I["a_D"].rearrange("(dt p) -> p dt", p=128))
        SYD(pvec[:, 48:64], I["b_conv_b"].rearrange("(dt p) -> p dt", p=128))
        SYD(pvec[:, 64:80], I["b_b_dt"].rearrange("(dt p) -> p dt", p=128))
        SYD(pvec[:, 80:96], I["b_W_dt"].rearrange("(dt p) -> p dt", p=128))
        SYD(vsm[:, 0:8], I["a_ln_g"].rearrange("(dt p) -> p dt", p=128))
        SYD(vsm[:, 8:16], I["a_ln_b"].rearrange("(dt p) -> p dt", p=128))
        SYD(vsm[:, 16:32], I["a_glu_b"].rearrange("(dt p) -> p dt", p=128))
        SYD(vsm[:, 32:48], I["b_D"].rearrange("(dt p) -> p dt", p=128))
        SYD(Acol[:, 0:256].rearrange("p (dt n) -> p dt n", n=16),
            I["a_A_log"].rearrange("(dt p) n -> p dt n", p=128))
        SYD(Acol[:, 272:336].rearrange("p (dt k) -> p dt k", k=4),
            I["a_conv_w"].rearrange("(dt p) k -> p dt k", p=128))
        SYD(Acol[:, 336:400].rearrange("p (dt k) -> p dt k", k=4),
            I["b_conv_w"].rearrange("(dt p) k -> p dt k", p=128))
        SYD(arow[0:1, 0:16], I["b_A_log"][0:1, :])
        t_bgrow = SYD(bgb[0:1, 0:16], I["f_bg"][0:1, :])
        GPD(dtpw[:, :], I["a_dt_proj"][:, :])
        GPD(wgb.rearrange("p (k e) -> p k e", e=NEXP), I["f_Wg"].rearrange("(k p) e -> p k e", p=128))
        t_ldc = tk("ld")
        t_gdc = tk("gd")

        EV(lambda v: v.memset(onesf[:, :], 1.0))
        EV(lambda v: v.memset(ones4[:, :], 1.0))
        EV(lambda v: v.memset(onesall[:, :], 1.0))
        EV(lambda v: v.memset(kcol[:, 0:1], 1.0))
        EV(lambda v: v.memset(kcol[:, 1:2], EPS))
        t_ones = EV(lambda v: v.memset(onesc[:, :], 1.0))

        def f(v):
            v.tensor_scalar_mul(cidb[:, 1:2], cidb[:, 0:1], 2.0)
            v.tensor_scalar_add(cidb[:, 2:3], cidb[:, 1:2], 1.0)
            v.tensor_scalar_add(cidb[:, 3:4], cidb[:, 3:4], 2048.0)
        t_cid = EV(f, waits=[("ld", t_ldc)])

        ES(lambda s: s.activation(Acol[:, 0:256], Acol[:, 0:256], AF.Exp), waits=[("ld", t_ldc)])
        t_expA = ES(lambda s: s.activation(arow[0:1, 0:16], arow[0:1, 0:16], AF.Exp))
        EV(lambda v: v.tensor_scalar_mul(Acol[:, 0:256], Acol[:, 0:256], -1.0), waits=[("s", t_expA)])
        t_abrow = EV(lambda v: v.tensor_scalar_mul(arow[0:1, 0:16], arow[0:1, 0:16], -1.0))
        t_mab = ET(lambda t: t.matmul(pss[0][:, 0:16], onesf[0:1, :], arow[0:1, 0:16],
                                      start=True, stop=True), waits=[("v", t_abrow)])
        EV(lambda v: v.tensor_copy(Acol[:, 256:272], pss[0][:, 0:16]), waits=[("t", t_mab)])
        t_mbg = ET(lambda t: t.matmul(pss[1][:, 0:16], onesf[0:1, :], bgb[0:1, 0:16],
                                      start=True, stop=True), waits=[("ld", t_bgrow), ("v", tk("v"))])
        EV(lambda v: v.tensor_copy(bgb[:, 0:16], pss[1][:, 0:16]), waits=[("t", t_mbg)])

        def layer_norm(nd, dst_tile, src_hbm=None, src_tile=None, gamma_beta=None):
            tprev_t = tk("t")
            t_entry_v = tk("v")
            for dt in range(nd):
                if src_hbm is not None:
                    t_x = GPD(xbf[:, :], src_hbm[SL(dt * 128, (dt + 1) * 128), :],
                              waits=[("t", tprev_t), ("s", tk("s")), ("gd", tk("gd"))])
                    xsrc = xbf[:, :]
                    wx = [("gd", t_x)]
                else:
                    xsrc = src_tile(dt)
                    wx = []
                t_sq = ES(lambda s, xsrc=xsrc: s.activation(sqb[:, :], xsrc, AF.Square),
                          waits=wx + [("t", tprev_t), ("v", t_entry_v)])

                def fmm(t, dt=dt, xsrc=xsrc):
                    for j in range(4):
                        t.matmul(pss[j][:, 0:512], onesall[:, :], xsrc[:, SL(j * 512, (j + 1) * 512)],
                                 start=(dt == 0), stop=(dt == nd - 1))
                    for j in range(4):
                        t.matmul(pss[4 + j][:, 0:512], onesall[:, :], sqb[:, SL(j * 512, (j + 1) * 512)],
                                 start=(dt == 0), stop=(dt == nd - 1))
                extra = [("s", tk("s")), ("v", tk("v"))] if dt == 0 else []
                tprev_t = ET(fmm, waits=wx + [("s", t_sq), ("v", t_ones)] + extra)
            nf = float(nd * 128)
            muf = deltf[:, :]
            exf = arf(44, T)
            m2f = arf(52, T)

            def fst(v):
                for j in range(4):
                    v.tensor_scalar_mul(muf[:, SL(j * 512, (j + 1) * 512)], pss[j][:, 0:512], 1.0 / nf)
                    v.tensor_scalar_mul(exf[:, SL(j * 512, (j + 1) * 512)], pss[4 + j][:, 0:512], 1.0 / nf)
                v.tensor_mul(m2f, muf, muf)
                v.tensor_sub(exf, exf, m2f)
                v.tensor_scalar_add(exf, exf, EPS)
            t_var = EV(fst, waits=[("t", tprev_t), ("gd", tk("gd"))])
            t_sd = ES(lambda s: s.activation(exf, exf, AF.Sqrt), waits=[("v", t_var)])

            def frc(v):
                v.reciprocal(m2f, exf)
                v.tensor_copy(rbc[:, :], m2f)
                v.tensor_mul(mbc[:, :], muf, m2f)
            t_bc = EV(frc, waits=[("s", t_sd)])
            t_last = t_bc
            for dt in range(nd):
                if src_hbm is not None:
                    t_x = GPD(xbf[:, :], src_hbm[SL(dt * 128, (dt + 1) * 128), :],
                              waits=[("v", tk("v")), ("gd", tk("gd"))])
                    xsrc = xbf[:, :]
                    wx = [("gd", t_x)]
                else:
                    xsrc = src_tile(dt)
                    wx = []

                def fn(v, dt=dt, xsrc=xsrc):
                    dst = dst_tile(dt)
                    v.tensor_mul(sqb[:, :], xsrc, rbc[:, :])
                    if gamma_beta is None:
                        v.tensor_sub(dst, sqb[:, :], mbc[:, :])
                        return
                    gcol, bcol = gamma_beta
                    v.tensor_sub(sqb[:, :], sqb[:, :], mbc[:, :])
                    v.tensor_scalar(dst, sqb[:, :], gcol[:, SL(dt, dt + 1)],
                                    bcol[:, SL(dt, dt + 1)], ALU.mult, ALU.add)
                t_last = EV(fn, waits=wx + [("v", t_bc)])
            return t_last

        def mm_proj(w_hbm, kt_n, mt_n, rhs_tile, evac, woff_kb, rhs_wait=()):
            wb = arb(woff_kb, kt_n * mt_n * 128)
            t_w = GPD(wb.rearrange("p (k m c) -> p k m c", k=kt_n, m=mt_n, c=128),
                      w_hbm.rearrange("(k p) (m c) -> p k m c", p=128, c=128),
                      waits=ALLW())
            for mt in range(mt_n):
                for j in range(4):
                    pa = pss[(mt * 4 + j) % 4]

                    def fm(t, mt=mt, j=j, pa=pa):
                        for kt in range(kt_n):
                            i = t.matmul(pa[:, 0:512],
                                         wb[:, SL((kt * mt_n + mt) * 128, (kt * mt_n + mt + 1) * 128)],
                                         rhs_tile(kt)[:, SL(j * 512, (j + 1) * 512)],
                                         start=(kt == 0), stop=(kt == kt_n - 1))
                        return i
                    guard = [("v", tk("v")), ("s", tk("s"))] if (mt * 4 + j) >= 4 else []
                    t_mm = ET(fm, waits=[("gd", t_w)] + guard + list(rhs_wait))
                    evac(mt, pa, j, t_mm)

        def conv_silu(dt, xi_ap, cw_off, cb_col, t_ready):
            def fc(v, dt=dt):
                def cw(k):
                    return Acol[:, SL(cw_off + dt * 4 + k, cw_off + dt * 4 + k + 1)]
                v.tensor_scalar_mul(deltf[:, :], xi_ap, cw(3))
                v.scalar_tensor_tensor(deltf[:, 1:T], xi_ap[:, 0:T - 1], cw(2), deltf[:, 1:T], ALU.mult, ALU.add)
                v.scalar_tensor_tensor(deltf[:, 2:T], xi_ap[:, 0:T - 2], cw(1), deltf[:, 2:T], ALU.mult, ALU.add)
                return v.scalar_tensor_tensor(deltf[:, 3:T], xi_ap[:, 0:T - 3], cw(0), deltf[:, 3:T],
                                              ALU.mult, ALU.add)
            t_cv = EV(fc, waits=list(t_ready))
            t_si = ES(lambda s, dt=dt: s.activation(xi_ap, deltf[:, :], AF.Silu,
                                                    bias=pvec[:, SL(cb_col + dt, cb_col + dt + 1)]),
                      waits=[("v", t_cv)])
            return GPD(S["u_sp"][SL(dt * 128, (dt + 1) * 128), :], xi_ap, waits=[("s", t_si)])

        def mamba(which):
            a = which == "a"

            def xln_t(dt):
                return arb(4 * dt, T)
            layer_norm(ND, xln_t, src_hbm=S["xres"])
            t_ln1 = tk("v")
            if a:
                for dt in range(ND):
                    t_cp = EV(lambda v, dt=dt: v.tensor_copy(deltf[:, :], xln_t(dt)),
                              waits=[("v", t_ln1), ("gd", tk("gd"))])
                    GPD(S["xres"][SL(dt * 128, (dt + 1) * 128), :], deltf[:, :],
                        waits=[("v", t_cp)], accum_op=ALU.add)
                layer_norm(ND, xln_t, src_tile=xln_t, gamma_beta=(vsm[:, 0:8], vsm[:, 8:16]))
            t_ln = tk("v")
            if ("tap_xln_" + which) in taps:
                for dt in range(ND):
                    t_cp = EV(lambda v, dt=dt: v.tensor_copy(deltf[:, :], xln_t(dt)), waits=[("v", t_ln)])
                    SYD(taps["tap_xln_" + which][SL(dt * 128, (dt + 1) * 128), :], deltf[:, :],
                        waits=[("v", t_cp)])

            w_hbm = I["a_in_proj"] if a else I["b_W_in"]
            cw_off = 272 if a else 336
            cb_col = 0 if a else 48

            def xi_stage(gmt):
                return arb(64 + 4 * (gmt % 2), T)

            for half in range(2):
                def evac(mt, pa, j, t_mm, half=half):
                    gmt = half * 16 + mt
                    if gmt < NDI:
                        t_e = EV(lambda v, pa=pa, j=j, gmt=gmt: v.tensor_copy(
                            xi_stage(gmt)[:, SL(j * 512, (j + 1) * 512)], pa[:, 0:512]),
                            waits=[("t", t_mm)] + ([("gd", tk("gd"))] if j == 0 else []))
                        if j == 3:
                            conv_silu(gmt, xi_stage(gmt), cw_off, cb_col, [("v", t_e)])
                    else:
                        t_e = ES(lambda s, pa=pa, j=j: s.activation(sqb[:, SL(j * 512, (j + 1) * 512)],
                                                                    pa[:, 0:512], AF.Silu),
                                 waits=[("t", t_mm)] + ([("gd", tk("gd"))] if j == 0 else []))
                        if j == 3:
                            GPD(S["sz_sp"][SL((gmt - NDI) * 128, (gmt - NDI + 1) * 128), :], sqb[:, :],
                                waits=[("s", t_e)])
                mm_proj(w_hbm[:, SL(half * 2048, (half + 1) * 2048)], ND, 16,
                        xln_t, evac, 32, rhs_wait=[("v", t_ln)])

            ncols = 96 if a else 33
            wx_hbm = I["a_x_proj"] if a else I["b_W_x"]
            wxb = arb(32, NDI * ncols)
            t_wx = GPD(wxb.rearrange("p (k c) -> p k c", k=NDI, c=ncols),
                       wx_hbm.rearrange("(k p) c -> p k c", p=128), waits=ALLW())
            for kt in range(NDI):
                tku = GPD(arb(64 + 4 * (kt % 2), T), S["u_sp"][SL(kt * 128, (kt + 1) * 128), :],
                          waits=[("t", tk("t")), ("v", tk("v") if kt == 0 else 0)])

                def fx(t, kt=kt):
                    for j in range(4):
                        i = t.matmul(pss[4 + j][0:ncols, 0:512], wxb[:, SL(kt * ncols, (kt + 1) * ncols)],
                                     arb(64 + 4 * (kt % 2), T)[:, SL(j * 512, (j + 1) * 512)],
                                     start=(kt == 0), stop=(kt == NDI - 1))
                    return i
                t_mx = ET(fx, waits=[("gd", tku)] + ([("gd", t_wx), ("v", tk("v")), ("s", tk("s"))]
                                                     if kt == 0 else []))
            for j in range(4):
                EV(lambda v, j=j: v.tensor_copy(xdbf[0:ncols, SL(j * 512, (j + 1) * 512)],
                                                pss[4 + j][0:ncols, 0:512]),
                   waits=[("t", t_mx)])
            t_xdb = tk("v")

            if a:
                t_dtc = EV(lambda v: v.tensor_copy(xdbb[:, :], xdbf[0:64, :]), waits=[("v", t_xdb)])
                t_br = GPD(brow[:, :], xdbf[64:80, :], waits=[("v", t_xdb)])
                t_cr = GPD(crow[:, :], xdbf[80:96, :], waits=[("v", t_xdb)])
            else:
                t_br = GPD(brow[:, :], xdbf[1:17, :], waits=[("v", t_xdb)])
                t_cr = GPD(crow[:, :], xdbf[17:33, :], waits=[("v", t_xdb)])
                for j in range(4):
                    t_bb = ET(lambda t, j=j: t.matmul(pss[5][:, 0:512], onesf[0:1, :],
                                                      xdbf[0:1, SL(j * 512, (j + 1) * 512)],
                                                      start=True, stop=True),
                              waits=[("v", t_xdb if j == 0 else tk("v"))])
                    EV(lambda v, j=j: v.tensor_copy(arf(72, T)[:, SL(j * 512, (j + 1) * 512)], pss[5][:, 0:512]),
                       waits=[("t", t_bb)])
            t_xp0 = tk("v")

            for dt in range(NDI):
                tku = GPD(arb(64 + 4 * (dt % 2), T), S["u_sp"][SL(dt * 128, (dt + 1) * 128), :],
                          waits=[("v", tk("v"))])
                if a:
                    for j in range(4):
                        t_m = ET(lambda t, dt=dt, j=j: t.matmul(pss[5][:, 0:512],
                                                                dtpw[:, SL(dt * 128, (dt + 1) * 128)],
                                                                xdbb[:, SL(j * 512, (j + 1) * 512)],
                                                                start=True, stop=True),
                                 waits=[("gd", t_gdc), ("v", t_dtc), ("s", tk("s"))])
                        ES(lambda s, j=j, dt=dt: s.activation(deltf[:, SL(j * 512, (j + 1) * 512)],
                                                              pss[5][:, 0:512], AF.Exp,
                                                              bias=pvec[:, SL(16 + dt, 16 + dt + 1)]),
                           waits=[("t", t_m)] + ([("v", tk("v"))] if j == 0 else []))
                        t_sp = ES(lambda s, j=j: s.activation(deltf[:, SL(j * 512, (j + 1) * 512)],
                                                              deltf[:, SL(j * 512, (j + 1) * 512)],
                                                              AF.Ln, bias=kcol[:, 0:1]))
                else:
                    t_dr = EV(lambda v, dt=dt: v.tensor_scalar(deltf[:, :], arf(72, T)[:, :],
                                                               pvec[:, SL(80 + dt, 80 + dt + 1)],
                                                               pvec[:, SL(64 + dt, 64 + dt + 1)],
                                                               ALU.mult, ALU.add),
                              waits=[("v", tk("v")), ("s", tk("s")), ("gd", tk("gd"))])
                    ES(lambda s: s.activation(deltf[:, :], deltf[:, :], AF.Exp), waits=[("v", t_dr)])
                    t_sp = ES(lambda s: s.activation(deltf[:, :], deltf[:, :], AF.Ln, bias=kcol[:, 0:1]))
                GPD(S["dt_sp"][SL(dt * 128, (dt + 1) * 128), :], deltf[:, :], waits=[("s", t_sp)])
                t_w = EV(lambda v, dt=dt: v.tensor_mul(sqb[:, :], deltf[:, :], arb(64 + 4 * (dt % 2), T)),
                         waits=[("s", t_sp), ("gd", tku)])
                GPD(S["w_sp"][SL(dt * 128, (dt + 1) * 128), :], sqb[:, :], waits=[("v", t_w)])

            acol_off = 0 if a else 256
            for q in range(4):
                for ni in range(4):
                    n = q * 4 + ni
                    for j in range(4):
                        t_m = ET(lambda t, n=n, j=j: t.matmul(pss[6][:, 0:512],
                                                              selb[:, SL(n * 128, (n + 1) * 128)],
                                                              brow[:, SL(j * 512, (j + 1) * 512)],
                                                              start=True, stop=True),
                                 waits=[("gd", t_br), ("v", tk("v"))])
                        EV(lambda v, ni=ni, j=j: v.tensor_copy(arb(4 * ni, T)[:, SL(j * 512, (j + 1) * 512)],
                                                               pss[6][:, 0:512]), waits=[("t", t_m)])
                        t_m2 = ET(lambda t, n=n, j=j: t.matmul(pss[7][:, 0:512],
                                                               selb[:, SL(n * 128, (n + 1) * 128)],
                                                               crow[:, SL(j * 512, (j + 1) * 512)],
                                                               start=True, stop=True),
                                  waits=[("gd", t_cr), ("v", tk("v"))])
                        EV(lambda v, ni=ni, j=j: v.tensor_copy(arb(16 + 4 * ni, T)[:, SL(j * 512, (j + 1) * 512)],
                                                               pss[7][:, 0:512]), waits=[("t", t_m2)])
                t_bcq = tk("v")

                for dt in range(NDI):
                    slt = dt % 2
                    t_dt = SYD(arf(64 + 8 * slt, T), S["dt_sp"][SL(dt * 128, (dt + 1) * 128), :],
                               waits=[("s", tk("s")), ("v", tk("v"))])
                    t_wt = SYD(arb(80 + 4 * slt, T), S["w_sp"][SL(dt * 128, (dt + 1) * 128), :])
                    for ni in range(4):
                        n = q * 4 + ni
                        ac_i = acol_off + (dt * 16 + n if a else n)
                        ac = Acol[:, SL(ac_i, ac_i + 1)]
                        t_da = ES(lambda s, slt=slt, ac=ac: s.activation(arf(104, T), arf(64 + 8 * slt, T),
                                                                         AF.Exp, scale=ac),
                                  waits=[("ld", t_dt), ("v", tk("v"))])
                        t_wb = EV(lambda v, slt=slt, ni=ni: v.tensor_mul(arb(48 + 4 * ni, T),
                                                                         arb(80 + 4 * slt, T),
                                                                         arb(4 * ni, T)),
                                  waits=[("ld", t_wt), ("v", t_bcq)])
                        EV(lambda v, ni=ni: v.tensor_tensor_scan(arb(32 + 4 * ni, T), arf(104, T),
                                                                 arb(48 + 4 * ni, T), 0.0,
                                                                 ALU.mult, ALU.add),
                           waits=[("s", t_da)])
                    EV(lambda v: v.tensor_mul(arb(48, 4 * T), arb(32, 4 * T), arb(16, 4 * T)))
                    EV(lambda v: v.tensor_add(arb(48, 2 * T), arb(48, 2 * T), arb(56, 2 * T)))
                    psl = dt % 2
                    t_a2 = EV(lambda v, psl=psl: v.tensor_tensor(arf(88 + 8 * psl, T), arb(48, T),
                                                                 arb(52, T), ALU.add),
                              waits=[("gd", tk("gd"))] if (dt >= 2 or q > 0) else [])
                    GPD(S["y_sp"][SL(dt * 128, (dt + 1) * 128), :], arf(88 + 8 * psl, T),
                        waits=[("v", t_a2)], accum_op=(ALU.add if q > 0 else ALU.bypass))
            t_scan = tk("gd")

            def Dcol(dt):
                return pvec[:, SL(32 + dt, 33 + dt)] if a else vsm[:, SL(32 + dt, 33 + dt)]
            for dt in range(NDI):
                t_y = SYD(arf(104, T), S["y_sp"][SL(dt * 128, (dt + 1) * 128), :],
                          waits=[("gd", t_scan), ("v", tk("v"))])
                GPD(xbf[:, :], S["u_sp"][SL(dt * 128, (dt + 1) * 128), :], waits=[("v", tk("v"))])
                t_z = GPD(sqb[:, :], S["sz_sp"][SL(dt * 128, (dt + 1) * 128), :])
                t_yd = EV(lambda v, dt=dt: v.scalar_tensor_tensor(deltf[:, :], xbf[:, :], Dcol(dt),
                                                                  arf(104, T), ALU.mult, ALU.add),
                          waits=[("ld", t_y), ("gd", t_z)])
                EV(lambda v, dt=dt: v.tensor_mul(arb(4 * dt, T), deltf[:, :], sqb[:, :]))
            t_rhs = tk("v")

            wout = I["a_out_proj"] if a else I["b_W_out"]

            def evac_out(mt, pa, j, t_mm):
                if a:
                    ES(lambda s, mt=mt, pa=pa, j=j: s.activation(
                        arb(96 + 4 * mt, T)[:, SL(j * 512, (j + 1) * 512)], pa[:, 0:512], AF.Gelu),
                       waits=[("t", t_mm)])
                else:
                    t_e = EV(lambda v, pa=pa, j=j: v.tensor_copy(deltf[:, SL(j * 512, (j + 1) * 512)],
                                                                 pa[:, 0:512]),
                             waits=[("t", t_mm)] + ([("gd", tk("gd"))] if j == 0 else []))
                    if j == 3:
                        GPD(S["xres"][SL(mt * 128, (mt + 1) * 128), :], deltf[:, :],
                            waits=[("v", t_e)], accum_op=ALU.add)
            mm_proj(wout, NDI, ND, lambda kt: arb(4 * kt, T), evac_out, 64, rhs_wait=[("v", t_rhs)])

            if a:
                t_gelu = tk("s")

                def evac_glu(mt, pa, j, t_mm):
                    if mt < 8:
                        EV(lambda v, mt=mt, pa=pa, j=j: v.tensor_scalar_add(
                            arb(4 * mt, T)[:, SL(j * 512, (j + 1) * 512)], pa[:, 0:512],
                            vsm[:, SL(16 + mt, 17 + mt)]), waits=[("t", t_mm)])
                    else:
                        t_e = ES(lambda s, mt=mt, pa=pa, j=j: s.activation(
                            arb(32 + 4 * (mt - 8), T)[:, SL(j * 512, (j + 1) * 512)], pa[:, 0:512],
                            AF.Sigmoid, bias=vsm[:, SL(16 + mt, 17 + mt)]), waits=[("t", t_mm)])
                        if j == 3:
                            mt8 = mt - 8
                            t_h = EV(lambda v, mt8=mt8: v.tensor_tensor(deltf[:, :], arb(4 * mt8, T),
                                                                        arb(32 + 4 * mt8, T), ALU.mult),
                                     waits=[("s", t_e), ("gd", tk("gd"))])
                            GPD(S["xres"][SL(mt8 * 128, (mt8 + 1) * 128), :], deltf[:, :],
                                waits=[("v", t_h)], accum_op=ALU.add)
                mm_proj(I["a_glu_w"], ND, 16, lambda kt: arb(96 + 4 * kt, T), evac_glu, 64,
                        rhs_wait=[("s", t_gelu)])

        def ffn():
            def xln_t(dt):
                return arb(4 * dt, T)
            layer_norm(ND, xln_t, src_hbm=S["xres"])
            t_ln = tk("v")

            for tt in range(NTT):
                def fsc(t, tt=tt):
                    for kt in range(ND):
                        i = t.matmul(pss[5][:, 0:16], xln_t(kt)[:, SL(tt * 128, (tt + 1) * 128)],
                                     wgb[:, SL(kt * 16, (kt + 1) * 16)],
                                     start=(kt == 0), stop=(kt == ND - 1))
                    return i
                t_m = ET(fsc, waits=[("v", t_ln if tt == 0 else tk("v")), ("gd", t_gdc)])
                EV(lambda v, tt=tt: v.tensor_tensor(sco[:, SL(tt * 16, (tt + 1) * 16)], pss[5][:, 0:16],
                                                    bgb[:, 0:16], ALU.add), waits=[("t", t_m)])
            t_sco = tk("v")

            for tt in range(NTT):
                sc = sco[:, SL(tt * 16, (tt + 1) * 16)]

                def fr(v, tt=tt, sc=sc):
                    v.max(rtw[:, 0:8], sc)
                    v.max_index(rtw[:, 8:16].bitcast(U32), rtw[:, 0:8], sc)
                    v.tensor_copy(idxg[:, SL(tt, tt + 1)], rtw[:, 8:16].bitcast(U32)[:, 0:1])
                    return v.tensor_scalar_mul(rtw[:, 16:17], rtw[:, 0:1], -1.0)
                t_mx = EV(fr, waits=[("v", t_sco)])
                t_ex = ES(lambda s, sc=sc: s.activation(rtw[:, 32:48], sc, AF.Exp, bias=rtw[:, 16:17],
                                                        accum_out=rtw[:, 17:18]), waits=[("v", t_mx)])

                def fg(v, tt=tt):
                    v.reciprocal(rtw[:, 18:19], rtw[:, 17:18])
                    v.tensor_copy(gateb[:, SL(tt, tt + 1)], rtw[:, 18:19])
                    v.tensor_single_scalar(onehf[:, :], iotab[:, :], idxg[:, SL(tt, tt + 1)], ALU.is_equal)
                    return v.tensor_copy(oneh[:, :], onehf[:, :])
                t_oh = EV(fg, waits=[("s", t_ex)])
                ET(lambda t: t.matmul(pss[6][:, 0:16], slob[:, :], oneh[:, :], start=True, stop=True),
                   waits=[("v", t_oh)])
                t_cm = ET(lambda t: t.matmul(pss[7][0:16, 0:1], oneh[:, :], onesc[:, :],
                                             start=True, stop=True))

                def fp(v, tt=tt):
                    v.tensor_mul(rtw[:, 32:48], onehf[:, :], pss[6][:, 0:16])
                    v.reduce_sum(posg[:, SL(tt, tt + 1)], rtw[:, 32:48], AX.X)
                    return v.tensor_copy(cnts[0:16, SL(tt, tt + 1)], pss[7][0:16, 0:1])
                EV(fp, waits=[("t", t_cm)])
            t_rt = tk("v")

            def fcs(v):
                v.tensor_tensor_scan(cnts[0:16, 16:32], ones4[:, :], cnts[0:16, 0:16], 0.0, ALU.mult, ALU.add)
                v.memset(cnts[0:16, 32:33], 0.0)
                return v.tensor_copy(cnts[0:16, 33:48], cnts[0:16, 16:31])
            t_cs = EV(fcs, waits=[("v", t_rt)])
            t_ci = GPD(S["cc_c_in"][0:1, :], cnts[0:16, 31:32], waits=[("v", t_cs)])
            t_cc1 = EG(lambda gp: gp.collective_compute("AllGather", ALU.bypass,
                                                        replica_groups=[[0, 1, 2, 3, 4, 5, 6, 7]],
                                                        ins=[S["cc_c_in"][:]], outs=[S["C_all"][:]]),
                       waits=[("gd", t_ci)])
            t_c8 = GPD(cnt8[:, :], S["C_all"][:, :], waits=[("g", t_cc1)])
            t_mk = EV(lambda v: v.tensor_tensor(mask8[:, :], iota8[:, :], cidb[0:8, 0:1], ALU.is_lt),
                      waits=[("gd", t_c8), ("v", t_cid)])
            t_oc = ET(lambda t: t.matmul(pss[5][0:1, 0:16], mask8[:, :], cnt8[:, :], start=True, stop=True),
                      waits=[("v", t_mk)])
            t_oce = EV(lambda v: v.tensor_copy(arow[0:1, 0:16], pss[5][0:1, 0:16]), waits=[("t", t_oc)])
            t_occ = GPD(cnts[0:16, 48:49], arow[0:1, 0:16], waits=[("v", t_oce)])
            t_toff = EV(lambda v: v.tensor_scalar_add(cnts[0:16, 32:48], cnts[0:16, 32:48],
                                                      cnts[0:16, 48:49]), waits=[("gd", t_occ)])
            t_tfr = GPD(xdbf[0:1, 0:256], cnts[0:16, 32:48], waits=[("v", t_toff)])
            t_tb = 0
            for j in range(1):
                t_bb = ET(lambda t: t.matmul(pss[5][:, 0:256], onesf[0:1, :], xdbf[0:1, 0:256],
                                             start=True, stop=True), waits=[("gd", t_tfr)])
                t_tb = EV(lambda v: v.tensor_copy(toffb[:, 0:256], pss[5][:, 0:256]), waits=[("t", t_bb)])

            for tt in range(NTT):
                def fq(v, tt=tt):
                    v.tensor_single_scalar(onehf[:, :], iotab[:, :], idxg[:, SL(tt, tt + 1)], ALU.is_equal)
                    v.tensor_mul(rtw[:, 32:48], onehf[:, :],
                                 toffb.rearrange("p (e tt) -> p e tt", tt=16)[:, :, SL(tt, tt + 1)])
                    v.reduce_sum(rtw[:, 19:20], rtw[:, 32:48], AX.X)
                    v.tensor_add(posg[:, SL(tt, tt + 1)], posg[:, SL(tt, tt + 1)], rtw[:, 19:20])
                    v.tensor_single_scalar(rtw[:, 20:21], posg[:, SL(tt, tt + 1)], 1024.0, ALU.is_lt)
                    v.tensor_mul(gateb[:, SL(tt, tt + 1)], gateb[:, SL(tt, tt + 1)], rtw[:, 20:21])
                    v.tensor_scalar_mul(rtw[:, 21:22], idxg[:, SL(tt, tt + 1)], 1024.0)
                    v.tensor_add(rtw[:, 21:22], rtw[:, 21:22], posg[:, SL(tt, tt + 1)])
                    v.tensor_mul(rtw[:, 21:22], rtw[:, 21:22], rtw[:, 20:21])
                    v.tensor_copy(slotc[:, SL(tt, tt + 1)], rtw[:, 21:22])
                    v.tensor_copy(meta2[:, 0:1], idxg[:, SL(tt, tt + 1)])
                    return v.tensor_copy(meta2[:, 1:2], posg[:, SL(tt, tt + 1)])
                t_q = EV(fq, waits=[("v", t_tb), ("gd", tk("gd"))])
                GPD(S["cc_m_in"][SL(tt * 128, (tt + 1) * 128), :], meta2[:, 0:2], waits=[("v", t_q)])
            t_meta = tk("gd")

            for tt in range(NTT):
                for dc in range(ND):
                    t_tp = ET(lambda t, tt=tt, dc=dc: t.transpose(
                        pss[5 + (dc % 2)].bitcast(BF16)[:, 0:128],
                        xln_t(dc)[:, SL(tt * 128, (tt + 1) * 128)], eyeb[:, :]),
                        waits=[("v", tk("v")), ("gd", tk("gd")), ("ld", t_ldc)])
                    EV(lambda v, tt=tt, dc=dc: v.tensor_copy(
                        arb(32 + 2 * (tt % 2), D)[:, SL(dc * 128, (dc + 1) * 128)],
                        pss[5 + (dc % 2)].bitcast(BF16)[:, 0:128]), waits=[("t", t_tp)])
                GPD(S["cc_x_in"][SL(tt * 128, (tt + 1) * 128), :], arb(32 + 2 * (tt % 2), D),
                    waits=[("v", tk("v"))])
            t_xr = tk("gd")
            t_ccx = EG(lambda gp: gp.collective_compute("AllGather", ALU.bypass,
                                                        replica_groups=[[0, 1, 2, 3, 4, 5, 6, 7]],
                                                        ins=[S["cc_x_in"][:]], outs=[S["X_all"][:]]),
                       waits=[("gd", t_xr)])
            t_ccm = EG(lambda gp: gp.collective_compute("AllGather", ALU.bypass,
                                                        replica_groups=[[0, 1, 2, 3, 4, 5, 6, 7]],
                                                        ins=[S["cc_m_in"][:]], outs=[S["M_all"][:]]),
                       waits=[("gd", t_meta)])

            t_z = EV(lambda v: v.memset(stage[:, 0:D], 0.0), waits=[("gd", tk("gd"))])
            t_bz = GPD(S["B_own"].rearrange("(r p) d -> p r d", p=128),
                       stage[:, 0:D].unsqueeze(1).broadcast_to([128, 17, D]), waits=[("v", t_z)])

            t_mm2 = GPD(meta2[:, 0:256], S["M_all"].rearrange("(pp gp) -> pp gp", pp=128)
                        if False else S["M_all"].rearrange("(pp g) two -> pp (g two)", pp=128),
                        waits=[("g", t_ccm)])
            eyef = arf(46, 128)
            t_ef = EV(lambda v: v.tensor_copy(eyef[:, :], eyeb[:, :]), waits=[("gd", t_mm2)])

            def fsi(v):
                mv = meta2[:, 0:256].rearrange("p (g two) -> p g two", two=2)
                mi = mv[:, :, SL(0, 1)]
                mp = mv[:, :, SL(1, 2)]
                m0 = meta2[:, 256:384]
                m1 = meta2[:, 384:512]
                kp = meta2[:, 512:640]
                v.tensor_single_scalar(m0, mi, cidb[:, 1:2], ALU.is_equal)
                v.tensor_single_scalar(m1, mi, cidb[:, 2:3], ALU.is_equal)
                v.tensor_single_scalar(kp, mp, 1024.0, ALU.is_lt)
                v.tensor_add(m0, m0, m1)
                v.tensor_mul(m0, m0, kp)
                v.tensor_scalar_mul(m1, m1, 1024.0)
                v.tensor_tensor(m1, m1, mp, ALU.add)
                v.tensor_scalar_sub(m1, m1, cidb[:, 3:4])
                v.tensor_mul(m1, m1, m0)
                v.tensor_scalar_add(m1, m1, cidb[:, 3:4])
            t_sif = EV(fsi, waits=[("v", t_ef), ("v", t_cid)])
            t_tps = ET(lambda t: t.transpose(pss[5][:, 0:128], meta2[:, 384:512], eyef[:, :]),
                       waits=[("v", t_sif)])
            t_si = EV(lambda v: v.tensor_copy(scat_i[:, :], pss[5][:, 0:128]), waits=[("t", t_tps)])

            for gq in range(128):
                slq = gq % 2
                t_ld = SYD(stage[:, SL(slq * D, (slq + 1) * D)], S["X_all"][SL(gq * 128, (gq + 1) * 128), :],
                           waits=([("gd", tk("gd"))] if gq >= 2 else [("gd", t_bz), ("g", t_ccx), ("v", t_si)]))
                GPI(S["B_own"][:, :], IndirectOffsetOnAxis(scat_i[:, SL(gq, gq + 1)], 0),
                    stage[:, SL(slq * D, (slq + 1) * D)], None, waits=[("ld", t_ld)])
            t_scat = tk("gd")

            t_b1 = SYD(b1col.rearrange("p (e ht) -> p e ht", e=2), I["f_b1c"].rearrange("e (ht p) -> p e ht", p=128))
            t_b2 = SYD(b2col.rearrange("p (e mt) -> p e mt", e=2), I["f_b2c"].rearrange("e (mt p) -> p e mt", p=128))
            for ei in range(2):
                for dc in range(ND):
                    SYD(arb(32 + 2 * dc, CAP), S["B_own"][SL(ei * CAP, (ei + 1) * CAP),
                                                          SL(dc * 128, (dc + 1) * 128)],
                        transpose=True,
                        waits=[("gd", t_scat), ("t", tk("t")), ("v", tk("v"))])
                t_xe = tk("ld")

                def xet(dc):
                    return arb(32 + 2 * dc, CAP)
                t_y0 = EV(lambda v: v.memset(arf(0, ND * CAP), 0.0), waits=[("gd", tk("gd")), ("s", tk("s")),
                                                                            ("t", tk("t"))])
                for hh in range(2):
                    t_w1 = GPD(arb(64, ND * 16 * 128).rearrange("p (k m c) -> p k m c", k=ND, m=16, c=128),
                               I["f_W1c"][ei, :, SL(hh * 2048, (hh + 1) * 2048)]
                               .rearrange("(k p) (m c) -> p k m c", p=128, c=128),
                               waits=[("t", tk("t")), ("gd", tk("gd"))])
                    for mt in range(16):
                        for j in range(2):
                            pa = pss[(mt * 2 + j) % 4]
                            guard = [("s", tk("s"))]

                            def fm1(t, mt=mt, j=j, pa=pa):
                                for kt in range(ND):
                                    i = t.matmul(pa[:, 0:512],
                                                 arb(64, ND * 16 * 128)[:, SL((kt * 16 + mt) * 128,
                                                                              (kt * 16 + mt + 1) * 128)],
                                                 xet(kt)[:, SL(j * 512, (j + 1) * 512)],
                                                 start=(kt == 0), stop=(kt == ND - 1))
                                return i
                            t_mm = ET(fm1, waits=[("gd", t_w1), ("ld", t_xe)] + guard)
                            ES(lambda s, mt=mt, j=j, pa=pa, hh=hh, ei=ei: s.activation(
                                arb(128 + 2 * mt, CAP)[:, SL(j * 512, (j + 1) * 512)], pa[:, 0:512],
                                AF.Relu, bias=b1col[:, SL(ei * 32 + hh * 16 + mt, ei * 32 + hh * 16 + mt + 1)]),
                               waits=[("t", t_mm), ("ld", t_b1)])
                    t_relu = tk("s")
                    t_w2 = GPD(arb(96, 16 * ND * 128).rearrange("p (k m c) -> p k m c", k=16, m=ND, c=128),
                               I["f_W2c"][ei, SL(hh * 2048, (hh + 1) * 2048), :]
                               .rearrange("(k p) (m c) -> p k m c", p=128, c=128),
                               waits=[("t", tk("t"))])
                    for mt in range(ND):
                        for j in range(2):
                            pa = pss[4 + (mt * 2 + j) % 4]
                            guard = [("v", tk("v"))]

                            def fm2(t, mt=mt, j=j, pa=pa):
                                for kt in range(16):
                                    i = t.matmul(pa[:, 0:512],
                                                 arb(96, 16 * ND * 128)[:, SL((kt * ND + mt) * 128,
                                                                              (kt * ND + mt + 1) * 128)],
                                                 arb(128 + 2 * kt, CAP)[:, SL(j * 512, (j + 1) * 512)],
                                                 start=(kt == 0), stop=(kt == 15))
                                return i
                            t_mm = ET(fm2, waits=[("gd", t_w2), ("s", t_relu)] + guard)
                            EV(lambda v, mt=mt, j=j, pa=pa: v.tensor_tensor(
                                arf(4 * mt, CAP)[:, SL(j * 512, (j + 1) * 512)],
                                arf(4 * mt, CAP)[:, SL(j * 512, (j + 1) * 512)], pa[:, 0:512], ALU.add),
                               waits=[("t", t_mm), ("v", t_y0)])
                t_yv = tk("v")
                for mt in range(ND):
                    EV(lambda v, mt=mt, ei=ei: v.tensor_scalar_add(arb(48 + 2 * mt, CAP), arf(4 * mt, CAP),
                                                                   b2col[:, SL(ei * 8 + mt, ei * 8 + mt + 1)]),
                       waits=[("v", t_yv), ("ld", t_b2)] + ([("ld", tk("ld"))] if mt == 0 else []))
                t_yb = tk("v")
                for st in range(8):
                    for mt in range(ND):
                        t_tp = ET(lambda t, st=st, mt=mt: t.transpose(
                            pss[5 + (mt % 2)].bitcast(BF16)[:, 0:128],
                            arb(48 + 2 * mt, CAP)[:, SL(st * 128, (st + 1) * 128)], eyeb[:, :]),
                            waits=[("v", tk("v")), ("gd", tk("gd"))])
                        EV(lambda v, st=st, mt=mt: v.tensor_copy(
                            arb(64 + 2 * (st % 2), D)[:, SL(mt * 128, (mt + 1) * 128)],
                            pss[5 + (mt % 2)].bitcast(BF16)[:, 0:128]), waits=[("t", t_tp)])
                    GPD(S["cc_y_in"][SL(ei * CAP + st * 128, ei * CAP + (st + 1) * 128), :],
                        arb(64 + 2 * (st % 2), D), waits=[("v", tk("v"))])
            t_yrows = tk("gd")
            t_ccy = EG(lambda gp: gp.collective_compute("AllGather", ALU.bypass,
                                                        replica_groups=[[0, 1, 2, 3, 4, 5, 6, 7]],
                                                        ins=[S["cc_y_in"][:]], outs=[S["Y_all"][:]]),
                       waits=[("gd", t_yrows)])

            for tt in range(NTT):
                slq = tt % 2
                t_g = GPI(arb(32 + 2 * slq, D), None, S["Y_all"][:, :],
                          IndirectOffsetOnAxis(slotc[:, SL(tt, tt + 1)], 0),
                          waits=[("g", t_ccy), ("v", tk("v")), ("t", tk("t"))])
                t_gt = EV(lambda v, tt=tt, slq=slq: v.tensor_scalar_mul(arb(32 + 2 * slq, D),
                                                                        arb(32 + 2 * slq, D),
                                                                        gateb[:, SL(tt, tt + 1)]),
                          waits=[("gd", t_g)])
                for dc in range(ND):
                    t_tp = ET(lambda t, tt=tt, dc=dc, slq=slq: t.transpose(
                        pss[5 + (dc % 2)].bitcast(BF16)[:, 0:128],
                        arb(32 + 2 * slq, D)[:, SL(dc * 128, (dc + 1) * 128)], eyeb[:, :]),
                        waits=[("v", t_gt if dc == 0 else tk("v"))])
                    EV(lambda v, tt=tt, dc=dc: v.tensor_copy(
                        arb(4 * dc, T)[:, SL(tt * 128, (tt + 1) * 128)],
                        pss[5 + (dc % 2)].bitcast(BF16)[:, 0:128]), waits=[("t", t_tp)])
            t_asm = tk("v")
            for dc in range(ND):
                t_cp = EV(lambda v, dc=dc: v.tensor_copy(deltf[:, :], arb(4 * dc, T)),
                          waits=[("v", t_asm), ("gd", tk("gd"))])
                GPD(S["xres"][SL(dc * 128, (dc + 1) * 128), :], deltf[:, :],
                    waits=[("v", t_cp)], accum_op=ALU.add)

        for ph in PHASES:
            if ph == "a":
                mamba("a")
            elif ph == "b":
                mamba("b")
            elif ph == "f":
                ffn()

        WAIT("sync", ALLW())
        for dt in range(ND):
            SYD(out_h[SL(dt * 128, (dt + 1) * 128), :], S["xres"][SL(dt * 128, (dt + 1) * 128), :])
        WAIT("sync", [("ld", tk("ld"))])
        WAIT("gpsimd", [("ld", tk("ld"))])

    return nc


def host_constants():
    import ml_dtypes
    eye = np.eye(128, dtype=np.float32).astype(ml_dtypes.bfloat16)
    slo = np.triu(np.ones((128, 128), np.float32), 1).astype(ml_dtypes.bfloat16)
    sel = np.zeros((NEXP, NEXP * 128), np.float32)
    for n in range(NEXP):
        sel[n, n * 128:(n + 1) * 128] = 1.0
    return {
        "c_eye": eye,
        "c_slo": slo,
        "c_sel": sel.astype(ml_dtypes.bfloat16),
        "c_iota16b": np.broadcast_to(np.arange(NEXP, dtype=np.float32), (128, NEXP)).copy(),
        "c_iota8c": np.arange(8, dtype=np.float32).reshape(8, 1),
        "c_iota128c": np.arange(128, dtype=np.float32).reshape(128, 1),
    }


def make_in_maps(inp):
    x = inp["L0_attn_a_x"]
    consts = host_constants()
    in_maps = []
    for c in range(8):
        m = {
            "xT": np.ascontiguousarray(x[c].T),
            "a_in_proj": inp["L0_attn_a_in_proj_w"],
            "a_conv_w": inp["L0_attn_a_conv_w"],
            "a_conv_b": inp["L0_attn_a_conv_b"],
            "a_x_proj": inp["L0_attn_a_x_proj_w"],
            "a_dt_proj": inp["L0_attn_a_dt_proj_w"],
            "a_dt_b": inp["L0_attn_a_dt_proj_b"],
            "a_A_log": inp["L0_attn_a_A_log"],
            "a_D": inp["L0_attn_a_D"],
            "a_out_proj": inp["L0_attn_a_out_proj_w"],
            "a_glu_w": inp["L0_attn_a_glu_w"],
            "a_glu_b": inp["L0_attn_a_glu_b"],
            "a_ln_g": inp["L0_attn_a_ln_gamma"],
            "a_ln_b": inp["L0_attn_a_ln_beta"],
            "b_W_in": inp["L0_attn_b_W_in"],
            "b_conv_w": np.ascontiguousarray(inp["L0_attn_b_conv_w"].reshape(DI, 4)),
            "b_conv_b": inp["L0_attn_b_conv_b"],
            "b_W_x": inp["L0_attn_b_W_x"],
            "b_W_dt": np.ascontiguousarray(inp["L0_attn_b_W_dt"].reshape(DI)),
            "b_b_dt": inp["L0_attn_b_b_dt"],
            "b_A_log": inp["L0_attn_b_A_log"].reshape(1, NST),
            "b_D": inp["L0_attn_b_D"],
            "b_W_out": inp["L0_attn_b_W_out"],
            "f_Wg": inp["L0_ffn_Wg"],
            "f_bg": inp["L0_ffn_bg"].reshape(1, NEXP),
            "f_W1c": np.ascontiguousarray(inp["L0_ffn_W1"][2 * c:2 * c + 2]),
            "f_b1c": np.ascontiguousarray(inp["L0_ffn_b1"][2 * c:2 * c + 2]),
            "f_W2c": np.ascontiguousarray(inp["L0_ffn_W2"][2 * c:2 * c + 2]),
            "f_b2c": np.ascontiguousarray(inp["L0_ffn_b2"][2 * c:2 * c + 2]),
            "c_cid": np.array([[float(c)]], dtype=np.float32),
        }
        m.update(consts)
        in_maps.append(m)
    return in_maps


def kernel(**inp):
    nc = build_program()
    in_maps = make_in_maps(inp)
    res = run_bass_kernel_spmd(nc, in_maps, core_ids=list(range(8)))
    return np.stack([np.ascontiguousarray(res.results[c]["outT"].T) for c in range(8)], axis=0)


# version 02
# speedup vs baseline: 1.2074x; 1.2074x over previous
import numpy as np
import concourse.bass as bass
import concourse.mybir as mybir
from concourse.bass import IndirectOffsetOnAxis
from concourse.bass_utils import run_bass_kernel_spmd

F32 = mybir.dt.float32
BF16 = mybir.dt.bfloat16
I32 = mybir.dt.int32
U32 = mybir.dt.uint32
AF = mybir.ActivationFunctionType
ALU = mybir.AluOpType
AX = mybir.AxisListType

D = 1024
T = 2048
DI = 2048
NST = 16
NEXP = 16
CAP = 1024
HF = 4096
ND = D // 128
NDI = DI // 128
NTT = T // 128
EPS = 1e-5
SL = slice

DEBUG_TAPS = {}
PHASES = ("a", "f", "b", "f")


def build_program():
    nc = bass.Bass("TRN2", target_bir_lowering=False, debug=False, num_devices=8)

    def din(name, shape, dt=F32):
        return nc.dram_tensor(name, shape, dt, kind="ExternalInput").ap()

    I = {}
    I["xT"] = din("xT", [D, T])
    I["a_in_proj"] = din("a_in_proj", [D, 2 * DI])
    I["a_conv_w"] = din("a_conv_w", [DI, 4])
    I["a_conv_b"] = din("a_conv_b", [DI])
    I["a_x_proj"] = din("a_x_proj", [DI, 96])
    I["a_dt_proj"] = din("a_dt_proj", [64, DI])
    I["a_dt_b"] = din("a_dt_b", [DI])
    I["a_A_log"] = din("a_A_log", [DI, NST])
    I["a_D"] = din("a_D", [DI])
    I["a_out_proj"] = din("a_out_proj", [DI, D])
    I["a_glu_w"] = din("a_glu_w", [D, 2 * D])
    I["a_glu_b"] = din("a_glu_b", [2 * D])
    I["a_ln_g"] = din("a_ln_g", [D])
    I["a_ln_b"] = din("a_ln_b", [D])
    I["b_W_in"] = din("b_W_in", [D, 2 * DI])
    I["b_conv_w"] = din("b_conv_w", [DI, 4])
    I["b_conv_b"] = din("b_conv_b", [DI])
    I["b_W_x"] = din("b_W_x", [DI, 33])
    I["b_W_dt"] = din("b_W_dt", [DI])
    I["b_b_dt"] = din("b_b_dt", [DI])
    I["b_A_log"] = din("b_A_log", [1, NST])
    I["b_D"] = din("b_D", [DI])
    I["b_W_out"] = din("b_W_out", [DI, D])
    I["f_Wg"] = din("f_Wg", [D, NEXP])
    I["f_bg"] = din("f_bg", [1, NEXP])
    I["f_W1c"] = din("f_W1c", [2, D, HF])
    I["f_b1c"] = din("f_b1c", [2, HF])
    I["f_W2c"] = din("f_W2c", [2, HF, D])
    I["f_b2c"] = din("f_b2c", [2, D])
    I["c_eye"] = din("c_eye", [128, 128], BF16)
    I["c_slo"] = din("c_slo", [128, 128], BF16)
    I["c_sel"] = din("c_sel", [NEXP, NEXP * 128], BF16)
    I["c_iota16b"] = din("c_iota16b", [128, NEXP])
    I["c_iota8c"] = din("c_iota8c", [8, 1])
    I["c_iota128c"] = din("c_iota128c", [128, 1])
    I["c_cid"] = din("c_cid", [1, 1])

    out_h = nc.dram_tensor("outT", [D, T], F32, kind="ExternalOutput").ap()

    S = {}
    S["xres"] = nc.dram_tensor("xres", [D, T], F32).ap()
    S["u_sp"] = nc.dram_tensor("u_sp", [DI, T], BF16).ap()
    S["sz_sp"] = nc.dram_tensor("sz_sp", [DI, T], BF16).ap()
    S["dt_sp"] = nc.dram_tensor("dt_sp", [DI, T], F32).ap()
    S["w_sp"] = nc.dram_tensor("w_sp", [DI, T], BF16).ap()
    S["y_sp"] = nc.dram_tensor("y_sp", [DI, T], F32).ap()
    S["cc_x_in"] = nc.dram_tensor("cc_x_in", [T, D], BF16).ap()
    S["X_all"] = nc.dram_tensor("X_all", [8 * T, D], BF16, addr_space="Shared").ap()
    S["cc_m_in"] = nc.dram_tensor("cc_m_in", [T, 2], F32).ap()
    S["M_all"] = nc.dram_tensor("M_all", [8 * T, 2], F32, addr_space="Shared").ap()
    S["cc_c_in"] = nc.dram_tensor("cc_c_in", [1, NEXP], F32).ap()
    S["C_all"] = nc.dram_tensor("C_all", [8, NEXP], F32, addr_space="Shared").ap()
    S["B_own"] = nc.dram_tensor("B_own", [2 * CAP + 128, D], BF16).ap()
    S["cc_y_in"] = nc.dram_tensor("cc_y_in", [2 * CAP, D], BF16).ap()
    S["Y_all"] = nc.dram_tensor("Y_all", [NEXP * CAP, D], BF16, addr_space="Shared").ap()

    taps = {}
    for name, (shape, dt) in DEBUG_TAPS.items():
        taps[name] = nc.dram_tensor(name, shape, dt, kind="ExternalOutput").ap()

    AR = nc.alloc_sbuf_tensor("AR", [128, 80 * 1024], BF16)

    def arb(off_kb, nbf16):
        return AR[:, off_kb * 512: off_kb * 512 + nbf16]

    def arf(off_kb, nf32):
        return AR[:, off_kb * 512: off_kb * 512 + 2 * nf32].bitcast(F32)

    arow = nc.alloc_sbuf_tensor("arow", [1, 16], F32)
    onesall = nc.alloc_sbuf_tensor("onesall", [128, 128], BF16)
    xbf = nc.alloc_sbuf_tensor("xbf", [128, T], BF16)
    sqb = nc.alloc_sbuf_tensor("sqb", [128, T], BF16)
    rbc = None
    mbc = None
    deltf = nc.alloc_sbuf_tensor("deltf", [128, T], F32)
    vsm = nc.alloc_sbuf_tensor("vsm", [128, 48], F32)
    pvec = nc.alloc_sbuf_tensor("pvec", [128, 96], F32)
    Acol = nc.alloc_sbuf_tensor("Acol", [128, 400], F32)
    selb = nc.alloc_sbuf_tensor("selb", [NEXP, NEXP * 128], BF16)
    eyeb = nc.alloc_sbuf_tensor("eyeb", [128, 128], BF16)
    slob = nc.alloc_sbuf_tensor("slob", [128, 128], BF16)
    onesc = nc.alloc_sbuf_tensor("onesc", [128, 1], BF16)
    ones4 = nc.alloc_sbuf_tensor("ones4", [16, 16], F32)
    onesf = nc.alloc_sbuf_tensor("onesf", [1, 128], F32)
    iotab = nc.alloc_sbuf_tensor("iotab", [128, NEXP], F32)
    iota8 = nc.alloc_sbuf_tensor("iota8", [8, 1], F32)
    cidb = nc.alloc_sbuf_tensor("cidb", [128, 4], F32)
    rbc = arb(112, T)
    mbc = arb(116, T)
    xdbf = arf(112, T)[0:96, :]
    xdbb = arb(120, T)[0:64, :]
    dtpw = arb(124, DI)[0:64, :]
    brow = arb(128, T)[0:16, :]
    crow = arb(132, T)[0:16, :]
    wgb = nc.alloc_sbuf_tensor("wgb", [128, ND * NEXP], BF16)
    bgb = nc.alloc_sbuf_tensor("bgb", [128, NEXP], F32)
    sco = arf(56, NTT * NEXP)
    rtw = arf(60, 64)
    oneh = arb(103, NEXP)
    onehf = arf(51, NEXP)
    posg = arf(61, NTT)
    idxg = arf(62, NTT)
    gateb = nc.alloc_sbuf_tensor("gateb", [128, NTT], F32)
    slotc = nc.alloc_sbuf_tensor("slotc", [128, NTT], I32)
    cnts = arf(52, 64)[0:16, :]
    cnt8 = arf(53, NEXP)[0:8, :]
    mask8 = arf(54, 1)[0:8, :]
    toffb = arf(58, NTT * NEXP)
    meta2 = arf(48, 768)
    scat_i = AR[:, 55 * 512: 55 * 512 + 256].bitcast(I32)
    stage = arb(64, 2 * D)
    b1col = nc.alloc_sbuf_tensor("b1col", [128, 64], F32)
    kcol = nc.alloc_sbuf_tensor("kcol", [128, 2], F32)
    b2col = nc.alloc_sbuf_tensor("b2col", [128, 16], F32)

    pss = [nc.alloc_psum_tensor("ps%d" % i, [128, 512], F32) for i in range(8)]

    with (
        nc.allow_non_contiguous_dma(reason="small per-partition vector loads"),
        nc.semaphore("sm_ld") as sm_ld,
        nc.semaphore("sm_gd") as sm_gd,
        nc.semaphore("sm_t") as sm_t,
        nc.semaphore("sm_v") as sm_v,
        nc.semaphore("sm_s") as sm_s,
        nc.semaphore("sm_g") as sm_g,
        nc.semaphore("sm_le") as sm_le,
        nc.semaphore("sm_ge") as sm_ge,
        nc.Block() as blk,
    ):
        sems = dict(ld=sm_ld, gd=sm_gd, t=sm_t, v=sm_v, s=sm_s, g=sm_g, le=sm_le, ge=sm_ge)
        cnt = {k: 0 for k in sems}
        echo = {"ld": [0, 0], "gd": [0, 0]}
        _q_eng = {"ld": "sync", "gd": "gpsimd"}
        _q_echo = {"ld": "le", "gd": "ge"}

        def _emit_echo(qname):
            raw = cnt[qname]
            esem = _q_echo[qname]

            def f(eng):
                eng.wait_ge(sems[qname], raw)
                eng.sem_inc(sems[esem], 1)
            getattr(blk, _q_eng[qname])(f)
            cnt[esem] += 1
            echo[qname][0] = raw
            echo[qname][1] = cnt[esem]
            return cnt[esem]

        def _xlate(waits, engname):
            out = []
            for nm, val in waits:
                if val <= 0:
                    continue
                if nm in ("ld", "gd"):
                    if engname == _q_eng[nm]:
                        out.append((nm, cnt[nm]))
                    else:
                        if echo[nm][0] < val:
                            _emit_echo(nm)
                        out.append((_q_echo[nm], echo[nm][1]))
                else:
                    out.append((nm, val))
            return out

        class _Proxy:

            def __init__(self, eng, semname, self_wait=True):
                self._eng = eng
                self._sn = semname
                self._sw = self_wait

            def __getattr__(self, name):
                real = getattr(self._eng, name)
                if not callable(real) or name.startswith("wait") or name == "dma_start":
                    return real

                def call(*a, **k):
                    if self._sw and cnt[self._sn] > 0:
                        self._eng.wait_ge(sems[self._sn], cnt[self._sn])
                    inst = real(*a, **k)
                    inst.then_inc(sems[self._sn], 1)
                    cnt[self._sn] += 1
                    return inst
                return call

        def _mk(engname, semname):
            self_wait = semname != "t"

            def emit(build, waits=()):
                tw = _xlate(waits, engname)
                def f(eng):
                    for nm, val in tw:
                        if val > 0:
                            eng.wait_ge(sems[nm], val)
                    build(_Proxy(eng, semname, self_wait))
                getattr(blk, engname)(f)
                return cnt[semname]
            return emit

        EV = _mk("vector", "v")
        ES = _mk("scalar", "s")
        ET = _mk("tensor", "t")
        EG = _mk("gpsimd", "g")

        def SYD(out, in_, waits=(), **kw):
            tw = _xlate(waits, "sync")
            def f(sy):
                for nm, val in tw:
                    if val > 0:
                        sy.wait_ge(sems[nm], val)
                sy.dma_start(out=out, in_=in_, **kw).then_inc(sems["ld"], 16)
            blk.sync(f)
            cnt["ld"] += 16
            return cnt["ld"]

        def GPD(out, in_, waits=(), **kw):
            tw = _xlate(waits, "gpsimd")
            def f(gp):
                for nm, val in tw:
                    if val > 0:
                        gp.wait_ge(sems[nm], val)
                gp.dma_start(out=out, in_=in_, **kw).then_inc(sems["gd"], 16)
            blk.gpsimd(f)
            cnt["gd"] += 16
            return cnt["gd"]

        def GPI(out, out_off, in_, in_off, waits=()):
            tw = _xlate(waits, "gpsimd")
            def f(gp):
                for nm, val in tw:
                    if val > 0:
                        gp.wait_ge(sems[nm], val)
                gp.indirect_dma_start(out, out_off, in_, in_off).then_inc(sems["gd"], 16)
            blk.gpsimd(f)
            cnt["gd"] += 16
            return cnt["gd"]

        def tk(nm):
            return cnt[nm]

        def WAIT(engname, waits):
            tw = _xlate(waits, engname)
            def f(eng):
                for nm, val in tw:
                    if val > 0:
                        eng.wait_ge(sems[nm], val)
            getattr(blk, engname)(f)

        def ALLW():
            return [("v", tk("v")), ("s", tk("s")), ("t", tk("t")),
                    ("g", tk("g")), ("gd", tk("gd")), ("ld", tk("ld"))]

        SYD(eyeb[:], I["c_eye"][:])
        SYD(slob[:], I["c_slo"][:])
        SYD(selb[:], I["c_sel"][:])
        SYD(iotab[:], I["c_iota16b"][:])
        SYD(iota8[:], I["c_iota8c"][:])
        SYD(cidb[:, 3:4], I["c_iota128c"][:])
        SYD(cidb[:, 0:1], I["c_cid"][0:1, 0:1].partition_broadcast(128))
        SYD(S["xres"][:, :], I["xT"][:, :])
        SYD(pvec[:, 0:16], I["a_conv_b"].rearrange("(dt p) -> p dt", p=128))
        SYD(pvec[:, 16:32], I["a_dt_b"].rearrange("(dt p) -> p dt", p=128))
        SYD(pvec[:, 32:48], I["a_D"].rearrange("(dt p) -> p dt", p=128))
        SYD(pvec[:, 48:64], I["b_conv_b"].rearrange("(dt p) -> p dt", p=128))
        SYD(pvec[:, 64:80], I["b_b_dt"].rearrange("(dt p) -> p dt", p=128))
        SYD(pvec[:, 80:96], I["b_W_dt"].rearrange("(dt p) -> p dt", p=128))
        SYD(vsm[:, 0:8], I["a_ln_g"].rearrange("(dt p) -> p dt", p=128))
        SYD(vsm[:, 8:16], I["a_ln_b"].rearrange("(dt p) -> p dt", p=128))
        SYD(vsm[:, 16:32], I["a_glu_b"].rearrange("(dt p) -> p dt", p=128))
        SYD(vsm[:, 32:48], I["b_D"].rearrange("(dt p) -> p dt", p=128))
        SYD(Acol[:, 0:256].rearrange("p (dt n) -> p dt n", n=16),
            I["a_A_log"].rearrange("(dt p) n -> p dt n", p=128))
        SYD(Acol[:, 272:336].rearrange("p (dt k) -> p dt k", k=4),
            I["a_conv_w"].rearrange("(dt p) k -> p dt k", p=128))
        SYD(Acol[:, 336:400].rearrange("p (dt k) -> p dt k", k=4),
            I["b_conv_w"].rearrange("(dt p) k -> p dt k", p=128))
        SYD(arow[0:1, 0:16], I["b_A_log"][0:1, :])
        t_bgrow = SYD(bgb[0:1, 0:16], I["f_bg"][0:1, :])
        GPD(dtpw[:, :], I["a_dt_proj"][:, :])
        GPD(wgb.rearrange("p (k e) -> p k e", e=NEXP), I["f_Wg"].rearrange("(k p) e -> p k e", p=128))
        t_ldc = tk("ld")
        t_gdc = tk("gd")

        EV(lambda v: v.memset(onesf[:, :], 1.0))
        EV(lambda v: v.memset(ones4[:, :], 1.0))
        EV(lambda v: v.memset(onesall[:, :], 1.0))
        EV(lambda v: v.memset(kcol[:, 0:1], 1.0))
        EV(lambda v: v.memset(kcol[:, 1:2], EPS))
        t_ones = EV(lambda v: v.memset(onesc[:, :], 1.0))

        def f(v):
            v.tensor_scalar_mul(cidb[:, 1:2], cidb[:, 0:1], 2.0)
            v.tensor_scalar_add(cidb[:, 2:3], cidb[:, 1:2], 1.0)
            v.tensor_scalar_add(cidb[:, 3:4], cidb[:, 3:4], 2048.0)
        t_cid = EV(f, waits=[("ld", t_ldc)])

        ES(lambda s: s.activation(Acol[:, 0:256], Acol[:, 0:256], AF.Exp), waits=[("ld", t_ldc)])
        t_expA = ES(lambda s: s.activation(arow[0:1, 0:16], arow[0:1, 0:16], AF.Exp))
        EV(lambda v: v.tensor_scalar_mul(Acol[:, 0:256], Acol[:, 0:256], -1.0), waits=[("s", t_expA)])
        t_abrow = EV(lambda v: v.tensor_scalar_mul(arow[0:1, 0:16], arow[0:1, 0:16], -1.0))
        t_mab = ET(lambda t: t.matmul(pss[0][:, 0:16], onesf[0:1, :], arow[0:1, 0:16],
                                      start=True, stop=True), waits=[("v", t_abrow)])
        EV(lambda v: v.tensor_copy(Acol[:, 256:272], pss[0][:, 0:16]), waits=[("t", t_mab)])
        t_mbg = ET(lambda t: t.matmul(pss[1][:, 0:16], onesf[0:1, :], bgb[0:1, 0:16],
                                      start=True, stop=True), waits=[("ld", t_bgrow), ("v", tk("v"))])
        EV(lambda v: v.tensor_copy(bgb[:, 0:16], pss[1][:, 0:16]), waits=[("t", t_mbg)])

        def layer_norm(nd, dst_tile, src_hbm=None, src_tile=None, gamma_beta=None):
            tprev_t = tk("t")
            t_entry_v = tk("v")
            for dt in range(nd):
                if src_hbm is not None:
                    t_x = GPD(xbf[:, :], src_hbm[SL(dt * 128, (dt + 1) * 128), :],
                              waits=[("t", tprev_t), ("s", tk("s")), ("gd", tk("gd"))])
                    xsrc = xbf[:, :]
                    wx = [("gd", t_x)]
                else:
                    xsrc = src_tile(dt)
                    wx = []
                t_sq = ES(lambda s, xsrc=xsrc: s.activation(sqb[:, :], xsrc, AF.Square),
                          waits=wx + [("t", tprev_t), ("v", t_entry_v)])

                def fmm(t, dt=dt, xsrc=xsrc):
                    for j in range(4):
                        t.matmul(pss[j][:, 0:512], onesall[:, :], xsrc[:, SL(j * 512, (j + 1) * 512)],
                                 start=(dt == 0), stop=(dt == nd - 1))
                    for j in range(4):
                        t.matmul(pss[4 + j][:, 0:512], onesall[:, :], sqb[:, SL(j * 512, (j + 1) * 512)],
                                 start=(dt == 0), stop=(dt == nd - 1))
                extra = [("s", tk("s")), ("v", tk("v"))] if dt == 0 else []
                tprev_t = ET(fmm, waits=wx + [("s", t_sq), ("v", t_ones)] + extra)
            nf = float(nd * 128)
            muf = deltf[:, :]
            exf = arf(96, T)
            m2f = arf(104, T)

            def fst(v):
                for j in range(4):
                    v.tensor_scalar_mul(muf[:, SL(j * 512, (j + 1) * 512)], pss[j][:, 0:512], 1.0 / nf)
                    v.tensor_scalar_mul(exf[:, SL(j * 512, (j + 1) * 512)], pss[4 + j][:, 0:512], 1.0 / nf)
                v.tensor_mul(m2f, muf, muf)
                v.tensor_sub(exf, exf, m2f)
                v.tensor_scalar_add(exf, exf, EPS)
            t_var = EV(fst, waits=[("t", tprev_t), ("gd", tk("gd"))])
            t_sd = ES(lambda s: s.activation(exf, exf, AF.Sqrt), waits=[("v", t_var)])

            def frc(v):
                v.reciprocal(m2f, exf)
                v.tensor_copy(rbc[:, :], m2f)
                v.tensor_mul(mbc[:, :], muf, m2f)
            t_bc = EV(frc, waits=[("s", t_sd)])
            t_last = t_bc
            for dt in range(nd):
                if src_hbm is not None:
                    t_x = GPD(xbf[:, :], src_hbm[SL(dt * 128, (dt + 1) * 128), :],
                              waits=[("v", tk("v")), ("gd", tk("gd"))])
                    xsrc = xbf[:, :]
                    wx = [("gd", t_x)]
                else:
                    xsrc = src_tile(dt)
                    wx = []

                def fn(v, dt=dt, xsrc=xsrc):
                    dst = dst_tile(dt)
                    v.tensor_mul(sqb[:, :], xsrc, rbc[:, :])
                    if gamma_beta is None:
                        v.tensor_sub(dst, sqb[:, :], mbc[:, :])
                        return
                    gcol, bcol = gamma_beta
                    v.tensor_sub(sqb[:, :], sqb[:, :], mbc[:, :])
                    v.tensor_scalar(dst, sqb[:, :], gcol[:, SL(dt, dt + 1)],
                                    bcol[:, SL(dt, dt + 1)], ALU.mult, ALU.add)
                t_last = EV(fn, waits=wx + [("v", t_bc)])
            return t_last

        def mm_proj(w_hbm, kt_n, mt_n, rhs_tile, evac, woff_kb, rhs_wait=()):
            wb = arb(woff_kb, kt_n * mt_n * 128)
            t_w = GPD(wb.rearrange("p (k m c) -> p k m c", k=kt_n, m=mt_n, c=128),
                      w_hbm.rearrange("(k p) (m c) -> p k m c", p=128, c=128),
                      waits=[("t", tk("t"))])
            evtoks = []
            for mt in range(mt_n):
                for j in range(4):
                    idx = mt * 4 + j
                    pa = pss[idx % 4]

                    def fm(t, mt=mt, j=j, pa=pa):
                        for kt in range(kt_n):
                            i = t.matmul(pa[:, 0:512],
                                         wb[:, SL((kt * mt_n + mt) * 128, (kt * mt_n + mt + 1) * 128)],
                                         rhs_tile(kt)[:, SL(j * 512, (j + 1) * 512)],
                                         start=(kt == 0), stop=(kt == kt_n - 1))
                        return i
                    guard = [evtoks[idx - 4]] if idx >= 4 else []
                    t_mm = ET(fm, waits=[("gd", t_w)] + guard + list(rhs_wait))
                    evtoks.append(evac(mt, pa, j, t_mm))

        def conv_silu(dt, xi_ap, cw_off, cb_col, t_ready):
            def fc(v, dt=dt):
                def cw(k):
                    return Acol[:, SL(cw_off + dt * 4 + k, cw_off + dt * 4 + k + 1)]
                v.tensor_scalar_mul(deltf[:, :], xi_ap, cw(3))
                v.scalar_tensor_tensor(deltf[:, 1:T], xi_ap[:, 0:T - 1], cw(2), deltf[:, 1:T], ALU.mult, ALU.add)
                v.scalar_tensor_tensor(deltf[:, 2:T], xi_ap[:, 0:T - 2], cw(1), deltf[:, 2:T], ALU.mult, ALU.add)
                return v.scalar_tensor_tensor(deltf[:, 3:T], xi_ap[:, 0:T - 3], cw(0), deltf[:, 3:T],
                                              ALU.mult, ALU.add)
            t_cv = EV(fc, waits=list(t_ready))
            t_si = ES(lambda s, dt=dt: s.activation(xi_ap, deltf[:, :], AF.Silu,
                                                    bias=pvec[:, SL(cb_col + dt, cb_col + dt + 1)]),
                      waits=[("v", t_cv)])
            return GPD(S["u_sp"][SL(dt * 128, (dt + 1) * 128), :], xi_ap, waits=[("s", t_si)])

        def mamba(which):
            a = which == "a"

            def xln_t(dt):
                return arb(4 * dt, T)
            layer_norm(ND, xln_t, src_hbm=S["xres"])
            t_ln1 = tk("v")
            if a:
                for dt in range(ND):
                    t_cp = EV(lambda v, dt=dt: v.tensor_copy(deltf[:, :], xln_t(dt)),
                              waits=[("v", t_ln1), ("gd", tk("gd"))])
                    GPD(S["xres"][SL(dt * 128, (dt + 1) * 128), :], deltf[:, :],
                        waits=[("v", t_cp)], accum_op=ALU.add)
                layer_norm(ND, xln_t, src_tile=xln_t, gamma_beta=(vsm[:, 0:8], vsm[:, 8:16]))
            t_ln = tk("v")
            if ("tap_xln_" + which) in taps:
                for dt in range(ND):
                    t_cp = EV(lambda v, dt=dt: v.tensor_copy(deltf[:, :], xln_t(dt)), waits=[("v", t_ln)])
                    SYD(taps["tap_xln_" + which][SL(dt * 128, (dt + 1) * 128), :], deltf[:, :],
                        waits=[("v", t_cp)])

            w_hbm = I["a_in_proj"] if a else I["b_W_in"]
            cw_off = 272 if a else 336
            cb_col = 0 if a else 48

            def xi_stage(gmt):
                return arb(64 + 4 * (gmt % 2), T)

            for half in range(2):
                def evac(mt, pa, j, t_mm, half=half):
                    gmt = half * 16 + mt
                    if gmt < NDI:
                        t_e = EV(lambda v, pa=pa, j=j, gmt=gmt: v.tensor_copy(
                            xi_stage(gmt)[:, SL(j * 512, (j + 1) * 512)], pa[:, 0:512]),
                            waits=[("t", t_mm)] + ([("gd", tk("gd"))] if j == 0 else []))
                        if j == 3:
                            conv_silu(gmt, xi_stage(gmt), cw_off, cb_col, [("v", t_e)])
                        return ("v", t_e)
                    else:
                        t_e = ES(lambda s, pa=pa, j=j: s.activation(sqb[:, SL(j * 512, (j + 1) * 512)],
                                                                    pa[:, 0:512], AF.Silu),
                                 waits=[("t", t_mm)] + ([("gd", tk("gd"))] if j == 0 else []))
                        if j == 3:
                            GPD(S["sz_sp"][SL((gmt - NDI) * 128, (gmt - NDI + 1) * 128), :], sqb[:, :],
                                waits=[("s", t_e)])
                        return ("s", t_e)
                mm_proj(w_hbm[:, SL(half * 2048, (half + 1) * 2048)], ND, 16,
                        xln_t, evac, 32, rhs_wait=[("v", t_ln)])

            ncols = 96 if a else 33
            wx_hbm = I["a_x_proj"] if a else I["b_W_x"]
            wxb = arb(32, NDI * ncols)
            t_wx = GPD(wxb.rearrange("p (k c) -> p k c", k=NDI, c=ncols),
                       wx_hbm.rearrange("(k p) c -> p k c", p=128), waits=[("t", tk("t"))])
            for kt in range(NDI):
                tku = GPD(arb(64 + 4 * (kt % 2), T), S["u_sp"][SL(kt * 128, (kt + 1) * 128), :],
                          waits=[("t", tk("t")), ("v", tk("v") if kt == 0 else 0)])

                def fx(t, kt=kt):
                    for j in range(4):
                        i = t.matmul(pss[4 + j][0:ncols, 0:512], wxb[:, SL(kt * ncols, (kt + 1) * ncols)],
                                     arb(64 + 4 * (kt % 2), T)[:, SL(j * 512, (j + 1) * 512)],
                                     start=(kt == 0), stop=(kt == NDI - 1))
                    return i
                t_mx = ET(fx, waits=[("gd", tku)] + ([("gd", t_wx), ("v", tk("v")), ("s", tk("s"))]
                                                     if kt == 0 else []))
            for j in range(4):
                EV(lambda v, j=j: v.tensor_copy(xdbf[0:ncols, SL(j * 512, (j + 1) * 512)],
                                                pss[4 + j][0:ncols, 0:512]),
                   waits=[("t", t_mx)])
            t_xdb = tk("v")

            if a:
                t_dtc = EV(lambda v: v.tensor_copy(xdbb[:, :], xdbf[0:64, :]), waits=[("v", t_xdb)])
                t_br = GPD(brow[:, :], xdbf[64:80, :], waits=[("v", t_xdb)])
                t_cr = GPD(crow[:, :], xdbf[80:96, :], waits=[("v", t_xdb)])
            else:
                t_br = GPD(brow[:, :], xdbf[1:17, :], waits=[("v", t_xdb)])
                t_cr = GPD(crow[:, :], xdbf[17:33, :], waits=[("v", t_xdb)])
                for j in range(4):
                    t_bb = ET(lambda t, j=j: t.matmul(pss[5][:, 0:512], onesf[0:1, :],
                                                      xdbf[0:1, SL(j * 512, (j + 1) * 512)],
                                                      start=True, stop=True),
                              waits=[("v", t_xdb if j == 0 else tk("v"))])
                    EV(lambda v, j=j: v.tensor_copy(arf(72, T)[:, SL(j * 512, (j + 1) * 512)], pss[5][:, 0:512]),
                       waits=[("t", t_bb)])
            t_xp0 = tk("v")

            for dt in range(NDI):
                tku = GPD(arb(64 + 4 * (dt % 2), T), S["u_sp"][SL(dt * 128, (dt + 1) * 128), :],
                          waits=[("v", tk("v"))])
                if a:
                    for j in range(4):
                        t_m = ET(lambda t, dt=dt, j=j: t.matmul(pss[5][:, 0:512],
                                                                dtpw[:, SL(dt * 128, (dt + 1) * 128)],
                                                                xdbb[:, SL(j * 512, (j + 1) * 512)],
                                                                start=True, stop=True),
                                 waits=[("gd", t_gdc), ("v", t_dtc), ("s", tk("s"))])
                        ES(lambda s, j=j, dt=dt: s.activation(deltf[:, SL(j * 512, (j + 1) * 512)],
                                                              pss[5][:, 0:512], AF.Exp,
                                                              bias=pvec[:, SL(16 + dt, 16 + dt + 1)]),
                           waits=[("t", t_m)] + ([("v", tk("v"))] if j == 0 else []))
                        t_sp = ES(lambda s, j=j: s.activation(deltf[:, SL(j * 512, (j + 1) * 512)],
                                                              deltf[:, SL(j * 512, (j + 1) * 512)],
                                                              AF.Ln, bias=kcol[:, 0:1]))
                else:
                    t_dr = EV(lambda v, dt=dt: v.tensor_scalar(deltf[:, :], arf(72, T)[:, :],
                                                               pvec[:, SL(80 + dt, 80 + dt + 1)],
                                                               pvec[:, SL(64 + dt, 64 + dt + 1)],
                                                               ALU.mult, ALU.add),
                              waits=[("v", tk("v")), ("s", tk("s")), ("gd", tk("gd"))])
                    ES(lambda s: s.activation(deltf[:, :], deltf[:, :], AF.Exp), waits=[("v", t_dr)])
                    t_sp = ES(lambda s: s.activation(deltf[:, :], deltf[:, :], AF.Ln, bias=kcol[:, 0:1]))
                GPD(S["dt_sp"][SL(dt * 128, (dt + 1) * 128), :], deltf[:, :], waits=[("s", t_sp)])
                t_w = EV(lambda v, dt=dt: v.tensor_mul(sqb[:, :], deltf[:, :], arb(64 + 4 * (dt % 2), T)),
                         waits=[("s", t_sp), ("gd", tku)])
                GPD(S["w_sp"][SL(dt * 128, (dt + 1) * 128), :], sqb[:, :], waits=[("v", t_w)])

            acol_off = 0 if a else 256
            dt_ld = [0, 0]
            wt_ld = [0, 0]
            dt_cons = [tk("s"), tk("s")]
            wt_cons = [tk("v"), tk("v")]
            da_cons = [tk("v"), tk("v")]
            par_cons = [0, 0]

            def issue_loads(dtile, slt, extra=()):
                dt_ld[slt] = SYD(arf(64 + 8 * slt, T), S["dt_sp"][SL(dtile * 128, (dtile + 1) * 128), :],
                                 waits=[("s", dt_cons[slt])] + list(extra))
                wt_ld[slt] = SYD(arb(80 + 4 * slt, T), S["w_sp"][SL(dtile * 128, (dtile + 1) * 128), :],
                                 waits=[("v", wt_cons[slt])])

            issue_loads(0, 0, extra=[("v", tk("v")), ("gd", tk("gd")), ("s", tk("s"))])
            for q in range(4):
                for ni in range(4):
                    n = q * 4 + ni
                    for j in range(4):
                        t_m = ET(lambda t, n=n, j=j: t.matmul(pss[6][:, 0:512],
                                                              selb[:, SL(n * 128, (n + 1) * 128)],
                                                              brow[:, SL(j * 512, (j + 1) * 512)],
                                                              start=True, stop=True),
                                 waits=[("gd", t_br), ("v", tk("v"))])
                        EV(lambda v, ni=ni, j=j: v.tensor_copy(arb(4 * ni, T)[:, SL(j * 512, (j + 1) * 512)],
                                                               pss[6][:, 0:512]), waits=[("t", t_m)])
                        t_m2 = ET(lambda t, n=n, j=j: t.matmul(pss[7][:, 0:512],
                                                               selb[:, SL(n * 128, (n + 1) * 128)],
                                                               crow[:, SL(j * 512, (j + 1) * 512)],
                                                               start=True, stop=True),
                                  waits=[("gd", t_cr), ("v", tk("v"))])
                        EV(lambda v, ni=ni, j=j: v.tensor_copy(arb(16 + 4 * ni, T)[:, SL(j * 512, (j + 1) * 512)],
                                                               pss[7][:, 0:512]), waits=[("t", t_m2)])
                t_bcq = tk("v")

                for dt in range(NDI):
                    slt = dt % 2
                    for ni in range(4):
                        n = q * 4 + ni
                        dslt = ni % 2
                        ac_i = acol_off + (dt * 16 + n if a else n)
                        ac = Acol[:, SL(ac_i, ac_i + 1)]
                        t_da = ES(lambda s, slt=slt, dslt=dslt, ac=ac: s.activation(
                            arf(104 + 8 * dslt, T), arf(64 + 8 * slt, T), AF.Exp, scale=ac),
                            waits=[("ld", dt_ld[slt]), ("v", da_cons[dslt])])
                        dt_cons[slt] = t_da
                        t_wb = EV(lambda v, slt=slt, ni=ni: v.tensor_mul(arb(48 + 4 * ni, T),
                                                                         arb(80 + 4 * slt, T),
                                                                         arb(4 * ni, T)),
                                  waits=[("ld", wt_ld[slt]), ("v", t_bcq)])
                        wt_cons[slt] = t_wb
                        t_sc = EV(lambda v, ni=ni, dslt=dslt: v.tensor_tensor_scan(
                            arb(32 + 4 * ni, T), arf(104 + 8 * dslt, T), arb(48 + 4 * ni, T), 0.0,
                            ALU.mult, ALU.add), waits=[("s", t_da)])
                        da_cons[dslt] = t_sc
                    if dt < NDI - 1:
                        issue_loads(dt + 1, (dt + 1) % 2)
                    elif q < 3:
                        issue_loads(0, 0)
                    EV(lambda v: v.tensor_mul(arb(48, 4 * T), arb(32, 4 * T), arb(16, 4 * T)))
                    EV(lambda v: v.tensor_add(arb(48, 2 * T), arb(48, 2 * T), arb(56, 2 * T)))
                    psl = dt % 2
                    t_a2 = EV(lambda v, psl=psl: v.tensor_tensor(arf(88 + 8 * psl, T), arb(48, T),
                                                                 arb(52, T), ALU.add),
                              waits=[("gd", par_cons[psl])] if par_cons[psl] else [])
                    par_cons[psl] = GPD(S["y_sp"][SL(dt * 128, (dt + 1) * 128), :], arf(88 + 8 * psl, T),
                                        waits=[("v", t_a2)], accum_op=(ALU.add if q > 0 else ALU.bypass))
            t_scan = tk("gd")

            def Dcol(dt):
                return pvec[:, SL(32 + dt, 33 + dt)] if a else vsm[:, SL(32 + dt, 33 + dt)]
            for dt in range(NDI):
                t_y = SYD(arf(104, T), S["y_sp"][SL(dt * 128, (dt + 1) * 128), :],
                          waits=[("gd", t_scan), ("v", tk("v"))])
                GPD(xbf[:, :], S["u_sp"][SL(dt * 128, (dt + 1) * 128), :], waits=[("v", tk("v"))])
                t_z = GPD(sqb[:, :], S["sz_sp"][SL(dt * 128, (dt + 1) * 128), :])
                t_yd = EV(lambda v, dt=dt: v.scalar_tensor_tensor(deltf[:, :], xbf[:, :], Dcol(dt),
                                                                  arf(104, T), ALU.mult, ALU.add),
                          waits=[("ld", t_y), ("gd", t_z)])
                EV(lambda v, dt=dt: v.tensor_mul(arb(4 * dt, T), deltf[:, :], sqb[:, :]))
            t_rhs = tk("v")

            wout = I["a_out_proj"] if a else I["b_W_out"]

            def evac_out(mt, pa, j, t_mm):
                if a:
                    t_e = ES(lambda s, mt=mt, pa=pa, j=j: s.activation(
                        arb(96 + 4 * mt, T)[:, SL(j * 512, (j + 1) * 512)], pa[:, 0:512], AF.Gelu),
                       waits=[("t", t_mm)])
                    return ("s", t_e)
                else:
                    t_e = EV(lambda v, pa=pa, j=j: v.tensor_copy(deltf[:, SL(j * 512, (j + 1) * 512)],
                                                                 pa[:, 0:512]),
                             waits=[("t", t_mm)] + ([("gd", tk("gd"))] if j == 0 else []))
                    if j == 3:
                        GPD(S["xres"][SL(mt * 128, (mt + 1) * 128), :], deltf[:, :],
                            waits=[("v", t_e)], accum_op=ALU.add)
                    return ("v", t_e)
            mm_proj(wout, NDI, ND, lambda kt: arb(4 * kt, T), evac_out, 64, rhs_wait=[("v", t_rhs)])

            if a:
                t_gelu = tk("s")

                def evac_glu(mt, pa, j, t_mm):
                    if mt < 8:
                        t_e = EV(lambda v, mt=mt, pa=pa, j=j: v.tensor_scalar_add(
                            arb(4 * mt, T)[:, SL(j * 512, (j + 1) * 512)], pa[:, 0:512],
                            vsm[:, SL(16 + mt, 17 + mt)]), waits=[("t", t_mm)])
                        return ("v", t_e)
                    else:
                        t_e = ES(lambda s, mt=mt, pa=pa, j=j: s.activation(
                            arb(32 + 4 * (mt - 8), T)[:, SL(j * 512, (j + 1) * 512)], pa[:, 0:512],
                            AF.Sigmoid, bias=vsm[:, SL(16 + mt, 17 + mt)]), waits=[("t", t_mm)])
                        if j == 3:
                            mt8 = mt - 8
                            t_h = EV(lambda v, mt8=mt8: v.tensor_tensor(deltf[:, :], arb(4 * mt8, T),
                                                                        arb(32 + 4 * mt8, T), ALU.mult),
                                     waits=[("s", t_e), ("gd", tk("gd"))])
                            GPD(S["xres"][SL(mt8 * 128, (mt8 + 1) * 128), :], deltf[:, :],
                                waits=[("v", t_h)], accum_op=ALU.add)
                        return ("s", t_e)
                mm_proj(I["a_glu_w"], ND, 16, lambda kt: arb(96 + 4 * kt, T), evac_glu, 64,
                        rhs_wait=[("s", t_gelu)])

        def ffn():
            def xln_t(dt):
                return arb(4 * dt, T)
            layer_norm(ND, xln_t, src_hbm=S["xres"])
            t_ln = tk("v")

            for tt in range(NTT):
                def fsc(t, tt=tt):
                    for kt in range(ND):
                        i = t.matmul(pss[5][:, 0:16], xln_t(kt)[:, SL(tt * 128, (tt + 1) * 128)],
                                     wgb[:, SL(kt * 16, (kt + 1) * 16)],
                                     start=(kt == 0), stop=(kt == ND - 1))
                    return i
                t_m = ET(fsc, waits=[("v", t_ln if tt == 0 else tk("v")), ("gd", t_gdc)])
                EV(lambda v, tt=tt: v.tensor_tensor(sco[:, SL(tt * 16, (tt + 1) * 16)], pss[5][:, 0:16],
                                                    bgb[:, 0:16], ALU.add), waits=[("t", t_m)])
            t_sco = tk("v")

            for tt in range(NTT):
                sc = sco[:, SL(tt * 16, (tt + 1) * 16)]

                def fr(v, tt=tt, sc=sc):
                    v.max(rtw[:, 0:8], sc)
                    v.max_index(rtw[:, 8:16].bitcast(U32), rtw[:, 0:8], sc)
                    v.tensor_copy(idxg[:, SL(tt, tt + 1)], rtw[:, 8:16].bitcast(U32)[:, 0:1])
                    return v.tensor_scalar_mul(rtw[:, 16:17], rtw[:, 0:1], -1.0)
                t_mx = EV(fr, waits=[("v", t_sco)])
                t_ex = ES(lambda s, sc=sc: s.activation(rtw[:, 32:48], sc, AF.Exp, bias=rtw[:, 16:17],
                                                        accum_out=rtw[:, 17:18]), waits=[("v", t_mx)])

                def fg(v, tt=tt):
                    v.reciprocal(rtw[:, 18:19], rtw[:, 17:18])
                    v.tensor_copy(gateb[:, SL(tt, tt + 1)], rtw[:, 18:19])
                    v.tensor_single_scalar(onehf[:, :], iotab[:, :], idxg[:, SL(tt, tt + 1)], ALU.is_equal)
                    return v.tensor_copy(oneh[:, :], onehf[:, :])
                t_oh = EV(fg, waits=[("s", t_ex)])
                ET(lambda t: t.matmul(pss[6][:, 0:16], slob[:, :], oneh[:, :], start=True, stop=True),
                   waits=[("v", t_oh)])
                t_cm = ET(lambda t: t.matmul(pss[7][0:16, 0:1], oneh[:, :], onesc[:, :],
                                             start=True, stop=True))

                def fp(v, tt=tt):
                    v.tensor_mul(rtw[:, 32:48], onehf[:, :], pss[6][:, 0:16])
                    v.reduce_sum(posg[:, SL(tt, tt + 1)], rtw[:, 32:48], AX.X)
                    return v.tensor_copy(cnts[0:16, SL(tt, tt + 1)], pss[7][0:16, 0:1])
                EV(fp, waits=[("t", t_cm)])
            t_rt = tk("v")

            def fcs(v):
                v.tensor_tensor_scan(cnts[0:16, 16:32], ones4[:, :], cnts[0:16, 0:16], 0.0, ALU.mult, ALU.add)
                v.memset(cnts[0:16, 32:33], 0.0)
                return v.tensor_copy(cnts[0:16, 33:48], cnts[0:16, 16:31])
            t_cs = EV(fcs, waits=[("v", t_rt)])
            t_ci = GPD(S["cc_c_in"][0:1, :], cnts[0:16, 31:32], waits=[("v", t_cs)])
            t_cc1 = EG(lambda gp: gp.collective_compute("AllGather", ALU.bypass,
                                                        replica_groups=[[0, 1, 2, 3, 4, 5, 6, 7]],
                                                        ins=[S["cc_c_in"][:]], outs=[S["C_all"][:]]),
                       waits=[("gd", t_ci)])
            t_c8 = GPD(cnt8[:, :], S["C_all"][:, :], waits=[("g", t_cc1)])
            t_mk = EV(lambda v: v.tensor_tensor(mask8[:, :], iota8[:, :], cidb[0:8, 0:1], ALU.is_lt),
                      waits=[("gd", t_c8), ("v", t_cid)])
            t_oc = ET(lambda t: t.matmul(pss[5][0:1, 0:16], mask8[:, :], cnt8[:, :], start=True, stop=True),
                      waits=[("v", t_mk)])
            t_oce = EV(lambda v: v.tensor_copy(arow[0:1, 0:16], pss[5][0:1, 0:16]), waits=[("t", t_oc)])
            t_occ = GPD(cnts[0:16, 48:49], arow[0:1, 0:16], waits=[("v", t_oce)])
            t_toff = EV(lambda v: v.tensor_scalar_add(cnts[0:16, 32:48], cnts[0:16, 32:48],
                                                      cnts[0:16, 48:49]), waits=[("gd", t_occ)])
            t_tfr = GPD(xdbf[0:1, 0:256], cnts[0:16, 32:48], waits=[("v", t_toff)])
            t_tb = 0
            for j in range(1):
                t_bb = ET(lambda t: t.matmul(pss[5][:, 0:256], onesf[0:1, :], xdbf[0:1, 0:256],
                                             start=True, stop=True), waits=[("gd", t_tfr)])
                t_tb = EV(lambda v: v.tensor_copy(toffb[:, 0:256], pss[5][:, 0:256]), waits=[("t", t_bb)])

            for tt in range(NTT):
                def fq(v, tt=tt):
                    v.tensor_single_scalar(onehf[:, :], iotab[:, :], idxg[:, SL(tt, tt + 1)], ALU.is_equal)
                    v.tensor_mul(rtw[:, 32:48], onehf[:, :],
                                 toffb.rearrange("p (e tt) -> p e tt", tt=16)[:, :, SL(tt, tt + 1)])
                    v.reduce_sum(rtw[:, 19:20], rtw[:, 32:48], AX.X)
                    v.tensor_add(posg[:, SL(tt, tt + 1)], posg[:, SL(tt, tt + 1)], rtw[:, 19:20])
                    v.tensor_single_scalar(rtw[:, 20:21], posg[:, SL(tt, tt + 1)], 1024.0, ALU.is_lt)
                    v.tensor_mul(gateb[:, SL(tt, tt + 1)], gateb[:, SL(tt, tt + 1)], rtw[:, 20:21])
                    v.tensor_scalar_mul(rtw[:, 21:22], idxg[:, SL(tt, tt + 1)], 1024.0)
                    v.tensor_add(rtw[:, 21:22], rtw[:, 21:22], posg[:, SL(tt, tt + 1)])
                    v.tensor_mul(rtw[:, 21:22], rtw[:, 21:22], rtw[:, 20:21])
                    v.tensor_copy(slotc[:, SL(tt, tt + 1)], rtw[:, 21:22])
                    v.tensor_copy(meta2[:, 0:1], idxg[:, SL(tt, tt + 1)])
                    return v.tensor_copy(meta2[:, 1:2], posg[:, SL(tt, tt + 1)])
                t_q = EV(fq, waits=[("v", t_tb), ("gd", tk("gd"))])
                GPD(S["cc_m_in"][SL(tt * 128, (tt + 1) * 128), :], meta2[:, 0:2], waits=[("v", t_q)])
            t_meta = tk("gd")

            for tt in range(NTT):
                for dc in range(ND):
                    t_tp = ET(lambda t, tt=tt, dc=dc: t.transpose(
                        pss[5 + (dc % 2)].bitcast(BF16)[:, 0:128],
                        xln_t(dc)[:, SL(tt * 128, (tt + 1) * 128)], eyeb[:, :]),
                        waits=[("v", tk("v")), ("gd", tk("gd")), ("ld", t_ldc)])
                    EV(lambda v, tt=tt, dc=dc: v.tensor_copy(
                        arb(32 + 2 * (tt % 2), D)[:, SL(dc * 128, (dc + 1) * 128)],
                        pss[5 + (dc % 2)].bitcast(BF16)[:, 0:128]), waits=[("t", t_tp)])
                GPD(S["cc_x_in"][SL(tt * 128, (tt + 1) * 128), :], arb(32 + 2 * (tt % 2), D),
                    waits=[("v", tk("v"))])
            t_xr = tk("gd")
            t_ccx = EG(lambda gp: gp.collective_compute("AllGather", ALU.bypass,
                                                        replica_groups=[[0, 1, 2, 3, 4, 5, 6, 7]],
                                                        ins=[S["cc_x_in"][:]], outs=[S["X_all"][:]]),
                       waits=[("gd", t_xr)])
            t_ccm = EG(lambda gp: gp.collective_compute("AllGather", ALU.bypass,
                                                        replica_groups=[[0, 1, 2, 3, 4, 5, 6, 7]],
                                                        ins=[S["cc_m_in"][:]], outs=[S["M_all"][:]]),
                       waits=[("gd", t_meta)])

            t_z = EV(lambda v: v.memset(stage[:, 0:D], 0.0), waits=[("gd", tk("gd"))])
            t_bz = GPD(S["B_own"].rearrange("(r p) d -> p r d", p=128),
                       stage[:, 0:D].unsqueeze(1).broadcast_to([128, 17, D]), waits=[("v", t_z)])

            t_mm2 = GPD(meta2[:, 0:256], S["M_all"].rearrange("(pp gp) -> pp gp", pp=128)
                        if False else S["M_all"].rearrange("(pp g) two -> pp (g two)", pp=128),
                        waits=[("g", t_ccm)])
            eyef = arf(46, 128)
            t_ef = EV(lambda v: v.tensor_copy(eyef[:, :], eyeb[:, :]), waits=[("gd", t_mm2)])

            def fsi(v):
                mv = meta2[:, 0:256].rearrange("p (g two) -> p g two", two=2)
                mi = mv[:, :, SL(0, 1)]
                mp = mv[:, :, SL(1, 2)]
                m0 = meta2[:, 256:384]
                m1 = meta2[:, 384:512]
                kp = meta2[:, 512:640]
                v.tensor_single_scalar(m0, mi, cidb[:, 1:2], ALU.is_equal)
                v.tensor_single_scalar(m1, mi, cidb[:, 2:3], ALU.is_equal)
                v.tensor_single_scalar(kp, mp, 1024.0, ALU.is_lt)
                v.tensor_add(m0, m0, m1)
                v.tensor_mul(m0, m0, kp)
                v.tensor_scalar_mul(m1, m1, 1024.0)
                v.tensor_tensor(m1, m1, mp, ALU.add)
                v.tensor_scalar_sub(m1, m1, cidb[:, 3:4])
                v.tensor_mul(m1, m1, m0)
                v.tensor_scalar_add(m1, m1, cidb[:, 3:4])
            t_sif = EV(fsi, waits=[("v", t_ef), ("v", t_cid)])
            t_tps = ET(lambda t: t.transpose(pss[5][:, 0:128], meta2[:, 384:512], eyef[:, :]),
                       waits=[("v", t_sif)])
            t_si = EV(lambda v: v.tensor_copy(scat_i[:, :], pss[5][:, 0:128]), waits=[("t", t_tps)])

            for gq in range(128):
                slq = gq % 2
                t_ld = SYD(stage[:, SL(slq * D, (slq + 1) * D)], S["X_all"][SL(gq * 128, (gq + 1) * 128), :],
                           waits=([("gd", tk("gd"))] if gq >= 2 else [("gd", t_bz), ("g", t_ccx), ("v", t_si)]))
                GPI(S["B_own"][:, :], IndirectOffsetOnAxis(scat_i[:, SL(gq, gq + 1)], 0),
                    stage[:, SL(slq * D, (slq + 1) * D)], None, waits=[("ld", t_ld)])
            t_scat = tk("gd")

            t_b1 = SYD(b1col.rearrange("p (e ht) -> p e ht", e=2), I["f_b1c"].rearrange("e (ht p) -> p e ht", p=128))
            t_b2 = SYD(b2col.rearrange("p (e mt) -> p e mt", e=2), I["f_b2c"].rearrange("e (mt p) -> p e mt", p=128))
            for ei in range(2):
                for dc in range(ND):
                    SYD(arb(32 + 2 * dc, CAP), S["B_own"][SL(ei * CAP, (ei + 1) * CAP),
                                                          SL(dc * 128, (dc + 1) * 128)],
                        transpose=True,
                        waits=[("gd", t_scat), ("t", tk("t")), ("v", tk("v"))])
                t_xe = tk("ld")

                def xet(dc):
                    return arb(32 + 2 * dc, CAP)
                t_y0 = EV(lambda v: v.memset(arf(0, ND * CAP), 0.0), waits=[("gd", tk("gd")), ("s", tk("s")),
                                                                            ("t", tk("t"))])
                for hh in range(2):
                    t_w1 = GPD(arb(64, ND * 16 * 128).rearrange("p (k m c) -> p k m c", k=ND, m=16, c=128),
                               I["f_W1c"][ei, :, SL(hh * 2048, (hh + 1) * 2048)]
                               .rearrange("(k p) (m c) -> p k m c", p=128, c=128),
                               waits=[("t", tk("t")), ("gd", tk("gd"))])
                    for mt in range(16):
                        for j in range(2):
                            pa = pss[(mt * 2 + j) % 4]
                            guard = [("s", tk("s"))]

                            def fm1(t, mt=mt, j=j, pa=pa):
                                for kt in range(ND):
                                    i = t.matmul(pa[:, 0:512],
                                                 arb(64, ND * 16 * 128)[:, SL((kt * 16 + mt) * 128,
                                                                              (kt * 16 + mt + 1) * 128)],
                                                 xet(kt)[:, SL(j * 512, (j + 1) * 512)],
                                                 start=(kt == 0), stop=(kt == ND - 1))
                                return i
                            t_mm = ET(fm1, waits=[("gd", t_w1), ("ld", t_xe)] + guard)
                            ES(lambda s, mt=mt, j=j, pa=pa, hh=hh, ei=ei: s.activation(
                                arb(128 + 2 * mt, CAP)[:, SL(j * 512, (j + 1) * 512)], pa[:, 0:512],
                                AF.Relu, bias=b1col[:, SL(ei * 32 + hh * 16 + mt, ei * 32 + hh * 16 + mt + 1)]),
                               waits=[("t", t_mm), ("ld", t_b1)])
                    t_relu = tk("s")
                    t_w2 = GPD(arb(96, 16 * ND * 128).rearrange("p (k m c) -> p k m c", k=16, m=ND, c=128),
                               I["f_W2c"][ei, SL(hh * 2048, (hh + 1) * 2048), :]
                               .rearrange("(k p) (m c) -> p k m c", p=128, c=128),
                               waits=[("t", tk("t"))])
                    for mt in range(ND):
                        for j in range(2):
                            pa = pss[4 + (mt * 2 + j) % 4]
                            guard = [("v", tk("v"))]

                            def fm2(t, mt=mt, j=j, pa=pa):
                                for kt in range(16):
                                    i = t.matmul(pa[:, 0:512],
                                                 arb(96, 16 * ND * 128)[:, SL((kt * ND + mt) * 128,
                                                                              (kt * ND + mt + 1) * 128)],
                                                 arb(128 + 2 * kt, CAP)[:, SL(j * 512, (j + 1) * 512)],
                                                 start=(kt == 0), stop=(kt == 15))
                                return i
                            t_mm = ET(fm2, waits=[("gd", t_w2), ("s", t_relu)] + guard)
                            EV(lambda v, mt=mt, j=j, pa=pa: v.tensor_tensor(
                                arf(4 * mt, CAP)[:, SL(j * 512, (j + 1) * 512)],
                                arf(4 * mt, CAP)[:, SL(j * 512, (j + 1) * 512)], pa[:, 0:512], ALU.add),
                               waits=[("t", t_mm), ("v", t_y0)])
                t_yv = tk("v")
                for mt in range(ND):
                    EV(lambda v, mt=mt, ei=ei: v.tensor_scalar_add(arb(48 + 2 * mt, CAP), arf(4 * mt, CAP),
                                                                   b2col[:, SL(ei * 8 + mt, ei * 8 + mt + 1)]),
                       waits=[("v", t_yv), ("ld", t_b2)] + ([("ld", tk("ld"))] if mt == 0 else []))
                t_yb = tk("v")
                for st in range(8):
                    for mt in range(ND):
                        t_tp = ET(lambda t, st=st, mt=mt: t.transpose(
                            pss[5 + (mt % 2)].bitcast(BF16)[:, 0:128],
                            arb(48 + 2 * mt, CAP)[:, SL(st * 128, (st + 1) * 128)], eyeb[:, :]),
                            waits=[("v", tk("v")), ("gd", tk("gd"))])
                        EV(lambda v, st=st, mt=mt: v.tensor_copy(
                            arb(64 + 2 * (st % 2), D)[:, SL(mt * 128, (mt + 1) * 128)],
                            pss[5 + (mt % 2)].bitcast(BF16)[:, 0:128]), waits=[("t", t_tp)])
                    GPD(S["cc_y_in"][SL(ei * CAP + st * 128, ei * CAP + (st + 1) * 128), :],
                        arb(64 + 2 * (st % 2), D), waits=[("v", tk("v"))])
            t_yrows = tk("gd")
            t_ccy = EG(lambda gp: gp.collective_compute("AllGather", ALU.bypass,
                                                        replica_groups=[[0, 1, 2, 3, 4, 5, 6, 7]],
                                                        ins=[S["cc_y_in"][:]], outs=[S["Y_all"][:]]),
                       waits=[("gd", t_yrows)])

            for tt in range(NTT):
                slq = tt % 2
                t_g = GPI(arb(32 + 2 * slq, D), None, S["Y_all"][:, :],
                          IndirectOffsetOnAxis(slotc[:, SL(tt, tt + 1)], 0),
                          waits=[("g", t_ccy), ("v", tk("v")), ("t", tk("t"))])
                t_gt = EV(lambda v, tt=tt, slq=slq: v.tensor_scalar_mul(arb(32 + 2 * slq, D),
                                                                        arb(32 + 2 * slq, D),
                                                                        gateb[:, SL(tt, tt + 1)]),
                          waits=[("gd", t_g)])
                for dc in range(ND):
                    t_tp = ET(lambda t, tt=tt, dc=dc, slq=slq: t.transpose(
                        pss[5 + (dc % 2)].bitcast(BF16)[:, 0:128],
                        arb(32 + 2 * slq, D)[:, SL(dc * 128, (dc + 1) * 128)], eyeb[:, :]),
                        waits=[("v", t_gt if dc == 0 else tk("v"))])
                    EV(lambda v, tt=tt, dc=dc: v.tensor_copy(
                        arb(4 * dc, T)[:, SL(tt * 128, (tt + 1) * 128)],
                        pss[5 + (dc % 2)].bitcast(BF16)[:, 0:128]), waits=[("t", t_tp)])
            t_asm = tk("v")
            for dc in range(ND):
                t_cp = EV(lambda v, dc=dc: v.tensor_copy(deltf[:, :], arb(4 * dc, T)),
                          waits=[("v", t_asm), ("gd", tk("gd"))])
                GPD(S["xres"][SL(dc * 128, (dc + 1) * 128), :], deltf[:, :],
                    waits=[("v", t_cp)], accum_op=ALU.add)

        for ph in PHASES:
            if ph == "a":
                mamba("a")
            elif ph == "b":
                mamba("b")
            elif ph == "f":
                ffn()

        WAIT("sync", ALLW())
        for dt in range(ND):
            SYD(out_h[SL(dt * 128, (dt + 1) * 128), :], S["xres"][SL(dt * 128, (dt + 1) * 128), :])
        WAIT("sync", [("ld", tk("ld"))])
        WAIT("gpsimd", [("ld", tk("ld"))])

    return nc


def host_constants():
    import ml_dtypes
    eye = np.eye(128, dtype=np.float32).astype(ml_dtypes.bfloat16)
    slo = np.triu(np.ones((128, 128), np.float32), 1).astype(ml_dtypes.bfloat16)
    sel = np.zeros((NEXP, NEXP * 128), np.float32)
    for n in range(NEXP):
        sel[n, n * 128:(n + 1) * 128] = 1.0
    return {
        "c_eye": eye,
        "c_slo": slo,
        "c_sel": sel.astype(ml_dtypes.bfloat16),
        "c_iota16b": np.broadcast_to(np.arange(NEXP, dtype=np.float32), (128, NEXP)).copy(),
        "c_iota8c": np.arange(8, dtype=np.float32).reshape(8, 1),
        "c_iota128c": np.arange(128, dtype=np.float32).reshape(128, 1),
    }


def make_in_maps(inp):
    x = inp["L0_attn_a_x"]
    consts = host_constants()
    in_maps = []
    for c in range(8):
        m = {
            "xT": np.ascontiguousarray(x[c].T),
            "a_in_proj": inp["L0_attn_a_in_proj_w"],
            "a_conv_w": inp["L0_attn_a_conv_w"],
            "a_conv_b": inp["L0_attn_a_conv_b"],
            "a_x_proj": inp["L0_attn_a_x_proj_w"],
            "a_dt_proj": inp["L0_attn_a_dt_proj_w"],
            "a_dt_b": inp["L0_attn_a_dt_proj_b"],
            "a_A_log": inp["L0_attn_a_A_log"],
            "a_D": inp["L0_attn_a_D"],
            "a_out_proj": inp["L0_attn_a_out_proj_w"],
            "a_glu_w": inp["L0_attn_a_glu_w"],
            "a_glu_b": inp["L0_attn_a_glu_b"],
            "a_ln_g": inp["L0_attn_a_ln_gamma"],
            "a_ln_b": inp["L0_attn_a_ln_beta"],
            "b_W_in": inp["L0_attn_b_W_in"],
            "b_conv_w": np.ascontiguousarray(inp["L0_attn_b_conv_w"].reshape(DI, 4)),
            "b_conv_b": inp["L0_attn_b_conv_b"],
            "b_W_x": inp["L0_attn_b_W_x"],
            "b_W_dt": np.ascontiguousarray(inp["L0_attn_b_W_dt"].reshape(DI)),
            "b_b_dt": inp["L0_attn_b_b_dt"],
            "b_A_log": inp["L0_attn_b_A_log"].reshape(1, NST),
            "b_D": inp["L0_attn_b_D"],
            "b_W_out": inp["L0_attn_b_W_out"],
            "f_Wg": inp["L0_ffn_Wg"],
            "f_bg": inp["L0_ffn_bg"].reshape(1, NEXP),
            "f_W1c": np.ascontiguousarray(inp["L0_ffn_W1"][2 * c:2 * c + 2]),
            "f_b1c": np.ascontiguousarray(inp["L0_ffn_b1"][2 * c:2 * c + 2]),
            "f_W2c": np.ascontiguousarray(inp["L0_ffn_W2"][2 * c:2 * c + 2]),
            "f_b2c": np.ascontiguousarray(inp["L0_ffn_b2"][2 * c:2 * c + 2]),
            "c_cid": np.array([[float(c)]], dtype=np.float32),
        }
        m.update(consts)
        in_maps.append(m)
    return in_maps


def kernel(**inp):
    nc = build_program()
    in_maps = make_in_maps(inp)
    res = run_bass_kernel_spmd(nc, in_maps, core_ids=list(range(8)))
    return np.stack([np.ascontiguousarray(res.results[c]["outT"].T) for c in range(8)], axis=0)


# version 03
# speedup vs baseline: 1.3959x; 1.1562x over previous
import numpy as np
import concourse.bass as bass
import concourse.mybir as mybir
from concourse.bass import IndirectOffsetOnAxis
from concourse.bass_utils import run_bass_kernel_spmd

F32 = mybir.dt.float32
BF16 = mybir.dt.bfloat16
I32 = mybir.dt.int32
U32 = mybir.dt.uint32
AF = mybir.ActivationFunctionType
ALU = mybir.AluOpType
AX = mybir.AxisListType

D = 1024
T = 2048
DI = 2048
NST = 16
NEXP = 16
CAP = 1024
HF = 4096
ND = D // 128
NDI = DI // 128
NTT = T // 128
EPS = 1e-5
SL = slice

DEBUG_TAPS = {}
PHASES = ("a", "f", "b", "f")


def build_program():
    nc = bass.Bass("TRN2", target_bir_lowering=False, debug=False, num_devices=8)

    def din(name, shape, dt=F32):
        return nc.dram_tensor(name, shape, dt, kind="ExternalInput").ap()

    I = {}
    I["xT"] = din("xT", [D, T])
    I["a_in_proj"] = din("a_in_proj", [D, 2 * DI])
    I["a_conv_w"] = din("a_conv_w", [DI, 4])
    I["a_conv_b"] = din("a_conv_b", [DI])
    I["a_x_proj"] = din("a_x_proj", [DI, 96])
    I["a_dt_proj"] = din("a_dt_proj", [64, DI])
    I["a_dt_b"] = din("a_dt_b", [DI])
    I["a_A_log"] = din("a_A_log", [DI, NST])
    I["a_D"] = din("a_D", [DI])
    I["a_out_proj"] = din("a_out_proj", [DI, D])
    I["a_glu_w"] = din("a_glu_w", [D, 2 * D])
    I["a_glu_b"] = din("a_glu_b", [2 * D])
    I["a_ln_g"] = din("a_ln_g", [D])
    I["a_ln_b"] = din("a_ln_b", [D])
    I["b_W_in"] = din("b_W_in", [D, 2 * DI])
    I["b_conv_w"] = din("b_conv_w", [DI, 4])
    I["b_conv_b"] = din("b_conv_b", [DI])
    I["b_W_x"] = din("b_W_x", [DI, 33])
    I["b_W_dt"] = din("b_W_dt", [DI])
    I["b_b_dt"] = din("b_b_dt", [DI])
    I["b_A_log"] = din("b_A_log", [1, NST])
    I["b_D"] = din("b_D", [DI])
    I["b_W_out"] = din("b_W_out", [DI, D])
    I["f_Wg"] = din("f_Wg", [D, NEXP])
    I["f_bg"] = din("f_bg", [1, NEXP])
    I["f_W1c"] = din("f_W1c", [2, D, HF])
    I["f_b1c"] = din("f_b1c", [2, HF])
    I["f_W2c"] = din("f_W2c", [2, HF, D])
    I["f_b2c"] = din("f_b2c", [2, D])
    I["c_eye"] = din("c_eye", [128, 128], BF16)
    I["c_slo"] = din("c_slo", [128, 128], BF16)
    I["c_sel"] = din("c_sel", [NEXP, NEXP * 128], BF16)
    I["c_iota16b"] = din("c_iota16b", [128, NEXP])
    I["c_iota8c"] = din("c_iota8c", [8, 1])
    I["c_iota128c"] = din("c_iota128c", [128, 1])
    I["c_cid"] = din("c_cid", [1, 1])

    out_h = nc.dram_tensor("outT", [D, T], F32, kind="ExternalOutput").ap()

    S = {}
    S["xres"] = nc.dram_tensor("xres", [D, T], F32).ap()
    S["u_sp"] = nc.dram_tensor("u_sp", [DI, T], BF16).ap()
    S["sz_sp"] = nc.dram_tensor("sz_sp", [DI, T], BF16).ap()
    S["dt_sp"] = nc.dram_tensor("dt_sp", [DI, T], F32).ap()
    S["w_sp"] = nc.dram_tensor("w_sp", [DI, T], BF16).ap()
    S["y_sp"] = nc.dram_tensor("y_sp", [DI, T], F32).ap()
    S["cc_x_in"] = nc.dram_tensor("cc_x_in", [T, D], BF16).ap()
    S["X_all"] = nc.dram_tensor("X_all", [8 * T, D], BF16, addr_space="Shared").ap()
    S["cc_m_in"] = nc.dram_tensor("cc_m_in", [T, 2], F32).ap()
    S["M_all"] = nc.dram_tensor("M_all", [8 * T, 2], F32, addr_space="Shared").ap()
    S["cc_c_in"] = nc.dram_tensor("cc_c_in", [1, NEXP], F32).ap()
    S["C_all"] = nc.dram_tensor("C_all", [8, NEXP], F32, addr_space="Shared").ap()
    S["B_own"] = nc.dram_tensor("B_own", [2 * CAP + 128, D], BF16).ap()
    S["cc_y_in"] = nc.dram_tensor("cc_y_in", [2 * CAP, D], BF16).ap()
    S["Y_all"] = nc.dram_tensor("Y_all", [NEXP * CAP, D], BF16, addr_space="Shared").ap()

    taps = {}
    for name, (shape, dt) in DEBUG_TAPS.items():
        taps[name] = nc.dram_tensor(name, shape, dt, kind="ExternalOutput").ap()

    AR = nc.alloc_sbuf_tensor("AR", [128, 80 * 1024], BF16)

    def arb(off_kb, nbf16):
        return AR[:, off_kb * 512: off_kb * 512 + nbf16]

    def arf(off_kb, nf32):
        return AR[:, off_kb * 512: off_kb * 512 + 2 * nf32].bitcast(F32)

    arow = nc.alloc_sbuf_tensor("arow", [1, 16], F32)
    onesall = nc.alloc_sbuf_tensor("onesall", [128, 128], BF16)
    xbf = nc.alloc_sbuf_tensor("xbf", [128, T], BF16)
    sqb = nc.alloc_sbuf_tensor("sqb", [128, T], BF16)
    rbc = None
    mbc = None
    deltf = nc.alloc_sbuf_tensor("deltf", [128, T], F32)
    vsm = nc.alloc_sbuf_tensor("vsm", [128, 48], F32)
    pvec = nc.alloc_sbuf_tensor("pvec", [128, 96], F32)
    Acol = nc.alloc_sbuf_tensor("Acol", [128, 400], F32)
    selb = nc.alloc_sbuf_tensor("selb", [NEXP, NEXP * 128], BF16)
    eyeb = nc.alloc_sbuf_tensor("eyeb", [128, 128], BF16)
    slob = nc.alloc_sbuf_tensor("slob", [128, 128], BF16)
    onesc = nc.alloc_sbuf_tensor("onesc", [128, 1], BF16)
    ones4 = nc.alloc_sbuf_tensor("ones4", [16, 16], F32)
    onesf = nc.alloc_sbuf_tensor("onesf", [1, 128], F32)
    iotab = nc.alloc_sbuf_tensor("iotab", [128, NEXP], F32)
    iota8 = nc.alloc_sbuf_tensor("iota8", [8, 1], F32)
    cidb = nc.alloc_sbuf_tensor("cidb", [128, 4], F32)
    rbc = arb(112, T)
    mbc = arb(116, T)
    xdbf = arf(112, T)[0:96, :]
    xdbb = arb(120, T)[0:64, :]
    dtpw = arb(124, DI)[0:64, :]
    brow = arb(128, T)[0:16, :]
    crow = arb(132, T)[0:16, :]
    wgb = nc.alloc_sbuf_tensor("wgb", [128, ND * NEXP], BF16)
    bgb = nc.alloc_sbuf_tensor("bgb", [128, NEXP], F32)
    sco = arf(56, NTT * NEXP)
    rtw = arf(60, 64)
    oneh = arb(103, NEXP)
    onehf = arf(51, NEXP)
    posg = arf(61, NTT)
    idxg = arf(62, NTT)
    gateb = nc.alloc_sbuf_tensor("gateb", [128, NTT], F32)
    slotc = nc.alloc_sbuf_tensor("slotc", [128, NTT], I32)
    cnts = arf(52, 64)[0:16, :]
    cnt8 = arf(53, NEXP)[0:8, :]
    mask8 = arf(54, 1)[0:8, :]
    toffb = arf(58, NTT * NEXP)
    meta2 = arf(48, 768)
    scat_i = AR[:, 55 * 512: 55 * 512 + 256].bitcast(I32)
    stage = arb(64, 2 * D)
    def stage4(i):
        return arb(64 + 2 * i, D)
    b1col = nc.alloc_sbuf_tensor("b1col", [128, 64], F32)
    kcol = nc.alloc_sbuf_tensor("kcol", [128, 2], F32)
    b2col = nc.alloc_sbuf_tensor("b2col", [128, 16], F32)

    pss = [nc.alloc_psum_tensor("ps%d" % i, [128, 512], F32) for i in range(8)]

    with (
        nc.allow_non_contiguous_dma(reason="small per-partition vector loads"),
        nc.semaphore("sm_ld") as sm_ld,
        nc.semaphore("sm_gd") as sm_gd,
        nc.semaphore("sm_t") as sm_t,
        nc.semaphore("sm_v") as sm_v,
        nc.semaphore("sm_s") as sm_s,
        nc.semaphore("sm_g") as sm_g,
        nc.semaphore("sm_le") as sm_le,
        nc.semaphore("sm_ge") as sm_ge,
        nc.Block() as blk,
    ):
        sems = dict(ld=sm_ld, gd=sm_gd, t=sm_t, v=sm_v, s=sm_s, g=sm_g, le=sm_le, ge=sm_ge)
        cnt = {k: 0 for k in sems}
        echo = {"ld": [0, 0], "gd": [0, 0]}
        _q_eng = {"ld": "sync", "gd": "gpsimd"}
        _q_echo = {"ld": "le", "gd": "ge"}

        def _emit_echo(qname):
            raw = cnt[qname]
            esem = _q_echo[qname]

            def f(eng):
                eng.wait_ge(sems[qname], raw)
                eng.sem_inc(sems[esem], 1)
            getattr(blk, _q_eng[qname])(f)
            cnt[esem] += 1
            echo[qname][0] = raw
            echo[qname][1] = cnt[esem]
            return cnt[esem]

        def _xlate(waits, engname):
            out = []
            for nm, val in waits:
                if val <= 0:
                    continue
                if nm in ("ld", "gd"):
                    if engname == _q_eng[nm]:
                        out.append((nm, cnt[nm]))
                    else:
                        if echo[nm][0] < val:
                            _emit_echo(nm)
                        out.append((_q_echo[nm], echo[nm][1]))
                else:
                    out.append((nm, val))
            return out

        class _Proxy:

            def __init__(self, eng, semname, self_wait=True):
                self._eng = eng
                self._sn = semname
                self._sw = self_wait

            def __getattr__(self, name):
                real = getattr(self._eng, name)
                if not callable(real) or name.startswith("wait") or name == "dma_start":
                    return real

                def call(*a, **k):
                    if self._sw and cnt[self._sn] > 0:
                        self._eng.wait_ge(sems[self._sn], cnt[self._sn])
                    inst = real(*a, **k)
                    inst.then_inc(sems[self._sn], 1)
                    cnt[self._sn] += 1
                    return inst
                return call

        def _mk(engname, semname):
            self_wait = semname != "t"

            def emit(build, waits=()):
                tw = _xlate(waits, engname)
                def f(eng):
                    for nm, val in tw:
                        if val > 0:
                            eng.wait_ge(sems[nm], val)
                    build(_Proxy(eng, semname, self_wait))
                getattr(blk, engname)(f)
                return cnt[semname]
            return emit

        EV = _mk("vector", "v")
        ET = _mk("tensor", "t")
        EG = _mk("gpsimd", "g")

        scalar_pending = []

        def ES(build, waits=()):
            tw = _xlate(waits, "scalar")
            cnt["s"] += 1
            tok = cnt["s"]

            def closure(eng, tw=tw, tok=tok, build=build):
                if tok > 1:
                    eng.wait_ge(sems["s"], tok - 1)
                for nm, val in tw:
                    if val > 0:
                        eng.wait_ge(sems[nm], val)
                build(eng).then_inc(sems["s"], 1)
            scalar_pending.append(closure)
            return tok

        def flush_scalar():
            if scalar_pending:
                pend = list(scalar_pending)
                scalar_pending.clear()
                def f(eng):
                    for c in pend:
                        c(eng)
                blk.scalar(f)

        def SYD(out, in_, waits=(), **kw):
            tw = _xlate(waits, "sync")
            def f(sy):
                for nm, val in tw:
                    if val > 0:
                        sy.wait_ge(sems[nm], val)
                sy.dma_start(out=out, in_=in_, **kw).then_inc(sems["ld"], 16)
            blk.sync(f)
            cnt["ld"] += 16
            return cnt["ld"]

        def GPD(out, in_, waits=(), **kw):
            tw = _xlate(waits, "gpsimd")
            def f(gp):
                for nm, val in tw:
                    if val > 0:
                        gp.wait_ge(sems[nm], val)
                gp.dma_start(out=out, in_=in_, **kw).then_inc(sems["gd"], 16)
            blk.gpsimd(f)
            cnt["gd"] += 16
            return cnt["gd"]

        def GPI(out, out_off, in_, in_off, waits=()):
            tw = _xlate(waits, "gpsimd")
            def f(gp):
                for nm, val in tw:
                    if val > 0:
                        gp.wait_ge(sems[nm], val)
                gp.indirect_dma_start(out, out_off, in_, in_off).then_inc(sems["gd"], 16)
            blk.gpsimd(f)
            cnt["gd"] += 16
            return cnt["gd"]

        def tk(nm):
            return cnt[nm]

        def WAIT(engname, waits):
            tw = _xlate(waits, engname)
            def f(eng):
                for nm, val in tw:
                    if val > 0:
                        eng.wait_ge(sems[nm], val)
            getattr(blk, engname)(f)

        def ALLW():
            return [("v", tk("v")), ("s", tk("s")), ("t", tk("t")),
                    ("g", tk("g")), ("gd", tk("gd")), ("ld", tk("ld"))]

        SYD(eyeb[:], I["c_eye"][:])
        SYD(slob[:], I["c_slo"][:])
        SYD(selb[:], I["c_sel"][:])
        SYD(iotab[:], I["c_iota16b"][:])
        SYD(iota8[:], I["c_iota8c"][:])
        SYD(cidb[:, 3:4], I["c_iota128c"][:])
        SYD(cidb[:, 0:1], I["c_cid"][0:1, 0:1].partition_broadcast(128))
        SYD(S["xres"][:, :], I["xT"][:, :])
        SYD(pvec[:, 0:16], I["a_conv_b"].rearrange("(dt p) -> p dt", p=128))
        SYD(pvec[:, 16:32], I["a_dt_b"].rearrange("(dt p) -> p dt", p=128))
        SYD(pvec[:, 32:48], I["a_D"].rearrange("(dt p) -> p dt", p=128))
        SYD(pvec[:, 48:64], I["b_conv_b"].rearrange("(dt p) -> p dt", p=128))
        SYD(pvec[:, 64:80], I["b_b_dt"].rearrange("(dt p) -> p dt", p=128))
        SYD(pvec[:, 80:96], I["b_W_dt"].rearrange("(dt p) -> p dt", p=128))
        SYD(vsm[:, 0:8], I["a_ln_g"].rearrange("(dt p) -> p dt", p=128))
        SYD(vsm[:, 8:16], I["a_ln_b"].rearrange("(dt p) -> p dt", p=128))
        SYD(vsm[:, 16:32], I["a_glu_b"].rearrange("(dt p) -> p dt", p=128))
        SYD(vsm[:, 32:48], I["b_D"].rearrange("(dt p) -> p dt", p=128))
        SYD(Acol[:, 0:256].rearrange("p (dt n) -> p dt n", n=16),
            I["a_A_log"].rearrange("(dt p) n -> p dt n", p=128))
        SYD(Acol[:, 272:336].rearrange("p (dt k) -> p dt k", k=4),
            I["a_conv_w"].rearrange("(dt p) k -> p dt k", p=128))
        SYD(Acol[:, 336:400].rearrange("p (dt k) -> p dt k", k=4),
            I["b_conv_w"].rearrange("(dt p) k -> p dt k", p=128))
        SYD(arow[0:1, 0:16], I["b_A_log"][0:1, :])
        t_bgrow = SYD(bgb[0:1, 0:16], I["f_bg"][0:1, :])
        GPD(dtpw[:, :], I["a_dt_proj"][:, :])
        GPD(wgb.rearrange("p (k e) -> p k e", e=NEXP), I["f_Wg"].rearrange("(k p) e -> p k e", p=128))
        t_ldc = tk("ld")
        t_gdc = tk("gd")

        EV(lambda v: v.memset(onesf[:, :], 1.0))
        EV(lambda v: v.memset(ones4[:, :], 1.0))
        EV(lambda v: v.memset(onesall[:, :], 1.0))
        EV(lambda v: v.memset(kcol[:, 0:1], 1.0))
        EV(lambda v: v.memset(kcol[:, 1:2], EPS))
        t_ones = EV(lambda v: v.memset(onesc[:, :], 1.0))

        def f(v):
            v.tensor_scalar_mul(cidb[:, 1:2], cidb[:, 0:1], 2.0)
            v.tensor_scalar_add(cidb[:, 2:3], cidb[:, 1:2], 1.0)
            v.tensor_scalar_add(cidb[:, 3:4], cidb[:, 3:4], 2048.0)
        t_cid = EV(f, waits=[("ld", t_ldc)])

        ES(lambda s: s.activation(Acol[:, 0:256], Acol[:, 0:256], AF.Exp), waits=[("ld", t_ldc)])
        t_expA = ES(lambda s: s.activation(arow[0:1, 0:16], arow[0:1, 0:16], AF.Exp))
        EV(lambda v: v.tensor_scalar_mul(Acol[:, 0:256], Acol[:, 0:256], -1.0), waits=[("s", t_expA)])
        t_abrow = EV(lambda v: v.tensor_scalar_mul(arow[0:1, 0:16], arow[0:1, 0:16], -1.0))
        t_mab = ET(lambda t: t.matmul(pss[0][:, 0:16], onesf[0:1, :], arow[0:1, 0:16],
                                      start=True, stop=True), waits=[("v", t_abrow)])
        EV(lambda v: v.tensor_copy(Acol[:, 256:272], pss[0][:, 0:16]), waits=[("t", t_mab)])
        t_mbg = ET(lambda t: t.matmul(pss[1][:, 0:16], onesf[0:1, :], bgb[0:1, 0:16],
                                      start=True, stop=True), waits=[("ld", t_bgrow), ("v", tk("v"))])
        EV(lambda v: v.tensor_copy(bgb[:, 0:16], pss[1][:, 0:16]), waits=[("t", t_mbg)])

        def layer_norm(nd, dst_tile, src_hbm=None, src_tile=None, gamma_beta=None):
            tprev_t = tk("t")
            t_entry_v = tk("v")
            for dt in range(nd):
                if src_hbm is not None:
                    t_x = GPD(xbf[:, :], src_hbm[SL(dt * 128, (dt + 1) * 128), :],
                              waits=[("t", tprev_t), ("s", tk("s")), ("gd", tk("gd"))])
                    xsrc = xbf[:, :]
                    wx = [("gd", t_x)]
                else:
                    xsrc = src_tile(dt)
                    wx = []
                t_sq = ES(lambda s, xsrc=xsrc: s.activation(sqb[:, :], xsrc, AF.Square),
                          waits=wx + [("t", tprev_t), ("v", t_entry_v)])

                def fmm(t, dt=dt, xsrc=xsrc):
                    for j in range(4):
                        t.matmul(pss[j][:, 0:512], onesall[:, :], xsrc[:, SL(j * 512, (j + 1) * 512)],
                                 start=(dt == 0), stop=(dt == nd - 1))
                    for j in range(4):
                        t.matmul(pss[4 + j][:, 0:512], onesall[:, :], sqb[:, SL(j * 512, (j + 1) * 512)],
                                 start=(dt == 0), stop=(dt == nd - 1))
                extra = [("s", tk("s")), ("v", tk("v"))] if dt == 0 else []
                tprev_t = ET(fmm, waits=wx + [("s", t_sq), ("v", t_ones)] + extra)
            nf = float(nd * 128)
            muf = deltf[:, :]
            exf = arf(96, T)
            m2f = arf(104, T)

            def fst(v):
                for j in range(4):
                    v.tensor_scalar_mul(muf[:, SL(j * 512, (j + 1) * 512)], pss[j][:, 0:512], 1.0 / nf)
                    v.tensor_scalar_mul(exf[:, SL(j * 512, (j + 1) * 512)], pss[4 + j][:, 0:512], 1.0 / nf)
                v.tensor_mul(m2f, muf, muf)
                v.tensor_sub(exf, exf, m2f)
                v.tensor_scalar_add(exf, exf, EPS)
            t_var = EV(fst, waits=[("t", tprev_t), ("gd", tk("gd"))])
            t_sd = ES(lambda s: s.activation(exf, exf, AF.Sqrt), waits=[("v", t_var)])

            def frc(v):
                v.reciprocal(m2f, exf)
                v.tensor_copy(rbc[:, :], m2f)
                v.tensor_mul(mbc[:, :], muf, m2f)
            t_bc = EV(frc, waits=[("s", t_sd)])
            t_last = t_bc
            for dt in range(nd):
                if src_hbm is not None:
                    t_x = GPD(xbf[:, :], src_hbm[SL(dt * 128, (dt + 1) * 128), :],
                              waits=[("v", tk("v")), ("gd", tk("gd"))])
                    xsrc = xbf[:, :]
                    wx = [("gd", t_x)]
                else:
                    xsrc = src_tile(dt)
                    wx = []

                def fn(v, dt=dt, xsrc=xsrc):
                    dst = dst_tile(dt)
                    v.tensor_mul(sqb[:, :], xsrc, rbc[:, :])
                    if gamma_beta is None:
                        v.tensor_sub(dst, sqb[:, :], mbc[:, :])
                        return
                    gcol, bcol = gamma_beta
                    v.tensor_sub(sqb[:, :], sqb[:, :], mbc[:, :])
                    v.tensor_scalar(dst, sqb[:, :], gcol[:, SL(dt, dt + 1)],
                                    bcol[:, SL(dt, dt + 1)], ALU.mult, ALU.add)
                t_last = EV(fn, waits=wx + [("v", t_bc)])
            return t_last

        def mm_proj(w_hbm, kt_n, mt_n, rhs_tile, evac, woff_kb, rhs_wait=()):
            wb = arb(woff_kb, kt_n * mt_n * 128)
            t_w = GPD(wb.rearrange("p (k m c) -> p k m c", k=kt_n, m=mt_n, c=128),
                      w_hbm.rearrange("(k p) (m c) -> p k m c", p=128, c=128),
                      waits=[("t", tk("t"))])
            evtoks = []
            for mt in range(mt_n):
                for j in range(4):
                    idx = mt * 4 + j
                    pa = pss[idx % 4]

                    def fm(t, mt=mt, j=j, pa=pa):
                        for kt in range(kt_n):
                            i = t.matmul(pa[:, 0:512],
                                         wb[:, SL((kt * mt_n + mt) * 128, (kt * mt_n + mt + 1) * 128)],
                                         rhs_tile(kt)[:, SL(j * 512, (j + 1) * 512)],
                                         start=(kt == 0), stop=(kt == kt_n - 1))
                        return i
                    guard = [evtoks[idx - 4]] if idx >= 4 else []
                    t_mm = ET(fm, waits=[("gd", t_w)] + guard + list(rhs_wait))
                    evtoks.append(evac(mt, pa, j, t_mm))

        def conv_silu(dt, xi_ap, cw_off, cb_col, t_ready):
            def fc(v, dt=dt):
                def cw(k):
                    return Acol[:, SL(cw_off + dt * 4 + k, cw_off + dt * 4 + k + 1)]
                v.tensor_scalar_mul(deltf[:, :], xi_ap, cw(3))
                v.scalar_tensor_tensor(deltf[:, 1:T], xi_ap[:, 0:T - 1], cw(2), deltf[:, 1:T], ALU.mult, ALU.add)
                v.scalar_tensor_tensor(deltf[:, 2:T], xi_ap[:, 0:T - 2], cw(1), deltf[:, 2:T], ALU.mult, ALU.add)
                return v.scalar_tensor_tensor(deltf[:, 3:T], xi_ap[:, 0:T - 3], cw(0), deltf[:, 3:T],
                                              ALU.mult, ALU.add)
            t_cv = EV(fc, waits=list(t_ready))
            t_si = ES(lambda s, dt=dt: s.activation(xi_ap, deltf[:, :], AF.Silu,
                                                    bias=pvec[:, SL(cb_col + dt, cb_col + dt + 1)]),
                      waits=[("v", t_cv)])
            return GPD(S["u_sp"][SL(dt * 128, (dt + 1) * 128), :], xi_ap, waits=[("s", t_si)])

        def mamba(which):
            a = which == "a"

            def xln_t(dt):
                return arb(4 * dt, T)
            layer_norm(ND, xln_t, src_hbm=S["xres"])
            t_ln1 = tk("v")
            if a:
                for dt in range(ND):
                    t_cp = EV(lambda v, dt=dt: v.tensor_copy(deltf[:, :], xln_t(dt)),
                              waits=[("v", t_ln1), ("gd", tk("gd"))])
                    GPD(S["xres"][SL(dt * 128, (dt + 1) * 128), :], deltf[:, :],
                        waits=[("v", t_cp)], accum_op=ALU.add)
                layer_norm(ND, xln_t, src_tile=xln_t, gamma_beta=(vsm[:, 0:8], vsm[:, 8:16]))
            t_ln = tk("v")
            if ("tap_xln_" + which) in taps:
                for dt in range(ND):
                    t_cp = EV(lambda v, dt=dt: v.tensor_copy(deltf[:, :], xln_t(dt)), waits=[("v", t_ln)])
                    SYD(taps["tap_xln_" + which][SL(dt * 128, (dt + 1) * 128), :], deltf[:, :],
                        waits=[("v", t_cp)])

            w_hbm = I["a_in_proj"] if a else I["b_W_in"]
            cw_off = 272 if a else 336
            cb_col = 0 if a else 48

            def xi_stage(gmt):
                return arb(64 + 4 * (gmt % 2), T)

            for half in range(2):
                def evac(mt, pa, j, t_mm, half=half):
                    gmt = half * 16 + mt
                    if gmt < NDI:
                        t_e = EV(lambda v, pa=pa, j=j, gmt=gmt: v.tensor_copy(
                            xi_stage(gmt)[:, SL(j * 512, (j + 1) * 512)], pa[:, 0:512]),
                            waits=[("t", t_mm)] + ([("gd", tk("gd"))] if j == 0 else []))
                        if j == 3:
                            conv_silu(gmt, xi_stage(gmt), cw_off, cb_col, [("v", t_e)])
                        return ("v", t_e)
                    else:
                        t_e = ES(lambda s, pa=pa, j=j: s.activation(sqb[:, SL(j * 512, (j + 1) * 512)],
                                                                    pa[:, 0:512], AF.Silu),
                                 waits=[("t", t_mm)] + ([("gd", tk("gd"))] if j == 0 else []))
                        if j == 3:
                            GPD(S["sz_sp"][SL((gmt - NDI) * 128, (gmt - NDI + 1) * 128), :], sqb[:, :],
                                waits=[("s", t_e)])
                        return ("s", t_e)
                mm_proj(w_hbm[:, SL(half * 2048, (half + 1) * 2048)], ND, 16,
                        xln_t, evac, 32, rhs_wait=[("v", t_ln)])

            ncols = 96 if a else 33
            wx_hbm = I["a_x_proj"] if a else I["b_W_x"]
            wxb = arb(32, NDI * ncols)
            t_wx = GPD(wxb.rearrange("p (k c) -> p k c", k=NDI, c=ncols),
                       wx_hbm.rearrange("(k p) c -> p k c", p=128), waits=[("t", tk("t"))])
            for kt in range(NDI):
                tku = GPD(arb(64 + 4 * (kt % 2), T), S["u_sp"][SL(kt * 128, (kt + 1) * 128), :],
                          waits=[("t", tk("t")), ("v", tk("v") if kt == 0 else 0)])

                def fx(t, kt=kt):
                    for j in range(4):
                        i = t.matmul(pss[4 + j][0:ncols, 0:512], wxb[:, SL(kt * ncols, (kt + 1) * ncols)],
                                     arb(64 + 4 * (kt % 2), T)[:, SL(j * 512, (j + 1) * 512)],
                                     start=(kt == 0), stop=(kt == NDI - 1))
                    return i
                t_mx = ET(fx, waits=[("gd", tku)] + ([("gd", t_wx), ("v", tk("v")), ("s", tk("s"))]
                                                     if kt == 0 else []))
            for j in range(4):
                EV(lambda v, j=j: v.tensor_copy(xdbf[0:ncols, SL(j * 512, (j + 1) * 512)],
                                                pss[4 + j][0:ncols, 0:512]),
                   waits=[("t", t_mx)])
            t_xdb = tk("v")

            if a:
                t_dtc = EV(lambda v: v.tensor_copy(xdbb[:, :], xdbf[0:64, :]), waits=[("v", t_xdb)])
                t_br = GPD(brow[:, :], xdbf[64:80, :], waits=[("v", t_xdb)])
                t_cr = GPD(crow[:, :], xdbf[80:96, :], waits=[("v", t_xdb)])
            else:
                t_br = GPD(brow[:, :], xdbf[1:17, :], waits=[("v", t_xdb)])
                t_cr = GPD(crow[:, :], xdbf[17:33, :], waits=[("v", t_xdb)])
                for j in range(4):
                    t_bb = ET(lambda t, j=j: t.matmul(pss[5][:, 0:512], onesf[0:1, :],
                                                      xdbf[0:1, SL(j * 512, (j + 1) * 512)],
                                                      start=True, stop=True),
                              waits=[("v", t_xdb if j == 0 else tk("v"))])
                    EV(lambda v, j=j: v.tensor_copy(arf(72, T)[:, SL(j * 512, (j + 1) * 512)], pss[5][:, 0:512]),
                       waits=[("t", t_bb)])
            t_xp0 = tk("v")

            for dt in range(NDI):
                tku = GPD(arb(64 + 4 * (dt % 2), T), S["u_sp"][SL(dt * 128, (dt + 1) * 128), :],
                          waits=[("v", tk("v"))])
                if a:
                    for j in range(4):
                        t_m = ET(lambda t, dt=dt, j=j: t.matmul(pss[5][:, 0:512],
                                                                dtpw[:, SL(dt * 128, (dt + 1) * 128)],
                                                                xdbb[:, SL(j * 512, (j + 1) * 512)],
                                                                start=True, stop=True),
                                 waits=[("gd", t_gdc), ("v", t_dtc), ("s", tk("s"))])
                        ES(lambda s, j=j, dt=dt: s.activation(deltf[:, SL(j * 512, (j + 1) * 512)],
                                                              pss[5][:, 0:512], AF.Exp,
                                                              bias=pvec[:, SL(16 + dt, 16 + dt + 1)]),
                           waits=[("t", t_m)] + ([("v", tk("v"))] if j == 0 else []))
                        t_sp = ES(lambda s, j=j: s.activation(deltf[:, SL(j * 512, (j + 1) * 512)],
                                                              deltf[:, SL(j * 512, (j + 1) * 512)],
                                                              AF.Ln, bias=kcol[:, 0:1]))
                else:
                    t_dr = EV(lambda v, dt=dt: v.tensor_scalar(deltf[:, :], arf(72, T)[:, :],
                                                               pvec[:, SL(80 + dt, 80 + dt + 1)],
                                                               pvec[:, SL(64 + dt, 64 + dt + 1)],
                                                               ALU.mult, ALU.add),
                              waits=[("v", tk("v")), ("s", tk("s")), ("gd", tk("gd"))])
                    ES(lambda s: s.activation(deltf[:, :], deltf[:, :], AF.Exp), waits=[("v", t_dr)])
                    t_sp = ES(lambda s: s.activation(deltf[:, :], deltf[:, :], AF.Ln, bias=kcol[:, 0:1]))
                GPD(S["dt_sp"][SL(dt * 128, (dt + 1) * 128), :], deltf[:, :], waits=[("s", t_sp)])
                t_w = EV(lambda v, dt=dt: v.tensor_mul(sqb[:, :], deltf[:, :], arb(64 + 4 * (dt % 2), T)),
                         waits=[("s", t_sp), ("gd", tku)])
                GPD(S["w_sp"][SL(dt * 128, (dt + 1) * 128), :], sqb[:, :], waits=[("v", t_w)])

            acol_off = 0 if a else 256
            dt_ld = [0, 0]
            wt_ld = [0, 0]
            dt_cons = [tk("s"), tk("s")]
            wt_cons = [tk("v"), tk("v")]
            da_cons = [tk("v"), tk("v")]
            par_cons = [0, 0]

            def issue_loads(dtile, slt, extra=()):
                dt_ld[slt] = SYD(arf(64 + 8 * slt, T), S["dt_sp"][SL(dtile * 128, (dtile + 1) * 128), :],
                                 waits=[("s", dt_cons[slt])] + list(extra))
                wt_ld[slt] = SYD(arb(80 + 4 * slt, T), S["w_sp"][SL(dtile * 128, (dtile + 1) * 128), :],
                                 waits=[("v", wt_cons[slt])])

            issue_loads(0, 0, extra=[("v", tk("v")), ("gd", tk("gd")), ("s", tk("s"))])
            for q in range(4):
                for ni in range(4):
                    n = q * 4 + ni
                    for j in range(4):
                        t_m = ET(lambda t, n=n, j=j: t.matmul(pss[6][:, 0:512],
                                                              selb[:, SL(n * 128, (n + 1) * 128)],
                                                              brow[:, SL(j * 512, (j + 1) * 512)],
                                                              start=True, stop=True),
                                 waits=[("gd", t_br), ("v", tk("v"))])
                        EV(lambda v, ni=ni, j=j: v.tensor_copy(arb(4 * ni, T)[:, SL(j * 512, (j + 1) * 512)],
                                                               pss[6][:, 0:512]), waits=[("t", t_m)])
                        t_m2 = ET(lambda t, n=n, j=j: t.matmul(pss[7][:, 0:512],
                                                               selb[:, SL(n * 128, (n + 1) * 128)],
                                                               crow[:, SL(j * 512, (j + 1) * 512)],
                                                               start=True, stop=True),
                                  waits=[("gd", t_cr), ("v", tk("v"))])
                        EV(lambda v, ni=ni, j=j: v.tensor_copy(arb(16 + 4 * ni, T)[:, SL(j * 512, (j + 1) * 512)],
                                                               pss[7][:, 0:512]), waits=[("t", t_m2)])
                t_bcq = tk("v")

                for dt in range(NDI):
                    slt = dt % 2
                    for ni in range(4):
                        n = q * 4 + ni
                        dslt = ni % 2
                        ac_i = acol_off + (dt * 16 + n if a else n)
                        ac = Acol[:, SL(ac_i, ac_i + 1)]
                        t_da = ES(lambda s, slt=slt, dslt=dslt, ac=ac: s.activation(
                            arf(104 + 8 * dslt, T), arf(64 + 8 * slt, T), AF.Exp, scale=ac),
                            waits=[("ld", dt_ld[slt]), ("v", da_cons[dslt])])
                        dt_cons[slt] = t_da
                        t_wb = EV(lambda v, slt=slt, ni=ni: v.tensor_mul(arb(48 + 4 * ni, T),
                                                                         arb(80 + 4 * slt, T),
                                                                         arb(4 * ni, T)),
                                  waits=[("ld", wt_ld[slt]), ("v", t_bcq)])
                        wt_cons[slt] = t_wb
                        t_sc = EV(lambda v, ni=ni, dslt=dslt: v.tensor_tensor_scan(
                            arb(32 + 4 * ni, T), arf(104 + 8 * dslt, T), arb(48 + 4 * ni, T), 0.0,
                            ALU.mult, ALU.add), waits=[("s", t_da)])
                        da_cons[dslt] = t_sc
                    if dt < NDI - 1:
                        issue_loads(dt + 1, (dt + 1) % 2)
                    elif q < 3:
                        issue_loads(0, 0)
                    EV(lambda v: v.tensor_mul(arb(48, 4 * T), arb(32, 4 * T), arb(16, 4 * T)))
                    EV(lambda v: v.tensor_add(arb(48, 2 * T), arb(48, 2 * T), arb(56, 2 * T)))
                    psl = dt % 2
                    t_a2 = EV(lambda v, psl=psl: v.tensor_tensor(arf(88 + 8 * psl, T), arb(48, T),
                                                                 arb(52, T), ALU.add),
                              waits=[("gd", par_cons[psl])] if par_cons[psl] else [])
                    par_cons[psl] = GPD(S["y_sp"][SL(dt * 128, (dt + 1) * 128), :], arf(88 + 8 * psl, T),
                                        waits=[("v", t_a2)], accum_op=(ALU.add if q > 0 else ALU.bypass))
            t_scan = tk("gd")

            def Dcol(dt):
                return pvec[:, SL(32 + dt, 33 + dt)] if a else vsm[:, SL(32 + dt, 33 + dt)]
            for dt in range(NDI):
                t_y = SYD(arf(104, T), S["y_sp"][SL(dt * 128, (dt + 1) * 128), :],
                          waits=[("gd", t_scan), ("v", tk("v"))])
                GPD(xbf[:, :], S["u_sp"][SL(dt * 128, (dt + 1) * 128), :], waits=[("v", tk("v"))])
                t_z = GPD(sqb[:, :], S["sz_sp"][SL(dt * 128, (dt + 1) * 128), :])
                t_yd = EV(lambda v, dt=dt: v.scalar_tensor_tensor(deltf[:, :], xbf[:, :], Dcol(dt),
                                                                  arf(104, T), ALU.mult, ALU.add),
                          waits=[("ld", t_y), ("gd", t_z)])
                EV(lambda v, dt=dt: v.tensor_mul(arb(4 * dt, T), deltf[:, :], sqb[:, :]))
            t_rhs = tk("v")

            wout = I["a_out_proj"] if a else I["b_W_out"]

            def evac_out(mt, pa, j, t_mm):
                if a:
                    t_e = ES(lambda s, mt=mt, pa=pa, j=j: s.activation(
                        arb(96 + 4 * mt, T)[:, SL(j * 512, (j + 1) * 512)], pa[:, 0:512], AF.Gelu),
                       waits=[("t", t_mm)])
                    return ("s", t_e)
                else:
                    t_e = EV(lambda v, pa=pa, j=j: v.tensor_copy(deltf[:, SL(j * 512, (j + 1) * 512)],
                                                                 pa[:, 0:512]),
                             waits=[("t", t_mm)] + ([("gd", tk("gd"))] if j == 0 else []))
                    if j == 3:
                        GPD(S["xres"][SL(mt * 128, (mt + 1) * 128), :], deltf[:, :],
                            waits=[("v", t_e)], accum_op=ALU.add)
                    return ("v", t_e)
            mm_proj(wout, NDI, ND, lambda kt: arb(4 * kt, T), evac_out, 64, rhs_wait=[("v", t_rhs)])

            if a:
                t_gelu = tk("s")

                def evac_glu(mt, pa, j, t_mm):
                    if mt < 8:
                        t_e = EV(lambda v, mt=mt, pa=pa, j=j: v.tensor_scalar_add(
                            arb(4 * mt, T)[:, SL(j * 512, (j + 1) * 512)], pa[:, 0:512],
                            vsm[:, SL(16 + mt, 17 + mt)]), waits=[("t", t_mm)])
                        return ("v", t_e)
                    else:
                        t_e = ES(lambda s, mt=mt, pa=pa, j=j: s.activation(
                            arb(32 + 4 * (mt - 8), T)[:, SL(j * 512, (j + 1) * 512)], pa[:, 0:512],
                            AF.Sigmoid, bias=vsm[:, SL(16 + mt, 17 + mt)]), waits=[("t", t_mm)])
                        if j == 3:
                            mt8 = mt - 8
                            t_h = EV(lambda v, mt8=mt8: v.tensor_tensor(deltf[:, :], arb(4 * mt8, T),
                                                                        arb(32 + 4 * mt8, T), ALU.mult),
                                     waits=[("s", t_e), ("gd", tk("gd"))])
                            GPD(S["xres"][SL(mt8 * 128, (mt8 + 1) * 128), :], deltf[:, :],
                                waits=[("v", t_h)], accum_op=ALU.add)
                        return ("s", t_e)
                mm_proj(I["a_glu_w"], ND, 16, lambda kt: arb(96 + 4 * kt, T), evac_glu, 64,
                        rhs_wait=[("s", t_gelu)])

        def ffn():
            relu_toks = []
            yadd_toks = []

            def xln_t(dt):
                return arb(4 * dt, T)
            layer_norm(ND, xln_t, src_hbm=S["xres"])
            t_ln = tk("v")

            for tt in range(NTT):
                def fsc(t, tt=tt):
                    for kt in range(ND):
                        i = t.matmul(pss[5][:, 0:16], xln_t(kt)[:, SL(tt * 128, (tt + 1) * 128)],
                                     wgb[:, SL(kt * 16, (kt + 1) * 16)],
                                     start=(kt == 0), stop=(kt == ND - 1))
                    return i
                t_m = ET(fsc, waits=[("v", t_ln if tt == 0 else tk("v")), ("gd", t_gdc)])
                EV(lambda v, tt=tt: v.tensor_tensor(sco[:, SL(tt * 16, (tt + 1) * 16)], pss[5][:, 0:16],
                                                    bgb[:, 0:16], ALU.add), waits=[("t", t_m)])
            t_sco = tk("v")

            for tt in range(NTT):
                sc = sco[:, SL(tt * 16, (tt + 1) * 16)]

                def fr(v, tt=tt, sc=sc):
                    v.max(rtw[:, 0:8], sc)
                    v.max_index(rtw[:, 8:16].bitcast(U32), rtw[:, 0:8], sc)
                    v.tensor_copy(idxg[:, SL(tt, tt + 1)], rtw[:, 8:16].bitcast(U32)[:, 0:1])
                    return v.tensor_scalar_mul(rtw[:, 16:17], rtw[:, 0:1], -1.0)
                t_mx = EV(fr, waits=[("v", t_sco)])
                t_ex = ES(lambda s, sc=sc: s.activation(rtw[:, 32:48], sc, AF.Exp, bias=rtw[:, 16:17],
                                                        accum_out=rtw[:, 17:18]), waits=[("v", t_mx)])

                def fg(v, tt=tt):
                    v.reciprocal(rtw[:, 18:19], rtw[:, 17:18])
                    v.tensor_copy(gateb[:, SL(tt, tt + 1)], rtw[:, 18:19])
                    v.tensor_single_scalar(onehf[:, :], iotab[:, :], idxg[:, SL(tt, tt + 1)], ALU.is_equal)
                    return v.tensor_copy(oneh[:, :], onehf[:, :])
                t_oh = EV(fg, waits=[("s", t_ex)])
                ET(lambda t: t.matmul(pss[6][:, 0:16], slob[:, :], oneh[:, :], start=True, stop=True),
                   waits=[("v", t_oh)])
                t_cm = ET(lambda t: t.matmul(pss[7][0:16, 0:1], oneh[:, :], onesc[:, :],
                                             start=True, stop=True))

                def fp(v, tt=tt):
                    v.tensor_mul(rtw[:, 32:48], onehf[:, :], pss[6][:, 0:16])
                    v.reduce_sum(posg[:, SL(tt, tt + 1)], rtw[:, 32:48], AX.X)
                    return v.tensor_copy(cnts[0:16, SL(tt, tt + 1)], pss[7][0:16, 0:1])
                EV(fp, waits=[("t", t_cm)])
            t_rt = tk("v")

            def fcs(v):
                v.tensor_tensor_scan(cnts[0:16, 16:32], ones4[:, :], cnts[0:16, 0:16], 0.0, ALU.mult, ALU.add)
                v.memset(cnts[0:16, 32:33], 0.0)
                return v.tensor_copy(cnts[0:16, 33:48], cnts[0:16, 16:31])
            t_cs = EV(fcs, waits=[("v", t_rt)])
            t_ci = GPD(S["cc_c_in"][0:1, :], cnts[0:16, 31:32], waits=[("v", t_cs)])
            t_cc1 = EG(lambda gp: gp.collective_compute("AllGather", ALU.bypass,
                                                        replica_groups=[[0, 1, 2, 3, 4, 5, 6, 7]],
                                                        ins=[S["cc_c_in"][:]], outs=[S["C_all"][:]]),
                       waits=[("gd", t_ci)])
            t_c8 = GPD(cnt8[:, :], S["C_all"][:, :], waits=[("g", t_cc1)])
            t_mk = EV(lambda v: v.tensor_tensor(mask8[:, :], iota8[:, :], cidb[0:8, 0:1], ALU.is_lt),
                      waits=[("gd", t_c8), ("v", t_cid)])
            t_oc = ET(lambda t: t.matmul(pss[5][0:1, 0:16], mask8[:, :], cnt8[:, :], start=True, stop=True),
                      waits=[("v", t_mk)])
            t_oce = EV(lambda v: v.tensor_copy(arow[0:1, 0:16], pss[5][0:1, 0:16]), waits=[("t", t_oc)])
            t_occ = GPD(cnts[0:16, 48:49], arow[0:1, 0:16], waits=[("v", t_oce)])
            t_toff = EV(lambda v: v.tensor_scalar_add(cnts[0:16, 32:48], cnts[0:16, 32:48],
                                                      cnts[0:16, 48:49]), waits=[("gd", t_occ)])
            t_tfr = GPD(xdbf[0:1, 0:256], cnts[0:16, 32:48], waits=[("v", t_toff)])
            t_tb = 0
            for j in range(1):
                t_bb = ET(lambda t: t.matmul(pss[5][:, 0:256], onesf[0:1, :], xdbf[0:1, 0:256],
                                             start=True, stop=True), waits=[("gd", t_tfr)])
                t_tb = EV(lambda v: v.tensor_copy(toffb[:, 0:256], pss[5][:, 0:256]), waits=[("t", t_bb)])

            for tt in range(NTT):
                def fq(v, tt=tt):
                    v.tensor_single_scalar(onehf[:, :], iotab[:, :], idxg[:, SL(tt, tt + 1)], ALU.is_equal)
                    v.tensor_mul(rtw[:, 32:48], onehf[:, :],
                                 toffb.rearrange("p (e tt) -> p e tt", tt=16)[:, :, SL(tt, tt + 1)])
                    v.reduce_sum(rtw[:, 19:20], rtw[:, 32:48], AX.X)
                    v.tensor_add(posg[:, SL(tt, tt + 1)], posg[:, SL(tt, tt + 1)], rtw[:, 19:20])
                    v.tensor_single_scalar(rtw[:, 20:21], posg[:, SL(tt, tt + 1)], 1024.0, ALU.is_lt)
                    v.tensor_mul(gateb[:, SL(tt, tt + 1)], gateb[:, SL(tt, tt + 1)], rtw[:, 20:21])
                    v.tensor_scalar_mul(rtw[:, 21:22], idxg[:, SL(tt, tt + 1)], 1024.0)
                    v.tensor_add(rtw[:, 21:22], rtw[:, 21:22], posg[:, SL(tt, tt + 1)])
                    v.tensor_mul(rtw[:, 21:22], rtw[:, 21:22], rtw[:, 20:21])
                    v.tensor_copy(slotc[:, SL(tt, tt + 1)], rtw[:, 21:22])
                    v.tensor_copy(meta2[:, 0:1], idxg[:, SL(tt, tt + 1)])
                    return v.tensor_copy(meta2[:, 1:2], posg[:, SL(tt, tt + 1)])
                t_q = EV(fq, waits=[("v", t_tb), ("gd", tk("gd"))])
                GPD(S["cc_m_in"][SL(tt * 128, (tt + 1) * 128), :], meta2[:, 0:2], waits=[("v", t_q)])
            t_meta = tk("gd")

            for tt in range(NTT):
                for dc in range(ND):
                    t_tp = ET(lambda t, tt=tt, dc=dc: t.transpose(
                        pss[5 + (dc % 2)].bitcast(BF16)[:, 0:128],
                        xln_t(dc)[:, SL(tt * 128, (tt + 1) * 128)], eyeb[:, :]),
                        waits=[("v", tk("v")), ("gd", tk("gd")), ("ld", t_ldc)])
                    EV(lambda v, tt=tt, dc=dc: v.tensor_copy(
                        arb(32 + 2 * (tt % 2), D)[:, SL(dc * 128, (dc + 1) * 128)],
                        pss[5 + (dc % 2)].bitcast(BF16)[:, 0:128]), waits=[("t", t_tp)])
                GPD(S["cc_x_in"][SL(tt * 128, (tt + 1) * 128), :], arb(32 + 2 * (tt % 2), D),
                    waits=[("v", tk("v"))])
            t_xr = tk("gd")
            t_ccx = EG(lambda gp: gp.collective_compute("AllGather", ALU.bypass,
                                                        replica_groups=[[0, 1, 2, 3, 4, 5, 6, 7]],
                                                        ins=[S["cc_x_in"][:]], outs=[S["X_all"][:]]),
                       waits=[("gd", t_xr)])
            t_ccm = EG(lambda gp: gp.collective_compute("AllGather", ALU.bypass,
                                                        replica_groups=[[0, 1, 2, 3, 4, 5, 6, 7]],
                                                        ins=[S["cc_m_in"][:]], outs=[S["M_all"][:]]),
                       waits=[("gd", t_meta)])

            t_z = EV(lambda v: v.memset(stage[:, 0:D], 0.0), waits=[("gd", tk("gd"))])
            t_bz = GPD(S["B_own"].rearrange("(r p) d -> p r d", p=128),
                       stage[:, 0:D].unsqueeze(1).broadcast_to([128, 17, D]), waits=[("v", t_z)])

            t_mm2 = GPD(meta2[:, 0:256], S["M_all"].rearrange("(pp gp) -> pp gp", pp=128)
                        if False else S["M_all"].rearrange("(pp g) two -> pp (g two)", pp=128),
                        waits=[("g", t_ccm)])
            eyef = arf(46, 128)
            t_ef = EV(lambda v: v.tensor_copy(eyef[:, :], eyeb[:, :]), waits=[("gd", t_mm2)])

            def fsi(v):
                mv = meta2[:, 0:256].rearrange("p (g two) -> p g two", two=2)
                mi = mv[:, :, SL(0, 1)]
                mp = mv[:, :, SL(1, 2)]
                m0 = meta2[:, 256:384]
                m1 = meta2[:, 384:512]
                kp = meta2[:, 512:640]
                v.tensor_single_scalar(m0, mi, cidb[:, 1:2], ALU.is_equal)
                v.tensor_single_scalar(m1, mi, cidb[:, 2:3], ALU.is_equal)
                v.tensor_single_scalar(kp, mp, 1024.0, ALU.is_lt)
                v.tensor_add(m0, m0, m1)
                v.tensor_mul(m0, m0, kp)
                v.tensor_scalar_mul(m1, m1, 1024.0)
                v.tensor_tensor(m1, m1, mp, ALU.add)
                v.tensor_scalar_sub(m1, m1, cidb[:, 3:4])
                v.tensor_mul(m1, m1, m0)
                v.tensor_scalar_add(m1, m1, cidb[:, 3:4])
            t_sif = EV(fsi, waits=[("v", t_ef), ("v", t_cid)])
            t_tps = ET(lambda t: t.transpose(pss[5][:, 0:128], meta2[:, 384:512], eyef[:, :]),
                       waits=[("v", t_sif)])
            t_si = EV(lambda v: v.tensor_copy(scat_i[:, :], pss[5][:, 0:128]), waits=[("t", t_tps)])

            scat_toks = []
            for gq in range(128):
                slq = gq % 4
                w_slot = ([("gd", scat_toks[gq - 4])] if gq >= 4
                          else [("gd", t_bz), ("g", t_ccx), ("v", t_si)])
                t_ld = SYD(stage[:, SL(slq * 512, (slq + 1) * 512)].bitcast(BF16)[:, 0:D] if False
                           else stage4(slq), S["X_all"][SL(gq * 128, (gq + 1) * 128), :],
                           waits=w_slot)
                scat_toks.append(GPI(S["B_own"][:, :], IndirectOffsetOnAxis(scat_i[:, SL(gq, gq + 1)], 0),
                                     stage4(slq), None, waits=[("ld", t_ld)]))
            t_scat = tk("gd")

            t_b1 = SYD(b1col.rearrange("p (e ht) -> p e ht", e=2), I["f_b1c"].rearrange("e (ht p) -> p e ht", p=128))
            t_b2 = SYD(b2col.rearrange("p (e mt) -> p e mt", e=2), I["f_b2c"].rearrange("e (mt p) -> p e mt", p=128))
            for ei in range(2):
                for dc in range(ND):
                    SYD(arb(32 + 2 * dc, CAP), S["B_own"][SL(ei * CAP, (ei + 1) * CAP),
                                                          SL(dc * 128, (dc + 1) * 128)],
                        transpose=True,
                        waits=[("gd", t_scat), ("t", tk("t")), ("v", tk("v"))])
                t_xe = tk("ld")

                def xet(dc):
                    return arb(32 + 2 * dc, CAP)
                t_y0 = EV(lambda v: v.memset(arf(0, ND * CAP), 0.0), waits=[("gd", tk("gd")), ("s", tk("s")),
                                                                            ("t", tk("t"))])
                for hh in range(2):
                    t_w1 = GPD(arb(64, ND * 16 * 128).rearrange("p (k m c) -> p k m c", k=ND, m=16, c=128),
                               I["f_W1c"][ei, :, SL(hh * 2048, (hh + 1) * 2048)]
                               .rearrange("(k p) (m c) -> p k m c", p=128, c=128),
                               waits=[("t", tk("t")), ("gd", tk("gd"))])
                    relu_toks.clear()
                    for mt in range(16):
                        for j in range(2):
                            idx2 = mt * 2 + j
                            pa = pss[idx2 % 4]
                            guard = [("s", relu_toks[idx2 - 4])] if idx2 >= 4 else                                 ([("s", tk("s"))] if idx2 == 0 else [])

                            def fm1(t, mt=mt, j=j, pa=pa):
                                for kt in range(ND):
                                    i = t.matmul(pa[:, 0:512],
                                                 arb(64, ND * 16 * 128)[:, SL((kt * 16 + mt) * 128,
                                                                              (kt * 16 + mt + 1) * 128)],
                                                 xet(kt)[:, SL(j * 512, (j + 1) * 512)],
                                                 start=(kt == 0), stop=(kt == ND - 1))
                                return i
                            t_mm = ET(fm1, waits=[("gd", t_w1), ("ld", t_xe)] + guard)
                            relu_toks.append(ES(lambda s, mt=mt, j=j, pa=pa, hh=hh, ei=ei: s.activation(
                                arb(128 + 2 * mt, CAP)[:, SL(j * 512, (j + 1) * 512)], pa[:, 0:512],
                                AF.Relu, bias=b1col[:, SL(ei * 32 + hh * 16 + mt, ei * 32 + hh * 16 + mt + 1)]),
                                waits=[("t", t_mm), ("ld", t_b1)]))
                    t_relu = tk("s")
                    t_w2 = GPD(arb(96, 16 * ND * 128).rearrange("p (k m c) -> p k m c", k=16, m=ND, c=128),
                               I["f_W2c"][ei, SL(hh * 2048, (hh + 1) * 2048), :]
                               .rearrange("(k p) (m c) -> p k m c", p=128, c=128),
                               waits=[("t", tk("t"))])
                    yadd_toks.clear()
                    for mt in range(ND):
                        for j in range(2):
                            idx2 = mt * 2 + j
                            pa = pss[4 + idx2 % 4]
                            guard = [("v", yadd_toks[idx2 - 4])] if idx2 >= 4 else                                 ([("v", tk("v"))] if idx2 == 0 else [])

                            def fm2(t, mt=mt, j=j, pa=pa):
                                for kt in range(16):
                                    i = t.matmul(pa[:, 0:512],
                                                 arb(96, 16 * ND * 128)[:, SL((kt * ND + mt) * 128,
                                                                              (kt * ND + mt + 1) * 128)],
                                                 arb(128 + 2 * kt, CAP)[:, SL(j * 512, (j + 1) * 512)],
                                                 start=(kt == 0), stop=(kt == 15))
                                return i
                            t_mm = ET(fm2, waits=[("gd", t_w2), ("s", t_relu)] + guard)
                            yadd_toks.append(EV(lambda v, mt=mt, j=j, pa=pa: v.tensor_tensor(
                                arf(4 * mt, CAP)[:, SL(j * 512, (j + 1) * 512)],
                                arf(4 * mt, CAP)[:, SL(j * 512, (j + 1) * 512)], pa[:, 0:512], ALU.add),
                                waits=[("t", t_mm), ("v", t_y0)]))
                t_yv = tk("v")
                for mt in range(ND):
                    EV(lambda v, mt=mt, ei=ei: v.tensor_scalar_add(arb(48 + 2 * mt, CAP), arf(4 * mt, CAP),
                                                                   b2col[:, SL(ei * 8 + mt, ei * 8 + mt + 1)]),
                       waits=[("v", t_yv), ("ld", t_b2)] + ([("ld", tk("ld"))] if mt == 0 else []))
                t_yb = tk("v")
                for st in range(8):
                    for mt in range(ND):
                        t_tp = ET(lambda t, st=st, mt=mt: t.transpose(
                            pss[5 + (mt % 2)].bitcast(BF16)[:, 0:128],
                            arb(48 + 2 * mt, CAP)[:, SL(st * 128, (st + 1) * 128)], eyeb[:, :]),
                            waits=[("v", tk("v")), ("gd", tk("gd"))])
                        EV(lambda v, st=st, mt=mt: v.tensor_copy(
                            arb(64 + 2 * (st % 2), D)[:, SL(mt * 128, (mt + 1) * 128)],
                            pss[5 + (mt % 2)].bitcast(BF16)[:, 0:128]), waits=[("t", t_tp)])
                    GPD(S["cc_y_in"][SL(ei * CAP + st * 128, ei * CAP + (st + 1) * 128), :],
                        arb(64 + 2 * (st % 2), D), waits=[("v", tk("v"))])
            t_yrows = tk("gd")
            t_ccy = EG(lambda gp: gp.collective_compute("AllGather", ALU.bypass,
                                                        replica_groups=[[0, 1, 2, 3, 4, 5, 6, 7]],
                                                        ins=[S["cc_y_in"][:]], outs=[S["Y_all"][:]]),
                       waits=[("gd", t_yrows)])

            for tt in range(NTT):
                slq = tt % 2
                t_g = GPI(arb(32 + 2 * slq, D), None, S["Y_all"][:, :],
                          IndirectOffsetOnAxis(slotc[:, SL(tt, tt + 1)], 0),
                          waits=[("g", t_ccy), ("v", tk("v")), ("t", tk("t"))])
                t_gt = EV(lambda v, tt=tt, slq=slq: v.tensor_scalar_mul(arb(32 + 2 * slq, D),
                                                                        arb(32 + 2 * slq, D),
                                                                        gateb[:, SL(tt, tt + 1)]),
                          waits=[("gd", t_g)])
                for dc in range(ND):
                    t_tp = ET(lambda t, tt=tt, dc=dc, slq=slq: t.transpose(
                        pss[5 + (dc % 2)].bitcast(BF16)[:, 0:128],
                        arb(32 + 2 * slq, D)[:, SL(dc * 128, (dc + 1) * 128)], eyeb[:, :]),
                        waits=[("v", t_gt if dc == 0 else tk("v"))])
                    EV(lambda v, tt=tt, dc=dc: v.tensor_copy(
                        arb(4 * dc, T)[:, SL(tt * 128, (tt + 1) * 128)],
                        pss[5 + (dc % 2)].bitcast(BF16)[:, 0:128]), waits=[("t", t_tp)])
            t_asm = tk("v")
            for dc in range(ND):
                t_cp = EV(lambda v, dc=dc: v.tensor_copy(deltf[:, :], arb(4 * dc, T)),
                          waits=[("v", t_asm), ("gd", tk("gd"))])
                GPD(S["xres"][SL(dc * 128, (dc + 1) * 128), :], deltf[:, :],
                    waits=[("v", t_cp)], accum_op=ALU.add)

        for ph in PHASES:
            if ph == "a":
                mamba("a")
            elif ph == "b":
                mamba("b")
            elif ph == "f":
                ffn()

        flush_scalar()
        WAIT("sync", ALLW())
        for dt in range(ND):
            SYD(out_h[SL(dt * 128, (dt + 1) * 128), :], S["xres"][SL(dt * 128, (dt + 1) * 128), :])
        WAIT("sync", [("ld", tk("ld"))])
        WAIT("gpsimd", [("ld", tk("ld"))])

    return nc


def host_constants():
    import ml_dtypes
    eye = np.eye(128, dtype=np.float32).astype(ml_dtypes.bfloat16)
    slo = np.triu(np.ones((128, 128), np.float32), 1).astype(ml_dtypes.bfloat16)
    sel = np.zeros((NEXP, NEXP * 128), np.float32)
    for n in range(NEXP):
        sel[n, n * 128:(n + 1) * 128] = 1.0
    return {
        "c_eye": eye,
        "c_slo": slo,
        "c_sel": sel.astype(ml_dtypes.bfloat16),
        "c_iota16b": np.broadcast_to(np.arange(NEXP, dtype=np.float32), (128, NEXP)).copy(),
        "c_iota8c": np.arange(8, dtype=np.float32).reshape(8, 1),
        "c_iota128c": np.arange(128, dtype=np.float32).reshape(128, 1),
    }


def make_in_maps(inp):
    x = inp["L0_attn_a_x"]
    consts = host_constants()
    in_maps = []
    for c in range(8):
        m = {
            "xT": np.ascontiguousarray(x[c].T),
            "a_in_proj": inp["L0_attn_a_in_proj_w"],
            "a_conv_w": inp["L0_attn_a_conv_w"],
            "a_conv_b": inp["L0_attn_a_conv_b"],
            "a_x_proj": inp["L0_attn_a_x_proj_w"],
            "a_dt_proj": inp["L0_attn_a_dt_proj_w"],
            "a_dt_b": inp["L0_attn_a_dt_proj_b"],
            "a_A_log": inp["L0_attn_a_A_log"],
            "a_D": inp["L0_attn_a_D"],
            "a_out_proj": inp["L0_attn_a_out_proj_w"],
            "a_glu_w": inp["L0_attn_a_glu_w"],
            "a_glu_b": inp["L0_attn_a_glu_b"],
            "a_ln_g": inp["L0_attn_a_ln_gamma"],
            "a_ln_b": inp["L0_attn_a_ln_beta"],
            "b_W_in": inp["L0_attn_b_W_in"],
            "b_conv_w": np.ascontiguousarray(inp["L0_attn_b_conv_w"].reshape(DI, 4)),
            "b_conv_b": inp["L0_attn_b_conv_b"],
            "b_W_x": inp["L0_attn_b_W_x"],
            "b_W_dt": np.ascontiguousarray(inp["L0_attn_b_W_dt"].reshape(DI)),
            "b_b_dt": inp["L0_attn_b_b_dt"],
            "b_A_log": inp["L0_attn_b_A_log"].reshape(1, NST),
            "b_D": inp["L0_attn_b_D"],
            "b_W_out": inp["L0_attn_b_W_out"],
            "f_Wg": inp["L0_ffn_Wg"],
            "f_bg": inp["L0_ffn_bg"].reshape(1, NEXP),
            "f_W1c": np.ascontiguousarray(inp["L0_ffn_W1"][2 * c:2 * c + 2]),
            "f_b1c": np.ascontiguousarray(inp["L0_ffn_b1"][2 * c:2 * c + 2]),
            "f_W2c": np.ascontiguousarray(inp["L0_ffn_W2"][2 * c:2 * c + 2]),
            "f_b2c": np.ascontiguousarray(inp["L0_ffn_b2"][2 * c:2 * c + 2]),
            "c_cid": np.array([[float(c)]], dtype=np.float32),
        }
        m.update(consts)
        in_maps.append(m)
    return in_maps


def kernel(**inp):
    nc = build_program()
    in_maps = make_in_maps(inp)
    res = run_bass_kernel_spmd(nc, in_maps, core_ids=list(range(8)))
    return np.stack([np.ascontiguousarray(res.results[c]["outT"].T) for c in range(8)], axis=0)


# version 04
# speedup vs baseline: 1.4295x; 1.0241x over previous
import numpy as np
import concourse.bass as bass
import concourse.mybir as mybir
from concourse.bass import IndirectOffsetOnAxis
from concourse.bass_utils import run_bass_kernel_spmd

F32 = mybir.dt.float32
BF16 = mybir.dt.bfloat16
I32 = mybir.dt.int32
U32 = mybir.dt.uint32
AF = mybir.ActivationFunctionType
ALU = mybir.AluOpType
AX = mybir.AxisListType

D = 1024
T = 2048
DI = 2048
NST = 16
NEXP = 16
CAP = 1024
HF = 4096
ND = D // 128
NDI = DI // 128
NTT = T // 128
EPS = 1e-5
SL = slice

DEBUG_TAPS = {}
PHASES = ("a", "f", "b", "f")


def build_program():
    nc = bass.Bass("TRN2", target_bir_lowering=False, debug=False, num_devices=8)

    def din(name, shape, dt=F32):
        return nc.dram_tensor(name, shape, dt, kind="ExternalInput").ap()

    I = {}
    I["xT"] = din("xT", [D, T])
    I["a_in_proj"] = din("a_in_proj", [D, 2 * DI])
    I["a_conv_w"] = din("a_conv_w", [DI, 4])
    I["a_conv_b"] = din("a_conv_b", [DI])
    I["a_x_proj"] = din("a_x_proj", [DI, 96])
    I["a_dt_proj"] = din("a_dt_proj", [64, DI])
    I["a_dt_b"] = din("a_dt_b", [DI])
    I["a_A_log"] = din("a_A_log", [DI, NST])
    I["a_D"] = din("a_D", [DI])
    I["a_out_proj"] = din("a_out_proj", [DI, D])
    I["a_glu_w"] = din("a_glu_w", [D, 2 * D])
    I["a_glu_b"] = din("a_glu_b", [2 * D])
    I["a_ln_g"] = din("a_ln_g", [D])
    I["a_ln_b"] = din("a_ln_b", [D])
    I["b_W_in"] = din("b_W_in", [D, 2 * DI])
    I["b_conv_w"] = din("b_conv_w", [DI, 4])
    I["b_conv_b"] = din("b_conv_b", [DI])
    I["b_W_x"] = din("b_W_x", [DI, 33])
    I["b_W_dt"] = din("b_W_dt", [DI])
    I["b_b_dt"] = din("b_b_dt", [DI])
    I["b_A_log"] = din("b_A_log", [1, NST])
    I["b_D"] = din("b_D", [DI])
    I["b_W_out"] = din("b_W_out", [DI, D])
    I["f_Wg"] = din("f_Wg", [D, NEXP])
    I["f_bg"] = din("f_bg", [1, NEXP])
    I["f_W1c"] = din("f_W1c", [2, D, HF])
    I["f_b1c"] = din("f_b1c", [2, HF])
    I["f_W2c"] = din("f_W2c", [2, HF, D])
    I["f_b2c"] = din("f_b2c", [2, D])
    I["c_eye"] = din("c_eye", [128, 128], BF16)
    I["c_slo"] = din("c_slo", [128, 128], BF16)
    I["c_sel"] = din("c_sel", [NEXP, NEXP * 128], BF16)
    I["c_iota16b"] = din("c_iota16b", [128, NEXP])
    I["c_iota8c"] = din("c_iota8c", [8, 1])
    I["c_iota128c"] = din("c_iota128c", [128, 1])
    I["c_cid"] = din("c_cid", [1, 1])

    out_h = nc.dram_tensor("outT", [D, T], F32, kind="ExternalOutput").ap()

    S = {}
    S["xres"] = nc.dram_tensor("xres", [D, T], F32).ap()
    S["u_sp"] = nc.dram_tensor("u_sp", [DI, T], BF16).ap()
    S["sz_sp"] = nc.dram_tensor("sz_sp", [DI, T], BF16).ap()
    S["dt_sp"] = nc.dram_tensor("dt_sp", [DI, T], F32).ap()
    S["w_sp"] = nc.dram_tensor("w_sp", [DI, T], BF16).ap()
    S["y_sp"] = nc.dram_tensor("y_sp", [DI, T], F32).ap()
    S["cc_x_in"] = nc.dram_tensor("cc_x_in", [T, D], BF16).ap()
    S["X_all"] = nc.dram_tensor("X_all", [8 * T, D], BF16, addr_space="Shared").ap()
    S["cc_m_in"] = nc.dram_tensor("cc_m_in", [T, 2], F32).ap()
    S["M_all"] = nc.dram_tensor("M_all", [8 * T, 2], F32, addr_space="Shared").ap()
    S["cc_c_in"] = nc.dram_tensor("cc_c_in", [1, NEXP], F32).ap()
    S["C_all"] = nc.dram_tensor("C_all", [8, NEXP], F32, addr_space="Shared").ap()
    S["B_own"] = nc.dram_tensor("B_own", [2 * CAP + 128, D], BF16).ap()
    S["cc_y_in"] = nc.dram_tensor("cc_y_in", [2 * CAP, D], BF16).ap()
    S["Y_all"] = nc.dram_tensor("Y_all", [NEXP * CAP, D], BF16, addr_space="Shared").ap()

    taps = {}
    for name, (shape, dt) in DEBUG_TAPS.items():
        taps[name] = nc.dram_tensor(name, shape, dt, kind="ExternalOutput").ap()

    AR = nc.alloc_sbuf_tensor("AR", [128, 80 * 1024], BF16)

    def arb(off_kb, nbf16):
        return AR[:, off_kb * 512: off_kb * 512 + nbf16]

    def arf(off_kb, nf32):
        return AR[:, off_kb * 512: off_kb * 512 + 2 * nf32].bitcast(F32)

    arow = nc.alloc_sbuf_tensor("arow", [1, 16], F32)
    onesall = nc.alloc_sbuf_tensor("onesall", [128, 128], BF16)
    xbf = nc.alloc_sbuf_tensor("xbf", [128, T], BF16)
    sqb = nc.alloc_sbuf_tensor("sqb", [128, T], BF16)
    rbc = None
    mbc = None
    deltf = nc.alloc_sbuf_tensor("deltf", [128, T], F32)
    vsm = nc.alloc_sbuf_tensor("vsm", [128, 48], F32)
    pvec = nc.alloc_sbuf_tensor("pvec", [128, 96], F32)
    Acol = nc.alloc_sbuf_tensor("Acol", [128, 400], F32)
    selb = nc.alloc_sbuf_tensor("selb", [NEXP, NEXP * 128], BF16)
    eyeb = nc.alloc_sbuf_tensor("eyeb", [128, 128], BF16)
    slob = nc.alloc_sbuf_tensor("slob", [128, 128], BF16)
    onesc = nc.alloc_sbuf_tensor("onesc", [128, 1], BF16)
    ones4 = nc.alloc_sbuf_tensor("ones4", [16, 16], F32)
    onesf = nc.alloc_sbuf_tensor("onesf", [1, 128], F32)
    iotab = nc.alloc_sbuf_tensor("iotab", [128, NEXP], F32)
    iota8 = nc.alloc_sbuf_tensor("iota8", [8, 1], F32)
    cidb = nc.alloc_sbuf_tensor("cidb", [128, 4], F32)
    rbc = arb(112, T)
    mbc = arb(116, T)
    xdbf = arf(112, T)[0:96, :]
    xdbb = arb(120, T)[0:64, :]
    dtpw = arb(124, DI)[0:64, :]
    brow = arb(128, T)[0:16, :]
    crow = arb(132, T)[0:16, :]
    wgb = nc.alloc_sbuf_tensor("wgb", [128, ND * NEXP], BF16)
    bgb = nc.alloc_sbuf_tensor("bgb", [128, NEXP], F32)
    sco = arf(56, NTT * NEXP)
    rtw = arf(60, 64)
    oneh = arb(103, NEXP)
    onehf = arf(51, NEXP)
    posg = arf(61, NTT)
    idxg = arf(62, NTT)
    gateb = nc.alloc_sbuf_tensor("gateb", [128, NTT], F32)
    slotc = nc.alloc_sbuf_tensor("slotc", [128, NTT], I32)
    cnts = arf(52, 64)[0:16, :]
    cnt8 = arf(53, NEXP)[0:8, :]
    mask8 = arf(54, 1)[0:8, :]
    toffb = arf(58, NTT * NEXP)
    meta2 = arf(48, 768)
    scat_i = AR[:, 55 * 512: 55 * 512 + 256].bitcast(I32)
    stage = arb(64, 2 * D)
    def stage4(i):
        return arb(64 + 2 * i, D)
    b1col = nc.alloc_sbuf_tensor("b1col", [128, 64], F32)
    kcol = nc.alloc_sbuf_tensor("kcol", [128, 2], F32)
    b2col = nc.alloc_sbuf_tensor("b2col", [128, 16], F32)

    pss = [nc.alloc_psum_tensor("ps%d" % i, [128, 512], F32) for i in range(8)]

    with (
        nc.allow_non_contiguous_dma(reason="small per-partition vector loads"),
        nc.semaphore("sm_ld") as sm_ld,
        nc.semaphore("sm_gd") as sm_gd,
        nc.semaphore("sm_t") as sm_t,
        nc.semaphore("sm_v") as sm_v,
        nc.semaphore("sm_s") as sm_s,
        nc.semaphore("sm_g") as sm_g,
        nc.semaphore("sm_le") as sm_le,
        nc.semaphore("sm_ge") as sm_ge,
        nc.Block() as blk,
    ):
        sems = dict(ld=sm_ld, gd=sm_gd, t=sm_t, v=sm_v, s=sm_s, g=sm_g, le=sm_le, ge=sm_ge)
        cnt = {k: 0 for k in sems}
        echo = {"ld": [0, 0], "gd": [0, 0]}
        _q_eng = {"ld": "sync", "gd": "gpsimd"}
        _q_echo = {"ld": "le", "gd": "ge"}

        def _emit_echo(qname):
            raw = cnt[qname]
            esem = _q_echo[qname]

            def f(eng):
                eng.wait_ge(sems[qname], raw)
                eng.sem_inc(sems[esem], 1)
            getattr(blk, _q_eng[qname])(f)
            cnt[esem] += 1
            echo[qname][0] = raw
            echo[qname][1] = cnt[esem]
            return cnt[esem]

        def _xlate(waits, engname):
            out = []
            for nm, val in waits:
                if val <= 0:
                    continue
                if nm in ("ld", "gd"):
                    if engname == _q_eng[nm]:
                        out.append((nm, cnt[nm]))
                    else:
                        if echo[nm][0] < val:
                            _emit_echo(nm)
                        out.append((_q_echo[nm], echo[nm][1]))
                else:
                    out.append((nm, val))
            return out

        class _Proxy:

            def __init__(self, eng, semname, self_wait=True):
                self._eng = eng
                self._sn = semname
                self._sw = self_wait

            def __getattr__(self, name):
                real = getattr(self._eng, name)
                if not callable(real) or name.startswith("wait") or name == "dma_start":
                    return real

                def call(*a, **k):
                    if self._sw and cnt[self._sn] > 0:
                        self._eng.wait_ge(sems[self._sn], cnt[self._sn])
                    inst = real(*a, **k)
                    inst.then_inc(sems[self._sn], 1)
                    cnt[self._sn] += 1
                    return inst
                return call

        def _mk(engname, semname):
            self_wait = semname != "t"

            def emit(build, waits=()):
                tw = _xlate(waits, engname)
                def f(eng):
                    for nm, val in tw:
                        if val > 0:
                            eng.wait_ge(sems[nm], val)
                    build(_Proxy(eng, semname, self_wait))
                getattr(blk, engname)(f)
                return cnt[semname]
            return emit

        EV = _mk("vector", "v")
        ET = _mk("tensor", "t")
        EG = _mk("gpsimd", "g")

        scalar_pending = []

        def ES(build, waits=()):
            tw = _xlate(waits, "scalar")
            cnt["s"] += 1
            tok = cnt["s"]

            def closure(eng, tw=tw, tok=tok, build=build):
                if tok > 1:
                    eng.wait_ge(sems["s"], tok - 1)
                for nm, val in tw:
                    if val > 0:
                        eng.wait_ge(sems[nm], val)
                build(eng).then_inc(sems["s"], 1)
            scalar_pending.append(closure)
            return tok

        def flush_scalar():
            if scalar_pending:
                pend = list(scalar_pending)
                scalar_pending.clear()
                def f(eng):
                    for c in pend:
                        c(eng)
                blk.scalar(f)

        def SYD(out, in_, waits=(), **kw):
            tw = _xlate(waits, "sync")
            def f(sy):
                for nm, val in tw:
                    if val > 0:
                        sy.wait_ge(sems[nm], val)
                sy.dma_start(out=out, in_=in_, **kw).then_inc(sems["ld"], 16)
            blk.sync(f)
            cnt["ld"] += 16
            return cnt["ld"]

        def GPD(out, in_, waits=(), **kw):
            tw = _xlate(waits, "gpsimd")
            def f(gp):
                for nm, val in tw:
                    if val > 0:
                        gp.wait_ge(sems[nm], val)
                gp.dma_start(out=out, in_=in_, **kw).then_inc(sems["gd"], 16)
            blk.gpsimd(f)
            cnt["gd"] += 16
            return cnt["gd"]

        def GPI(out, out_off, in_, in_off, waits=()):
            tw = _xlate(waits, "gpsimd")
            def f(gp):
                for nm, val in tw:
                    if val > 0:
                        gp.wait_ge(sems[nm], val)
                gp.indirect_dma_start(out, out_off, in_, in_off).then_inc(sems["gd"], 16)
            blk.gpsimd(f)
            cnt["gd"] += 16
            return cnt["gd"]

        def tk(nm):
            return cnt[nm]

        def WAIT(engname, waits):
            tw = _xlate(waits, engname)
            def f(eng):
                for nm, val in tw:
                    if val > 0:
                        eng.wait_ge(sems[nm], val)
            getattr(blk, engname)(f)

        def ALLW():
            return [("v", tk("v")), ("s", tk("s")), ("t", tk("t")),
                    ("g", tk("g")), ("gd", tk("gd")), ("ld", tk("ld"))]

        SYD(eyeb[:], I["c_eye"][:])
        SYD(slob[:], I["c_slo"][:])
        SYD(selb[:], I["c_sel"][:])
        SYD(iotab[:], I["c_iota16b"][:])
        SYD(iota8[:], I["c_iota8c"][:])
        SYD(cidb[:, 3:4], I["c_iota128c"][:])
        SYD(cidb[:, 0:1], I["c_cid"][0:1, 0:1].partition_broadcast(128))
        SYD(S["xres"][:, :], I["xT"][:, :])
        SYD(pvec[:, 0:16], I["a_conv_b"].rearrange("(dt p) -> p dt", p=128))
        SYD(pvec[:, 16:32], I["a_dt_b"].rearrange("(dt p) -> p dt", p=128))
        SYD(pvec[:, 32:48], I["a_D"].rearrange("(dt p) -> p dt", p=128))
        SYD(pvec[:, 48:64], I["b_conv_b"].rearrange("(dt p) -> p dt", p=128))
        SYD(pvec[:, 64:80], I["b_b_dt"].rearrange("(dt p) -> p dt", p=128))
        SYD(pvec[:, 80:96], I["b_W_dt"].rearrange("(dt p) -> p dt", p=128))
        SYD(vsm[:, 0:8], I["a_ln_g"].rearrange("(dt p) -> p dt", p=128))
        SYD(vsm[:, 8:16], I["a_ln_b"].rearrange("(dt p) -> p dt", p=128))
        SYD(vsm[:, 16:32], I["a_glu_b"].rearrange("(dt p) -> p dt", p=128))
        SYD(vsm[:, 32:48], I["b_D"].rearrange("(dt p) -> p dt", p=128))
        SYD(Acol[:, 0:256].rearrange("p (dt n) -> p dt n", n=16),
            I["a_A_log"].rearrange("(dt p) n -> p dt n", p=128))
        SYD(Acol[:, 272:336].rearrange("p (dt k) -> p dt k", k=4),
            I["a_conv_w"].rearrange("(dt p) k -> p dt k", p=128))
        SYD(Acol[:, 336:400].rearrange("p (dt k) -> p dt k", k=4),
            I["b_conv_w"].rearrange("(dt p) k -> p dt k", p=128))
        SYD(arow[0:1, 0:16], I["b_A_log"][0:1, :])
        t_bgrow = SYD(bgb[0:1, 0:16], I["f_bg"][0:1, :])
        GPD(dtpw[:, :], I["a_dt_proj"][:, :])
        GPD(wgb.rearrange("p (k e) -> p k e", e=NEXP), I["f_Wg"].rearrange("(k p) e -> p k e", p=128))
        t_ldc = tk("ld")
        t_gdc = tk("gd")

        EV(lambda v: v.memset(onesf[:, :], 1.0))
        EV(lambda v: v.memset(ones4[:, :], 1.0))
        EV(lambda v: v.memset(onesall[:, :], 1.0))
        EV(lambda v: v.memset(kcol[:, 0:1], 1.0))
        EV(lambda v: v.memset(kcol[:, 1:2], EPS))
        t_ones = EV(lambda v: v.memset(onesc[:, :], 1.0))

        def f(v):
            v.tensor_scalar_mul(cidb[:, 1:2], cidb[:, 0:1], 2.0)
            v.tensor_scalar_add(cidb[:, 2:3], cidb[:, 1:2], 1.0)
            v.tensor_scalar_add(cidb[:, 3:4], cidb[:, 3:4], 2048.0)
        t_cid = EV(f, waits=[("ld", t_ldc)])

        ES(lambda s: s.activation(Acol[:, 0:256], Acol[:, 0:256], AF.Exp), waits=[("ld", t_ldc)])
        t_expA = ES(lambda s: s.activation(arow[0:1, 0:16], arow[0:1, 0:16], AF.Exp))
        EV(lambda v: v.tensor_scalar_mul(Acol[:, 0:256], Acol[:, 0:256], -1.0), waits=[("s", t_expA)])
        t_abrow = EV(lambda v: v.tensor_scalar_mul(arow[0:1, 0:16], arow[0:1, 0:16], -1.0))
        t_mab = ET(lambda t: t.matmul(pss[0][:, 0:16], onesf[0:1, :], arow[0:1, 0:16],
                                      start=True, stop=True), waits=[("v", t_abrow)])
        EV(lambda v: v.tensor_copy(Acol[:, 256:272], pss[0][:, 0:16]), waits=[("t", t_mab)])
        t_mbg = ET(lambda t: t.matmul(pss[1][:, 0:16], onesf[0:1, :], bgb[0:1, 0:16],
                                      start=True, stop=True), waits=[("ld", t_bgrow), ("v", tk("v"))])
        EV(lambda v: v.tensor_copy(bgb[:, 0:16], pss[1][:, 0:16]), waits=[("t", t_mbg)])

        def layer_norm(nd, dst_tile, src_hbm=None, src_tile=None, gamma_beta=None):
            tprev_t = tk("t")
            t_entry_v = tk("v")
            deltf_bf = deltf[:, 0:1024].bitcast(BF16)
            x_slots = (xbf[:, :], deltf_bf[:, 0:T])
            sq_slots = (sqb[:, :], rbc[:, :])
            mm_toks = [tprev_t, tprev_t]
            for dt in range(nd):
                sl = dt % 2
                if src_hbm is not None:
                    t_x = GPD(x_slots[sl], src_hbm[SL(dt * 128, (dt + 1) * 128), :],
                              waits=[("t", mm_toks[sl]), ("s", tk("s") if dt < 2 else 0),
                                     ("gd", tk("gd") if dt == 0 else 0),
                                     ("v", t_entry_v if dt < 2 else 0)])
                    xsrc = x_slots[sl]
                    wx = [("gd", t_x)]
                else:
                    xsrc = src_tile(dt)
                    wx = []
                t_sq = ES(lambda s, xsrc=xsrc, sl=sl: s.activation(sq_slots[sl], xsrc, AF.Square),
                          waits=wx + [("t", mm_toks[sl]), ("v", t_entry_v if dt < 2 else 0)])

                def fmm(t, dt=dt, xsrc=xsrc, sl=sl):
                    for j in range(4):
                        t.matmul(pss[j][:, 0:512], onesall[:, :], xsrc[:, SL(j * 512, (j + 1) * 512)],
                                 start=(dt == 0), stop=(dt == nd - 1))
                    for j in range(4):
                        t.matmul(pss[4 + j][:, 0:512], onesall[:, :], sq_slots[sl][:, SL(j * 512, (j + 1) * 512)],
                                 start=(dt == 0), stop=(dt == nd - 1))
                extra = [("s", tk("s")), ("v", tk("v"))] if dt == 0 else []
                mm_toks[sl] = ET(fmm, waits=wx + [("s", t_sq), ("v", t_ones)] + extra)
            tprev_t = max(mm_toks)
            nf = float(nd * 128)
            muf = deltf[:, :]
            exf = arf(96, T)
            m2f = arf(104, T)

            def fst(v):
                for j in range(4):
                    v.tensor_scalar_mul(muf[:, SL(j * 512, (j + 1) * 512)], pss[j][:, 0:512], 1.0 / nf)
                    v.tensor_scalar_mul(exf[:, SL(j * 512, (j + 1) * 512)], pss[4 + j][:, 0:512], 1.0 / nf)
                v.tensor_mul(m2f, muf, muf)
                v.tensor_sub(exf, exf, m2f)
                v.tensor_scalar_add(exf, exf, EPS)
            t_var = EV(fst, waits=[("t", tprev_t), ("gd", tk("gd"))])
            t_sd = ES(lambda s: s.activation(exf, exf, AF.Sqrt), waits=[("v", t_var)])

            def frc(v):
                v.reciprocal(m2f, exf)
                v.tensor_copy(rbc[:, :], m2f)
                v.tensor_mul(mbc[:, :], muf, m2f)
            t_bc = EV(frc, waits=[("s", t_sd)])
            t_last = t_bc
            fn_toks = [t_bc, t_bc]
            for dt in range(nd):
                sl = dt % 2
                if src_hbm is not None:
                    t_x = GPD(x_slots[sl], src_hbm[SL(dt * 128, (dt + 1) * 128), :],
                              waits=[("v", fn_toks[sl] if dt >= 2 else t_bc)])
                    xsrc = x_slots[sl]
                    wx = [("gd", t_x)]
                else:
                    xsrc = src_tile(dt)
                    wx = []

                def fn(v, dt=dt, xsrc=xsrc):
                    dst = dst_tile(dt)
                    v.tensor_mul(sqb[:, :], xsrc, rbc[:, :])
                    if gamma_beta is None:
                        v.tensor_sub(dst, sqb[:, :], mbc[:, :])
                        return
                    gcol, bcol = gamma_beta
                    v.tensor_sub(sqb[:, :], sqb[:, :], mbc[:, :])
                    v.tensor_scalar(dst, sqb[:, :], gcol[:, SL(dt, dt + 1)],
                                    bcol[:, SL(dt, dt + 1)], ALU.mult, ALU.add)
                t_last = EV(fn, waits=wx + [("v", t_bc)])
                fn_toks[sl] = t_last
            return t_last

        def mm_proj(w_hbm, kt_n, mt_n, rhs_tile, evac, woff_kb, rhs_wait=()):
            wb = arb(woff_kb, kt_n * mt_n * 128)
            t_w = GPD(wb.rearrange("p (k m c) -> p k m c", k=kt_n, m=mt_n, c=128),
                      w_hbm.rearrange("(k p) (m c) -> p k m c", p=128, c=128),
                      waits=[("t", tk("t")), ("v", tk("v")), ("s", tk("s"))])
            evtoks = []
            for mt in range(mt_n):
                for j in range(4):
                    idx = mt * 4 + j
                    pa = pss[idx % 4]

                    def fm(t, mt=mt, j=j, pa=pa):
                        for kt in range(kt_n):
                            i = t.matmul(pa[:, 0:512],
                                         wb[:, SL((kt * mt_n + mt) * 128, (kt * mt_n + mt + 1) * 128)],
                                         rhs_tile(kt)[:, SL(j * 512, (j + 1) * 512)],
                                         start=(kt == 0), stop=(kt == kt_n - 1))
                        return i
                    guard = [evtoks[idx - 4]] if idx >= 4 else []
                    t_mm = ET(fm, waits=[("gd", t_w)] + guard + list(rhs_wait))
                    evtoks.append(evac(mt, pa, j, t_mm))

        def conv_silu(dt, xi_ap, cw_off, cb_col, t_ready):
            def fc(v, dt=dt):
                def cw(k):
                    return Acol[:, SL(cw_off + dt * 4 + k, cw_off + dt * 4 + k + 1)]
                v.tensor_scalar_mul(deltf[:, :], xi_ap, cw(3))
                v.scalar_tensor_tensor(deltf[:, 1:T], xi_ap[:, 0:T - 1], cw(2), deltf[:, 1:T], ALU.mult, ALU.add)
                v.scalar_tensor_tensor(deltf[:, 2:T], xi_ap[:, 0:T - 2], cw(1), deltf[:, 2:T], ALU.mult, ALU.add)
                return v.scalar_tensor_tensor(deltf[:, 3:T], xi_ap[:, 0:T - 3], cw(0), deltf[:, 3:T],
                                              ALU.mult, ALU.add)
            t_cv = EV(fc, waits=list(t_ready))
            t_si = ES(lambda s, dt=dt: s.activation(xi_ap, deltf[:, :], AF.Silu,
                                                    bias=pvec[:, SL(cb_col + dt, cb_col + dt + 1)]),
                      waits=[("v", t_cv)])
            return GPD(S["u_sp"][SL(dt * 128, (dt + 1) * 128), :], xi_ap, waits=[("s", t_si)])

        def mamba(which):
            a = which == "a"

            def xln_t(dt):
                return arb(4 * dt, T)
            layer_norm(ND, xln_t, src_hbm=S["xres"])
            t_ln1 = tk("v")
            if a:
                for dt in range(ND):
                    t_cp = EV(lambda v, dt=dt: v.tensor_copy(deltf[:, :], xln_t(dt)),
                              waits=[("v", t_ln1), ("gd", tk("gd"))])
                    GPD(S["xres"][SL(dt * 128, (dt + 1) * 128), :], deltf[:, :],
                        waits=[("v", t_cp)], accum_op=ALU.add)
                layer_norm(ND, xln_t, src_tile=xln_t, gamma_beta=(vsm[:, 0:8], vsm[:, 8:16]))
            t_ln = tk("v")
            if ("tap_xln_" + which) in taps:
                for dt in range(ND):
                    t_cp = EV(lambda v, dt=dt: v.tensor_copy(deltf[:, :], xln_t(dt)), waits=[("v", t_ln)])
                    SYD(taps["tap_xln_" + which][SL(dt * 128, (dt + 1) * 128), :], deltf[:, :],
                        waits=[("v", t_cp)])

            w_hbm = I["a_in_proj"] if a else I["b_W_in"]
            cw_off = 272 if a else 336
            cb_col = 0 if a else 48

            def xi_stage(gmt):
                return arb(64 + 4 * (gmt % 2), T)

            for half in range(2):
                def evac(mt, pa, j, t_mm, half=half):
                    gmt = half * 16 + mt
                    if gmt < NDI:
                        t_e = EV(lambda v, pa=pa, j=j, gmt=gmt: v.tensor_copy(
                            xi_stage(gmt)[:, SL(j * 512, (j + 1) * 512)], pa[:, 0:512]),
                            waits=[("t", t_mm)] + ([("gd", tk("gd"))] if j == 0 else []))
                        if j == 3:
                            conv_silu(gmt, xi_stage(gmt), cw_off, cb_col, [("v", t_e)])
                        return ("v", t_e)
                    else:
                        t_e = ES(lambda s, pa=pa, j=j: s.activation(sqb[:, SL(j * 512, (j + 1) * 512)],
                                                                    pa[:, 0:512], AF.Silu),
                                 waits=[("t", t_mm)] + ([("gd", tk("gd"))] if j == 0 else []))
                        if j == 3:
                            GPD(S["sz_sp"][SL((gmt - NDI) * 128, (gmt - NDI + 1) * 128), :], sqb[:, :],
                                waits=[("s", t_e)])
                        return ("s", t_e)
                mm_proj(w_hbm[:, SL(half * 2048, (half + 1) * 2048)], ND, 16,
                        xln_t, evac, 32, rhs_wait=[("v", t_ln)])

            ncols = 96 if a else 33
            wx_hbm = I["a_x_proj"] if a else I["b_W_x"]
            wxb = arb(32, NDI * ncols)
            t_wx = GPD(wxb.rearrange("p (k c) -> p k c", k=NDI, c=ncols),
                       wx_hbm.rearrange("(k p) c -> p k c", p=128), waits=[("t", tk("t"))])
            for kt in range(NDI):
                tku = GPD(arb(64 + 4 * (kt % 2), T), S["u_sp"][SL(kt * 128, (kt + 1) * 128), :],
                          waits=[("t", tk("t")), ("v", tk("v") if kt == 0 else 0)])

                def fx(t, kt=kt):
                    for j in range(4):
                        i = t.matmul(pss[4 + j][0:ncols, 0:512], wxb[:, SL(kt * ncols, (kt + 1) * ncols)],
                                     arb(64 + 4 * (kt % 2), T)[:, SL(j * 512, (j + 1) * 512)],
                                     start=(kt == 0), stop=(kt == NDI - 1))
                    return i
                t_mx = ET(fx, waits=[("gd", tku)] + ([("gd", t_wx), ("v", tk("v")), ("s", tk("s"))]
                                                     if kt == 0 else []))
            for j in range(4):
                EV(lambda v, j=j: v.tensor_copy(xdbf[0:ncols, SL(j * 512, (j + 1) * 512)],
                                                pss[4 + j][0:ncols, 0:512]),
                   waits=[("t", t_mx)])
            t_xdb = tk("v")

            if a:
                t_dtc = EV(lambda v: v.tensor_copy(xdbb[:, :], xdbf[0:64, :]), waits=[("v", t_xdb)])
                t_br = GPD(brow[:, :], xdbf[64:80, :], waits=[("v", t_xdb)])
                t_cr = GPD(crow[:, :], xdbf[80:96, :], waits=[("v", t_xdb)])
            else:
                t_br = GPD(brow[:, :], xdbf[1:17, :], waits=[("v", t_xdb)])
                t_cr = GPD(crow[:, :], xdbf[17:33, :], waits=[("v", t_xdb)])
                for j in range(4):
                    t_bb = ET(lambda t, j=j: t.matmul(pss[5][:, 0:512], onesf[0:1, :],
                                                      xdbf[0:1, SL(j * 512, (j + 1) * 512)],
                                                      start=True, stop=True),
                              waits=[("v", t_xdb if j == 0 else tk("v"))])
                    EV(lambda v, j=j: v.tensor_copy(arf(72, T)[:, SL(j * 512, (j + 1) * 512)], pss[5][:, 0:512]),
                       waits=[("t", t_bb)])
            t_xp0 = tk("v")

            sp_toks = []
            w_toks = []
            for dt in range(NDI):
                tku = GPD(arb(64 + 4 * (dt % 2), T), S["u_sp"][SL(dt * 128, (dt + 1) * 128), :],
                          waits=[("v", tk("v"))])
                if a:
                    for j in range(4):
                        jj = dt * 4 + j
                        pa = pss[4 + jj % 2]
                        guard = [("s", sp_toks[jj - 2])] if jj >= 2 else [("s", tk("s"))]
                        t_m = ET(lambda t, dt=dt, j=j, pa=pa: t.matmul(pa[:, 0:512],
                                                                       dtpw[:, SL(dt * 128, (dt + 1) * 128)],
                                                                       xdbb[:, SL(j * 512, (j + 1) * 512)],
                                                                       start=True, stop=True),
                                 waits=[("gd", t_gdc), ("v", t_dtc)] + guard)
                        ES(lambda s, j=j, dt=dt, pa=pa: s.activation(deltf[:, SL(j * 512, (j + 1) * 512)],
                                                                     pa[:, 0:512], AF.Exp,
                                                                     bias=pvec[:, SL(16 + dt, 16 + dt + 1)]),
                           waits=[("t", t_m)] + ([("v", w_toks[-1] if w_toks else tk("v"))] if j == 0 else []))
                        t_sp = ES(lambda s, j=j: s.activation(deltf[:, SL(j * 512, (j + 1) * 512)],
                                                              deltf[:, SL(j * 512, (j + 1) * 512)],
                                                              AF.Ln, bias=kcol[:, 0:1]))
                        sp_toks.append(t_sp)
                else:
                    t_dr = EV(lambda v, dt=dt: v.tensor_scalar(deltf[:, :], arf(72, T)[:, :],
                                                               pvec[:, SL(80 + dt, 80 + dt + 1)],
                                                               pvec[:, SL(64 + dt, 64 + dt + 1)],
                                                               ALU.mult, ALU.add),
                              waits=[("v", tk("v")), ("s", tk("s")), ("gd", tk("gd"))])
                    ES(lambda s: s.activation(deltf[:, :], deltf[:, :], AF.Exp), waits=[("v", t_dr)])
                    t_sp = ES(lambda s: s.activation(deltf[:, :], deltf[:, :], AF.Ln, bias=kcol[:, 0:1]))
                GPD(S["dt_sp"][SL(dt * 128, (dt + 1) * 128), :], deltf[:, :], waits=[("s", t_sp)])
                t_w = EV(lambda v, dt=dt: v.tensor_mul(sqb[:, :], deltf[:, :], arb(64 + 4 * (dt % 2), T)),
                         waits=[("s", t_sp), ("gd", tku)])
                w_toks.append(t_w)
                GPD(S["w_sp"][SL(dt * 128, (dt + 1) * 128), :], sqb[:, :], waits=[("v", t_w)])

            acol_off = 0 if a else 256
            dt_ld = [0, 0]
            wt_ld = [0, 0]
            dt_cons = [tk("s"), tk("s")]
            wt_cons = [tk("v"), tk("v")]
            da_cons = [tk("v"), tk("v")]
            par_cons = [0, 0]

            def issue_loads(dtile, slt, extra=()):
                dt_ld[slt] = SYD(arf(64 + 8 * slt, T), S["dt_sp"][SL(dtile * 128, (dtile + 1) * 128), :],
                                 waits=[("s", dt_cons[slt])] + list(extra))
                wt_ld[slt] = SYD(arb(80 + 4 * slt, T), S["w_sp"][SL(dtile * 128, (dtile + 1) * 128), :],
                                 waits=[("v", wt_cons[slt])])

            issue_loads(0, 0, extra=[("v", tk("v")), ("gd", tk("gd")), ("s", tk("s"))])
            for q in range(4):
                for ni in range(4):
                    n = q * 4 + ni
                    for j in range(4):
                        t_m = ET(lambda t, n=n, j=j: t.matmul(pss[6][:, 0:512],
                                                              selb[:, SL(n * 128, (n + 1) * 128)],
                                                              brow[:, SL(j * 512, (j + 1) * 512)],
                                                              start=True, stop=True),
                                 waits=[("gd", t_br), ("v", tk("v"))])
                        EV(lambda v, ni=ni, j=j: v.tensor_copy(arb(4 * ni, T)[:, SL(j * 512, (j + 1) * 512)],
                                                               pss[6][:, 0:512]), waits=[("t", t_m)])
                        t_m2 = ET(lambda t, n=n, j=j: t.matmul(pss[7][:, 0:512],
                                                               selb[:, SL(n * 128, (n + 1) * 128)],
                                                               crow[:, SL(j * 512, (j + 1) * 512)],
                                                               start=True, stop=True),
                                  waits=[("gd", t_cr), ("v", tk("v"))])
                        EV(lambda v, ni=ni, j=j: v.tensor_copy(arb(16 + 4 * ni, T)[:, SL(j * 512, (j + 1) * 512)],
                                                               pss[7][:, 0:512]), waits=[("t", t_m2)])
                t_bcq = tk("v")

                for dt in range(NDI):
                    slt = dt % 2
                    for ni in range(4):
                        n = q * 4 + ni
                        dslt = ni % 2
                        ac_i = acol_off + (dt * 16 + n if a else n)
                        ac = Acol[:, SL(ac_i, ac_i + 1)]
                        t_da = ES(lambda s, slt=slt, dslt=dslt, ac=ac: s.activation(
                            arf(104 + 8 * dslt, T), arf(64 + 8 * slt, T), AF.Exp, scale=ac),
                            waits=[("ld", dt_ld[slt]), ("v", da_cons[dslt])])
                        dt_cons[slt] = t_da
                        t_wb = EV(lambda v, slt=slt, ni=ni: v.tensor_mul(arb(48 + 4 * ni, T),
                                                                         arb(80 + 4 * slt, T),
                                                                         arb(4 * ni, T)),
                                  waits=[("ld", wt_ld[slt]), ("v", t_bcq)])
                        wt_cons[slt] = t_wb
                        t_sc = EV(lambda v, ni=ni, dslt=dslt: v.tensor_tensor_scan(
                            arb(32 + 4 * ni, T), arf(104 + 8 * dslt, T), arb(48 + 4 * ni, T), 0.0,
                            ALU.mult, ALU.add), waits=[("s", t_da)])
                        da_cons[dslt] = t_sc
                    if dt < NDI - 1:
                        issue_loads(dt + 1, (dt + 1) % 2)
                    elif q < 3:
                        issue_loads(0, 0)
                    EV(lambda v: v.tensor_mul(arb(48, 4 * T), arb(32, 4 * T), arb(16, 4 * T)))
                    EV(lambda v: v.tensor_add(arb(48, 2 * T), arb(48, 2 * T), arb(56, 2 * T)))
                    psl = dt % 2
                    t_a2 = EV(lambda v, psl=psl: v.tensor_tensor(arf(88 + 8 * psl, T), arb(48, T),
                                                                 arb(52, T), ALU.add),
                              waits=[("gd", par_cons[psl])] if par_cons[psl] else [])
                    par_cons[psl] = GPD(S["y_sp"][SL(dt * 128, (dt + 1) * 128), :], arf(88 + 8 * psl, T),
                                        waits=[("v", t_a2)], accum_op=(ALU.add if q > 0 else ALU.bypass))
            t_scan = tk("gd")

            def Dcol(dt):
                return pvec[:, SL(32 + dt, 33 + dt)] if a else vsm[:, SL(32 + dt, 33 + dt)]
            mul_toks = [tk("v"), tk("v")]
            yd_toks = [tk("v"), tk("v")]
            for dt in range(NDI):
                sl = dt % 2
                y_t = arf(64 + 8 * sl, T)
                u_t = arb(80 + 4 * sl, T)
                z_t = arb(88 + 4 * sl, T)
                q_t = deltf[:, :] if sl == 0 else arf(96, T)
                t_y = SYD(y_t, S["y_sp"][SL(dt * 128, (dt + 1) * 128), :],
                          waits=[("gd", t_scan if dt == 0 else 0), ("v", mul_toks[sl])])
                GPD(u_t, S["u_sp"][SL(dt * 128, (dt + 1) * 128), :], waits=[("v", yd_toks[sl])])
                t_z = GPD(z_t, S["sz_sp"][SL(dt * 128, (dt + 1) * 128), :], waits=[("v", mul_toks[sl])])
                t_yd = EV(lambda v, dt=dt, u_t=u_t, y_t=y_t, q_t=q_t: v.scalar_tensor_tensor(
                    q_t, u_t, Dcol(dt), y_t, ALU.mult, ALU.add),
                    waits=[("ld", t_y), ("gd", t_z)])
                yd_toks[sl] = t_yd
                mul_toks[sl] = EV(lambda v, dt=dt, q_t=q_t, z_t=z_t: v.tensor_mul(arb(4 * dt, T), q_t, z_t))
            t_rhs = tk("v")

            wout = I["a_out_proj"] if a else I["b_W_out"]

            def evac_out(mt, pa, j, t_mm):
                if a:
                    t_e = ES(lambda s, mt=mt, pa=pa, j=j: s.activation(
                        arb(96 + 4 * mt, T)[:, SL(j * 512, (j + 1) * 512)], pa[:, 0:512], AF.Gelu),
                       waits=[("t", t_mm)])
                    return ("s", t_e)
                else:
                    t_e = EV(lambda v, pa=pa, j=j: v.tensor_copy(deltf[:, SL(j * 512, (j + 1) * 512)],
                                                                 pa[:, 0:512]),
                             waits=[("t", t_mm)] + ([("gd", tk("gd"))] if j == 0 else []))
                    if j == 3:
                        GPD(S["xres"][SL(mt * 128, (mt + 1) * 128), :], deltf[:, :],
                            waits=[("v", t_e)], accum_op=ALU.add)
                    return ("v", t_e)
            mm_proj(wout, NDI, ND, lambda kt: arb(4 * kt, T), evac_out, 64, rhs_wait=[("v", t_rhs)])

            if a:
                t_gelu = tk("s")

                def evac_glu(mt, pa, j, t_mm):
                    if mt < 8:
                        t_e = EV(lambda v, mt=mt, pa=pa, j=j: v.tensor_scalar_add(
                            arb(4 * mt, T)[:, SL(j * 512, (j + 1) * 512)], pa[:, 0:512],
                            vsm[:, SL(16 + mt, 17 + mt)]), waits=[("t", t_mm)])
                        return ("v", t_e)
                    else:
                        t_e = ES(lambda s, mt=mt, pa=pa, j=j: s.activation(
                            arb(32 + 4 * (mt - 8), T)[:, SL(j * 512, (j + 1) * 512)], pa[:, 0:512],
                            AF.Sigmoid, bias=vsm[:, SL(16 + mt, 17 + mt)]), waits=[("t", t_mm)])
                        if j == 3:
                            mt8 = mt - 8
                            t_h = EV(lambda v, mt8=mt8: v.tensor_tensor(deltf[:, :], arb(4 * mt8, T),
                                                                        arb(32 + 4 * mt8, T), ALU.mult),
                                     waits=[("s", t_e), ("gd", tk("gd"))])
                            GPD(S["xres"][SL(mt8 * 128, (mt8 + 1) * 128), :], deltf[:, :],
                                waits=[("v", t_h)], accum_op=ALU.add)
                        return ("s", t_e)
                mm_proj(I["a_glu_w"], ND, 16, lambda kt: arb(96 + 4 * kt, T), evac_glu, 64,
                        rhs_wait=[("s", t_gelu)])

        def ffn():
            relu_toks = []
            yadd_toks = []

            def xln_t(dt):
                return arb(4 * dt, T)
            layer_norm(ND, xln_t, src_hbm=S["xres"])
            t_ln = tk("v")

            for tt in range(NTT):
                def fsc(t, tt=tt):
                    for kt in range(ND):
                        i = t.matmul(pss[5][:, 0:16], xln_t(kt)[:, SL(tt * 128, (tt + 1) * 128)],
                                     wgb[:, SL(kt * 16, (kt + 1) * 16)],
                                     start=(kt == 0), stop=(kt == ND - 1))
                    return i
                t_m = ET(fsc, waits=[("v", t_ln if tt == 0 else tk("v")), ("gd", t_gdc)])
                EV(lambda v, tt=tt: v.tensor_tensor(sco[:, SL(tt * 16, (tt + 1) * 16)], pss[5][:, 0:16],
                                                    bgb[:, 0:16], ALU.add), waits=[("t", t_m)])
            t_sco = tk("v")

            for tt in range(NTT):
                sc = sco[:, SL(tt * 16, (tt + 1) * 16)]

                def fr(v, tt=tt, sc=sc):
                    v.max(rtw[:, 0:8], sc)
                    v.max_index(rtw[:, 8:16].bitcast(U32), rtw[:, 0:8], sc)
                    v.tensor_copy(idxg[:, SL(tt, tt + 1)], rtw[:, 8:16].bitcast(U32)[:, 0:1])
                    return v.tensor_scalar_mul(rtw[:, 16:17], rtw[:, 0:1], -1.0)
                t_mx = EV(fr, waits=[("v", t_sco)])
                t_ex = ES(lambda s, sc=sc: s.activation(rtw[:, 32:48], sc, AF.Exp, bias=rtw[:, 16:17],
                                                        accum_out=rtw[:, 17:18]), waits=[("v", t_mx)])

                def fg(v, tt=tt):
                    v.reciprocal(rtw[:, 18:19], rtw[:, 17:18])
                    v.tensor_copy(gateb[:, SL(tt, tt + 1)], rtw[:, 18:19])
                    v.tensor_single_scalar(onehf[:, :], iotab[:, :], idxg[:, SL(tt, tt + 1)], ALU.is_equal)
                    return v.tensor_copy(oneh[:, :], onehf[:, :])
                t_oh = EV(fg, waits=[("s", t_ex)])
                ET(lambda t: t.matmul(pss[6][:, 0:16], slob[:, :], oneh[:, :], start=True, stop=True),
                   waits=[("v", t_oh)])
                t_cm = ET(lambda t: t.matmul(pss[7][0:16, 0:1], oneh[:, :], onesc[:, :],
                                             start=True, stop=True))

                def fp(v, tt=tt):
                    v.tensor_mul(rtw[:, 32:48], onehf[:, :], pss[6][:, 0:16])
                    v.reduce_sum(posg[:, SL(tt, tt + 1)], rtw[:, 32:48], AX.X)
                    return v.tensor_copy(cnts[0:16, SL(tt, tt + 1)], pss[7][0:16, 0:1])
                EV(fp, waits=[("t", t_cm)])
            t_rt = tk("v")

            def fcs(v):
                v.tensor_tensor_scan(cnts[0:16, 16:32], ones4[:, :], cnts[0:16, 0:16], 0.0, ALU.mult, ALU.add)
                v.memset(cnts[0:16, 32:33], 0.0)
                return v.tensor_copy(cnts[0:16, 33:48], cnts[0:16, 16:31])
            t_cs = EV(fcs, waits=[("v", t_rt)])
            t_ci = GPD(S["cc_c_in"][0:1, :], cnts[0:16, 31:32], waits=[("v", t_cs)])
            t_cc1 = EG(lambda gp: gp.collective_compute("AllGather", ALU.bypass,
                                                        replica_groups=[[0, 1, 2, 3, 4, 5, 6, 7]],
                                                        ins=[S["cc_c_in"][:]], outs=[S["C_all"][:]]),
                       waits=[("gd", t_ci)])
            t_c8 = GPD(cnt8[:, :], S["C_all"][:, :], waits=[("g", t_cc1)])
            t_mk = EV(lambda v: v.tensor_tensor(mask8[:, :], iota8[:, :], cidb[0:8, 0:1], ALU.is_lt),
                      waits=[("gd", t_c8), ("v", t_cid)])
            t_oc = ET(lambda t: t.matmul(pss[5][0:1, 0:16], mask8[:, :], cnt8[:, :], start=True, stop=True),
                      waits=[("v", t_mk)])
            t_oce = EV(lambda v: v.tensor_copy(arow[0:1, 0:16], pss[5][0:1, 0:16]), waits=[("t", t_oc)])
            t_occ = GPD(cnts[0:16, 48:49], arow[0:1, 0:16], waits=[("v", t_oce)])
            t_toff = EV(lambda v: v.tensor_scalar_add(cnts[0:16, 32:48], cnts[0:16, 32:48],
                                                      cnts[0:16, 48:49]), waits=[("gd", t_occ)])
            t_tfr = GPD(xdbf[0:1, 0:256], cnts[0:16, 32:48], waits=[("v", t_toff)])
            t_tb = 0
            for j in range(1):
                t_bb = ET(lambda t: t.matmul(pss[5][:, 0:256], onesf[0:1, :], xdbf[0:1, 0:256],
                                             start=True, stop=True), waits=[("gd", t_tfr)])
                t_tb = EV(lambda v: v.tensor_copy(toffb[:, 0:256], pss[5][:, 0:256]), waits=[("t", t_bb)])

            for tt in range(NTT):
                def fq(v, tt=tt):
                    v.tensor_single_scalar(onehf[:, :], iotab[:, :], idxg[:, SL(tt, tt + 1)], ALU.is_equal)
                    v.tensor_mul(rtw[:, 32:48], onehf[:, :],
                                 toffb.rearrange("p (e tt) -> p e tt", tt=16)[:, :, SL(tt, tt + 1)])
                    v.reduce_sum(rtw[:, 19:20], rtw[:, 32:48], AX.X)
                    v.tensor_add(posg[:, SL(tt, tt + 1)], posg[:, SL(tt, tt + 1)], rtw[:, 19:20])
                    v.tensor_single_scalar(rtw[:, 20:21], posg[:, SL(tt, tt + 1)], 1024.0, ALU.is_lt)
                    v.tensor_mul(gateb[:, SL(tt, tt + 1)], gateb[:, SL(tt, tt + 1)], rtw[:, 20:21])
                    v.tensor_scalar_mul(rtw[:, 21:22], idxg[:, SL(tt, tt + 1)], 1024.0)
                    v.tensor_add(rtw[:, 21:22], rtw[:, 21:22], posg[:, SL(tt, tt + 1)])
                    v.tensor_mul(rtw[:, 21:22], rtw[:, 21:22], rtw[:, 20:21])
                    v.tensor_copy(slotc[:, SL(tt, tt + 1)], rtw[:, 21:22])
                    v.tensor_copy(meta2[:, 0:1], idxg[:, SL(tt, tt + 1)])
                    return v.tensor_copy(meta2[:, 1:2], posg[:, SL(tt, tt + 1)])
                t_q = EV(fq, waits=[("v", t_tb), ("gd", tk("gd"))])
                GPD(S["cc_m_in"][SL(tt * 128, (tt + 1) * 128), :], meta2[:, 0:2], waits=[("v", t_q)])
            t_meta = tk("gd")

            for tt in range(NTT):
                for dc in range(ND):
                    t_tp = ET(lambda t, tt=tt, dc=dc: t.transpose(
                        pss[5 + (dc % 2)].bitcast(BF16)[:, 0:128],
                        xln_t(dc)[:, SL(tt * 128, (tt + 1) * 128)], eyeb[:, :]),
                        waits=[("v", tk("v")), ("gd", tk("gd")), ("ld", t_ldc)])
                    EV(lambda v, tt=tt, dc=dc: v.tensor_copy(
                        arb(32 + 2 * (tt % 2), D)[:, SL(dc * 128, (dc + 1) * 128)],
                        pss[5 + (dc % 2)].bitcast(BF16)[:, 0:128]), waits=[("t", t_tp)])
                GPD(S["cc_x_in"][SL(tt * 128, (tt + 1) * 128), :], arb(32 + 2 * (tt % 2), D),
                    waits=[("v", tk("v"))])
            t_xr = tk("gd")
            t_ccx = EG(lambda gp: gp.collective_compute("AllGather", ALU.bypass,
                                                        replica_groups=[[0, 1, 2, 3, 4, 5, 6, 7]],
                                                        ins=[S["cc_x_in"][:]], outs=[S["X_all"][:]]),
                       waits=[("gd", t_xr)])
            t_ccm = EG(lambda gp: gp.collective_compute("AllGather", ALU.bypass,
                                                        replica_groups=[[0, 1, 2, 3, 4, 5, 6, 7]],
                                                        ins=[S["cc_m_in"][:]], outs=[S["M_all"][:]]),
                       waits=[("gd", t_meta)])

            t_z = EV(lambda v: v.memset(stage[:, 0:D], 0.0), waits=[("gd", tk("gd"))])
            t_bz = GPD(S["B_own"].rearrange("(r p) d -> p r d", p=128),
                       stage[:, 0:D].unsqueeze(1).broadcast_to([128, 17, D]), waits=[("v", t_z)])

            t_mm2 = GPD(meta2[:, 0:256], S["M_all"].rearrange("(pp gp) -> pp gp", pp=128)
                        if False else S["M_all"].rearrange("(pp g) two -> pp (g two)", pp=128),
                        waits=[("g", t_ccm)])
            eyef = arf(46, 128)
            t_ef = EV(lambda v: v.tensor_copy(eyef[:, :], eyeb[:, :]), waits=[("gd", t_mm2)])

            def fsi(v):
                mv = meta2[:, 0:256].rearrange("p (g two) -> p g two", two=2)
                mi = mv[:, :, SL(0, 1)]
                mp = mv[:, :, SL(1, 2)]
                m0 = meta2[:, 256:384]
                m1 = meta2[:, 384:512]
                kp = meta2[:, 512:640]
                v.tensor_single_scalar(m0, mi, cidb[:, 1:2], ALU.is_equal)
                v.tensor_single_scalar(m1, mi, cidb[:, 2:3], ALU.is_equal)
                v.tensor_single_scalar(kp, mp, 1024.0, ALU.is_lt)
                v.tensor_add(m0, m0, m1)
                v.tensor_mul(m0, m0, kp)
                v.tensor_scalar_mul(m1, m1, 1024.0)
                v.tensor_tensor(m1, m1, mp, ALU.add)
                v.tensor_scalar_sub(m1, m1, cidb[:, 3:4])
                v.tensor_mul(m1, m1, m0)
                v.tensor_scalar_add(m1, m1, cidb[:, 3:4])
            t_sif = EV(fsi, waits=[("v", t_ef), ("v", t_cid)])
            t_tps = ET(lambda t: t.transpose(pss[5][:, 0:128], meta2[:, 384:512], eyef[:, :]),
                       waits=[("v", t_sif)])
            t_si = EV(lambda v: v.tensor_copy(scat_i[:, :], pss[5][:, 0:128]), waits=[("t", t_tps)])

            scat_toks = []
            for gq in range(128):
                slq = gq % 4
                w_slot = ([("gd", scat_toks[gq - 4])] if gq >= 4
                          else [("gd", t_bz), ("g", t_ccx), ("v", t_si)])
                t_ld = SYD(stage[:, SL(slq * 512, (slq + 1) * 512)].bitcast(BF16)[:, 0:D] if False
                           else stage4(slq), S["X_all"][SL(gq * 128, (gq + 1) * 128), :],
                           waits=w_slot)
                scat_toks.append(GPI(S["B_own"][:, :], IndirectOffsetOnAxis(scat_i[:, SL(gq, gq + 1)], 0),
                                     stage4(slq), None, waits=[("ld", t_ld)]))
            t_scat = tk("gd")

            t_b1 = SYD(b1col.rearrange("p (e ht) -> p e ht", e=2), I["f_b1c"].rearrange("e (ht p) -> p e ht", p=128))
            t_b2 = SYD(b2col.rearrange("p (e mt) -> p e mt", e=2), I["f_b2c"].rearrange("e (mt p) -> p e mt", p=128))
            for ei in range(2):
                for dc in range(ND):
                    SYD(arb(32 + 2 * dc, CAP), S["B_own"][SL(ei * CAP, (ei + 1) * CAP),
                                                          SL(dc * 128, (dc + 1) * 128)],
                        transpose=True,
                        waits=[("gd", t_scat), ("t", tk("t")), ("v", tk("v"))])
                t_xe = tk("ld")

                def xet(dc):
                    return arb(32 + 2 * dc, CAP)
                t_y0 = EV(lambda v: v.memset(arf(0, ND * CAP), 0.0), waits=[("gd", tk("gd")), ("s", tk("s")),
                                                                            ("t", tk("t"))])
                for hh in range(2):
                    t_w1 = GPD(arb(64, ND * 16 * 128).rearrange("p (k m c) -> p k m c", k=ND, m=16, c=128),
                               I["f_W1c"][ei, :, SL(hh * 2048, (hh + 1) * 2048)]
                               .rearrange("(k p) (m c) -> p k m c", p=128, c=128),
                               waits=[("t", tk("t")), ("gd", tk("gd"))])
                    relu_toks.clear()
                    for mt in range(16):
                        for j in range(2):
                            idx2 = mt * 2 + j
                            pa = pss[idx2 % 4]
                            guard = [("s", relu_toks[idx2 - 4])] if idx2 >= 4 else                                 ([("s", tk("s"))] if idx2 == 0 else [])

                            def fm1(t, mt=mt, j=j, pa=pa):
                                for kt in range(ND):
                                    i = t.matmul(pa[:, 0:512],
                                                 arb(64, ND * 16 * 128)[:, SL((kt * 16 + mt) * 128,
                                                                              (kt * 16 + mt + 1) * 128)],
                                                 xet(kt)[:, SL(j * 512, (j + 1) * 512)],
                                                 start=(kt == 0), stop=(kt == ND - 1))
                                return i
                            t_mm = ET(fm1, waits=[("gd", t_w1), ("ld", t_xe)] + guard)
                            relu_toks.append(ES(lambda s, mt=mt, j=j, pa=pa, hh=hh, ei=ei: s.activation(
                                arb(128 + 2 * mt, CAP)[:, SL(j * 512, (j + 1) * 512)], pa[:, 0:512],
                                AF.Relu, bias=b1col[:, SL(ei * 32 + hh * 16 + mt, ei * 32 + hh * 16 + mt + 1)]),
                                waits=[("t", t_mm), ("ld", t_b1)]))
                    t_relu = tk("s")
                    t_w2 = GPD(arb(96, 16 * ND * 128).rearrange("p (k m c) -> p k m c", k=16, m=ND, c=128),
                               I["f_W2c"][ei, SL(hh * 2048, (hh + 1) * 2048), :]
                               .rearrange("(k p) (m c) -> p k m c", p=128, c=128),
                               waits=[("t", tk("t"))])
                    yadd_toks.clear()
                    for mt in range(ND):
                        for j in range(2):
                            idx2 = mt * 2 + j
                            pa = pss[4 + idx2 % 4]
                            guard = [("v", yadd_toks[idx2 - 4])] if idx2 >= 4 else                                 ([("v", tk("v"))] if idx2 == 0 else [])

                            def fm2(t, mt=mt, j=j, pa=pa):
                                for kt in range(16):
                                    i = t.matmul(pa[:, 0:512],
                                                 arb(96, 16 * ND * 128)[:, SL((kt * ND + mt) * 128,
                                                                              (kt * ND + mt + 1) * 128)],
                                                 arb(128 + 2 * kt, CAP)[:, SL(j * 512, (j + 1) * 512)],
                                                 start=(kt == 0), stop=(kt == 15))
                                return i
                            t_mm = ET(fm2, waits=[("gd", t_w2), ("s", t_relu)] + guard)
                            yadd_toks.append(EV(lambda v, mt=mt, j=j, pa=pa: v.tensor_tensor(
                                arf(4 * mt, CAP)[:, SL(j * 512, (j + 1) * 512)],
                                arf(4 * mt, CAP)[:, SL(j * 512, (j + 1) * 512)], pa[:, 0:512], ALU.add),
                                waits=[("t", t_mm), ("v", t_y0)]))
                t_yv = tk("v")
                for mt in range(ND):
                    EV(lambda v, mt=mt, ei=ei: v.tensor_scalar_add(arb(48 + 2 * mt, CAP), arf(4 * mt, CAP),
                                                                   b2col[:, SL(ei * 8 + mt, ei * 8 + mt + 1)]),
                       waits=[("v", t_yv), ("ld", t_b2)] + ([("ld", tk("ld"))] if mt == 0 else []))
                t_yb = tk("v")
                for st in range(8):
                    for mt in range(ND):
                        t_tp = ET(lambda t, st=st, mt=mt: t.transpose(
                            pss[5 + (mt % 2)].bitcast(BF16)[:, 0:128],
                            arb(48 + 2 * mt, CAP)[:, SL(st * 128, (st + 1) * 128)], eyeb[:, :]),
                            waits=[("v", tk("v")), ("gd", tk("gd"))])
                        EV(lambda v, st=st, mt=mt: v.tensor_copy(
                            arb(64 + 2 * (st % 2), D)[:, SL(mt * 128, (mt + 1) * 128)],
                            pss[5 + (mt % 2)].bitcast(BF16)[:, 0:128]), waits=[("t", t_tp)])
                    GPD(S["cc_y_in"][SL(ei * CAP + st * 128, ei * CAP + (st + 1) * 128), :],
                        arb(64 + 2 * (st % 2), D), waits=[("v", tk("v"))])
            t_yrows = tk("gd")
            t_ccy = EG(lambda gp: gp.collective_compute("AllGather", ALU.bypass,
                                                        replica_groups=[[0, 1, 2, 3, 4, 5, 6, 7]],
                                                        ins=[S["cc_y_in"][:]], outs=[S["Y_all"][:]]),
                       waits=[("gd", t_yrows)])

            for tt in range(NTT):
                slq = tt % 2
                t_g = GPI(arb(32 + 2 * slq, D), None, S["Y_all"][:, :],
                          IndirectOffsetOnAxis(slotc[:, SL(tt, tt + 1)], 0),
                          waits=[("g", t_ccy), ("v", tk("v")), ("t", tk("t"))])
                t_gt = EV(lambda v, tt=tt, slq=slq: v.tensor_scalar_mul(arb(32 + 2 * slq, D),
                                                                        arb(32 + 2 * slq, D),
                                                                        gateb[:, SL(tt, tt + 1)]),
                          waits=[("gd", t_g)])
                for dc in range(ND):
                    t_tp = ET(lambda t, tt=tt, dc=dc, slq=slq: t.transpose(
                        pss[5 + (dc % 2)].bitcast(BF16)[:, 0:128],
                        arb(32 + 2 * slq, D)[:, SL(dc * 128, (dc + 1) * 128)], eyeb[:, :]),
                        waits=[("v", t_gt if dc == 0 else tk("v"))])
                    EV(lambda v, tt=tt, dc=dc: v.tensor_copy(
                        arb(4 * dc, T)[:, SL(tt * 128, (tt + 1) * 128)],
                        pss[5 + (dc % 2)].bitcast(BF16)[:, 0:128]), waits=[("t", t_tp)])
            t_asm = tk("v")
            for dc in range(ND):
                t_cp = EV(lambda v, dc=dc: v.tensor_copy(deltf[:, :], arb(4 * dc, T)),
                          waits=[("v", t_asm), ("gd", tk("gd"))])
                GPD(S["xres"][SL(dc * 128, (dc + 1) * 128), :], deltf[:, :],
                    waits=[("v", t_cp)], accum_op=ALU.add)

        for ph in PHASES:
            if ph == "a":
                mamba("a")
            elif ph == "b":
                mamba("b")
            elif ph == "f":
                ffn()

        flush_scalar()
        WAIT("sync", ALLW())
        for dt in range(ND):
            SYD(out_h[SL(dt * 128, (dt + 1) * 128), :], S["xres"][SL(dt * 128, (dt + 1) * 128), :])
        WAIT("sync", [("ld", tk("ld"))])
        WAIT("gpsimd", [("ld", tk("ld"))])

    return nc


def host_constants():
    import ml_dtypes
    eye = np.eye(128, dtype=np.float32).astype(ml_dtypes.bfloat16)
    slo = np.triu(np.ones((128, 128), np.float32), 1).astype(ml_dtypes.bfloat16)
    sel = np.zeros((NEXP, NEXP * 128), np.float32)
    for n in range(NEXP):
        sel[n, n * 128:(n + 1) * 128] = 1.0
    return {
        "c_eye": eye,
        "c_slo": slo,
        "c_sel": sel.astype(ml_dtypes.bfloat16),
        "c_iota16b": np.broadcast_to(np.arange(NEXP, dtype=np.float32), (128, NEXP)).copy(),
        "c_iota8c": np.arange(8, dtype=np.float32).reshape(8, 1),
        "c_iota128c": np.arange(128, dtype=np.float32).reshape(128, 1),
    }


def make_in_maps(inp):
    x = inp["L0_attn_a_x"]
    consts = host_constants()
    in_maps = []
    for c in range(8):
        m = {
            "xT": np.ascontiguousarray(x[c].T),
            "a_in_proj": inp["L0_attn_a_in_proj_w"],
            "a_conv_w": inp["L0_attn_a_conv_w"],
            "a_conv_b": inp["L0_attn_a_conv_b"],
            "a_x_proj": inp["L0_attn_a_x_proj_w"],
            "a_dt_proj": inp["L0_attn_a_dt_proj_w"],
            "a_dt_b": inp["L0_attn_a_dt_proj_b"],
            "a_A_log": inp["L0_attn_a_A_log"],
            "a_D": inp["L0_attn_a_D"],
            "a_out_proj": inp["L0_attn_a_out_proj_w"],
            "a_glu_w": inp["L0_attn_a_glu_w"],
            "a_glu_b": inp["L0_attn_a_glu_b"],
            "a_ln_g": inp["L0_attn_a_ln_gamma"],
            "a_ln_b": inp["L0_attn_a_ln_beta"],
            "b_W_in": inp["L0_attn_b_W_in"],
            "b_conv_w": np.ascontiguousarray(inp["L0_attn_b_conv_w"].reshape(DI, 4)),
            "b_conv_b": inp["L0_attn_b_conv_b"],
            "b_W_x": inp["L0_attn_b_W_x"],
            "b_W_dt": np.ascontiguousarray(inp["L0_attn_b_W_dt"].reshape(DI)),
            "b_b_dt": inp["L0_attn_b_b_dt"],
            "b_A_log": inp["L0_attn_b_A_log"].reshape(1, NST),
            "b_D": inp["L0_attn_b_D"],
            "b_W_out": inp["L0_attn_b_W_out"],
            "f_Wg": inp["L0_ffn_Wg"],
            "f_bg": inp["L0_ffn_bg"].reshape(1, NEXP),
            "f_W1c": np.ascontiguousarray(inp["L0_ffn_W1"][2 * c:2 * c + 2]),
            "f_b1c": np.ascontiguousarray(inp["L0_ffn_b1"][2 * c:2 * c + 2]),
            "f_W2c": np.ascontiguousarray(inp["L0_ffn_W2"][2 * c:2 * c + 2]),
            "f_b2c": np.ascontiguousarray(inp["L0_ffn_b2"][2 * c:2 * c + 2]),
            "c_cid": np.array([[float(c)]], dtype=np.float32),
        }
        m.update(consts)
        in_maps.append(m)
    return in_maps


def kernel(**inp):
    nc = build_program()
    in_maps = make_in_maps(inp)
    res = run_bass_kernel_spmd(nc, in_maps, core_ids=list(range(8)))
    return np.stack([np.ascontiguousarray(res.results[c]["outT"].T) for c in range(8)], axis=0)
